# version 01
# baseline (speedup 1.0000x reference)
import contextlib
import numpy as np
import concourse.bass as bass
import concourse.mybir as mybir

F32 = mybir.dt.float32
BF16 = mybir.dt.bfloat16
I32 = mybir.dt.int32
AF = mybir.ActivationFunctionType
ALU = mybir.AluOpType
AX = mybir.AxisListType

P = 128
D = 2048
NC_CHUNKS = D // P
TOK = 1024
NTT = TOK // P


def build_l0qkv():
    nc = bass.Bass("TRN2", target_bir_lowering=False)
    x_f = nc.dram_tensor("x_f", [D, TOK], F32, kind="ExternalInput")
    w_attn = nc.dram_tensor("w_attn", [D, 3 * D], F32, kind="ExternalInput")
    b_attn = nc.dram_tensor("b_attn", [3 * D], F32, kind="ExternalInput")
    qkv = nc.dram_tensor("qkv", [TOK, 3 * D], BF16, kind="ExternalOutput")

    NOB = (3 * D) // 512

    with contextlib.ExitStack() as ctx:
        E = ctx.enter_context
        xfb = E(nc.sbuf_tensor("xfb", [P, NC_CHUNKS * TOK], BF16))
        wblk = [E(nc.sbuf_tensor(f"wblk{i}", [P, NC_CHUNKS * 512], BF16)) for i in range(2)]
        brow = E(nc.sbuf_tensor("brow", [1, 3 * D], BF16))
        ones1 = E(nc.sbuf_tensor("ones1", [1, P], BF16))
        ev = [E(nc.sbuf_tensor(f"ev{i}", [P, NTT * 512], BF16)) for i in range(2)]
        ps = [E(nc.psum_tensor(f"ps{i}", [P, 512], F32)) for i in range(2)]
        ldx = E(nc.semaphore("ldx"))
        ldw = [E(nc.semaphore(f"ldw{i}")) for i in range(2)]
        gc = E(nc.semaphore("gc"))
        mm = E(nc.semaphore("mm"))
        evs_s = E(nc.semaphore("evs_s"))
        evs_v = E(nc.semaphore("evs_v"))
        so = [E(nc.semaphore(f"so{i}")) for i in range(2)]
        block = E(nc.Block())

        w_r = w_attn.ap().rearrange("(c p) o -> p c o", p=P)
        qkv_r = qkv.ap().rearrange("(tt p) o -> p tt o", p=P)

        @block.gpsimd
        def _(g):
            g.dma_start(out=xfb[:, :].rearrange("p (c t) -> p c t", c=NC_CHUNKS),
                        in_=x_f.ap().rearrange("(c p) t -> p c t", p=P)).then_inc(ldx, 16)
            g.dma_start(out=brow[:, :], in_=b_attn.ap().unsqueeze(0)).then_inc(ldx, 16)
            g.memset(ones1[:, :], 1.0).then_inc(gc, 1)
            for ob in range(NOB):
                if ob >= 2:
                    g.wait_ge(mm, 8 * (ob - 1))
                g.dma_start(out=wblk[ob % 2][:, :].rearrange("p (c o) -> p c o", c=NC_CHUNKS),
                            in_=w_r[:, :, ob * 512:(ob + 1) * 512]).then_inc(ldw[ob % 2], 16)

        @block.tensor
        def _(t):
            t.wait_ge(ldx, 32)
            t.wait_ge(gc, 1)
            gi = 0
            for ob in range(NOB):
                t.wait_ge(ldw[ob % 2], 16 * (ob // 2 + 1))
                for tt in range(NTT):
                    pb = ps[gi % 2]
                    if gi >= 2:
                        if gi % 2 == 0:
                            t.wait_ge(evs_s, gi // 2)
                        else:
                            t.wait_ge(evs_v, (gi - 1) // 2)
                    for c in range(NC_CHUNKS):
                        t.matmul(pb[:, :],
                                 xfb[:, c * TOK + tt * P: c * TOK + (tt + 1) * P],
                                 wblk[ob % 2][:, c * 512:(c + 1) * 512],
                                 start=(c == 0), stop=False)
                    t.matmul(pb[:, :], ones1[:, :],
                             brow[0:1, ob * 512:(ob + 1) * 512],
                             start=False, stop=True).then_inc(mm, 1)
                    gi += 1

        @block.scalar
        def _(s):
            gi = 0
            for ob in range(NOB):
                for tt in range(NTT):
                    if tt % 2 == 0:
                        s.wait_ge(mm, gi + 1)
                        if ob >= 2:
                            s.wait_ge(so[ob % 2], 16 * (ob // 2))
                        s.copy(ev[ob % 2][:, tt * 512:(tt + 1) * 512],
                               ps[gi % 2][:, :]).then_inc(evs_s, 1)
                    gi += 1

        @block.vector
        def _(v):
            gi = 0
            for ob in range(NOB):
                for tt in range(NTT):
                    if tt % 2 == 1:
                        v.wait_ge(mm, gi + 1)
                        if ob >= 2:
                            v.wait_ge(so[ob % 2], 16 * (ob // 2))
                        v.tensor_copy(ev[ob % 2][:, tt * 512:(tt + 1) * 512],
                                      ps[gi % 2][:, :]).then_inc(evs_v, 1)
                    gi += 1

        @block.sync
        def _(sy):
            for ob in range(NOB):
                sy.wait_ge(evs_s, 4 * (ob + 1))
                sy.wait_ge(evs_v, 4 * (ob + 1))
                if ob >= 2:
                    sy.wait_ge(so[ob % 2], 16 * (ob // 2))
                sy.dma_start(out=qkv_r[:, :, ob * 512:(ob + 1) * 512],
                             in_=ev[ob % 2][:, :].rearrange("p (tt o) -> p tt o", tt=NTT)
                             ).then_inc(so[ob % 2], 16)
            sy.wait_ge(so[0], 16 * (NOB // 2))
            sy.wait_ge(so[1], 16 * (NOB // 2))

    return nc


def build_l0attn():
    nc = bass.Bass("TRN2", target_bir_lowering=False)
    NH = 8
    T = 2048
    HD = 128
    q_f = nc.dram_tensor("q_f", [NH, HD, T], BF16, kind="ExternalInput")
    k_f = nc.dram_tensor("k_f", [NH, HD, T], BF16, kind="ExternalInput")
    v_t = nc.dram_tensor("v_t", [NH, T, HD], BF16, kind="ExternalInput")
    maskneg = nc.dram_tensor("maskneg", [P, P], F32, kind="ExternalInput")
    ident_bf = nc.dram_tensor("ident_bf", [P, P], BF16, kind="ExternalInput")
    o_f = nc.dram_tensor("o_f", [NH * HD, T], BF16, kind="ExternalOutput")

    SCALE = float(HD) ** -0.5
    SHIFT = -16.0

    def chunks_of(qt):
        klen = (qt + 1) * P
        out = []
        kc = 0
        while kc * 512 < klen:
            out.append((kc, min(512, klen - kc * 512)))
            kc += 1
        return out

    sched = []
    c_smm = c_mask = c_exp = c_zsc = c_kt = c_oev = 0
    for h in range(NH):
        for g in range(4):
            rec = {"smm0": c_smm, "mask0": c_mask, "exp0": c_exp,
                   "zsc0": c_zsc, "kt0": c_kt, "oev0": c_oev}
            for i in range(4):
                qt = 4 * g + i
                for (kc, cw) in chunks_of(qt):
                    c_smm += 1
                    c_exp += 1
                c_mask += 1
                c_zsc += 1
            c_kt += 4 * g + 4
            c_oev += 1
            rec.update({"smm1": c_smm, "mask1": c_mask, "exp1": c_exp,
                        "zsc1": c_zsc, "kt1": c_kt, "oev1": c_oev})
            sched.append(rec)

    with contextlib.ExitStack() as ctx:
        E = ctx.enter_context
        qsb = [E(nc.sbuf_tensor(f"qsb{i}", [P, T], BF16)) for i in range(2)]
        ksb = [E(nc.sbuf_tensor(f"ksb{i}", [P, T], BF16)) for i in range(2)]
        vsb = [E(nc.sbuf_tensor(f"vsb{i}", [P, 16 * HD], BF16)) for i in range(2)]
        mneg = E(nc.sbuf_tensor("mneg", [P, P], F32))
        idbf = E(nc.sbuf_tensor("idbf", [P, P], BF16))
        psb = E(nc.sbuf_tensor("psb", [P, 4 * T], BF16))
        ptsb = [E(nc.sbuf_tensor(f"ptsb{i}", [P, 512], BF16)) for i in range(2)]
        zp = E(nc.sbuf_tensor("zp", [P, 16], F32))
        zinv = E(nc.sbuf_tensor("zinv", [P, 4], F32))
        osb = [E(nc.sbuf_tensor(f"osb{i}", [P, T], BF16)) for i in range(2)]
        shcol = E(nc.sbuf_tensor("shcol", [P, 1], F32))
        ps_s = [E(nc.psum_tensor(f"ps_s{i}", [P, 512], F32)) for i in range(2)]
        pt_ps = [E(nc.psum_tensor(f"pt_ps{i}", [P, 1024], BF16)) for i in range(2)]
        o_ps = E(nc.psum_tensor("o_ps", [P, 512], F32))
        qld = E(nc.semaphore("qld"))
        cld = E(nc.semaphore("cld"))
        ssem = E(nc.semaphore("ssem"))
        msem = E(nc.semaphore("msem"))
        esem = E(nc.semaphore("esem"))
        zsem = E(nc.semaphore("zsem"))
        tsem = E(nc.semaphore("tsem"))
        ptsem = E(nc.semaphore("ptsem"))
        avsem = E(nc.semaphore("avsem"))
        osem = E(nc.semaphore("osem"))
        vself = E(nc.semaphore("vself"))
        sto = [E(nc.semaphore(f"sto{i}")) for i in range(2)]
        cst = E(nc.semaphore("cst"))
        block = E(nc.Block())

        @block.sync
        def _(sy):
            sy.dma_start(out=mneg[:, :], in_=maskneg[:, :]).then_inc(cld, 16)
            sy.dma_start(out=idbf[:, :], in_=ident_bf[:, :]).then_inc(cld, 16)
            for h in range(NH):
                if h >= 1:
                    sy.wait_ge(qld, 48 * h)
                if h >= 2:
                    sy.wait_ge(avsem, sched[(h - 2) * 4 + 3]["kt1"])
                sy.dma_start(out=qsb[h % 2][:, :], in_=q_f[h, :, :]).then_inc(qld, 16)
                sy.dma_start(out=ksb[h % 2][:, :], in_=k_f[h, :, :]).then_inc(qld, 16)
                sy.dma_start(out=vsb[h % 2][:, :].rearrange("p (kt d) -> p kt d", kt=16),
                             in_=v_t[h, :, :].rearrange("(kt p) d -> p kt d", p=P)
                             ).then_inc(qld, 16)

        @block.tensor
        def _(t):
            t.wait_ge(cld, 32)
            for h in range(NH):
                t.wait_ge(qld, 48 * (h + 1))
                for g in range(4):
                    rec = sched[h * 4 + g]
                    smm = rec["smm0"]
                    for i in range(4):
                        qt = 4 * g + i
                        for (kc, cw) in chunks_of(qt):
                            if smm >= 2:
                                t.wait_ge(esem, smm - 1)
                            t.matmul(ps_s[smm % 2][:, 0:cw],
                                     qsb[h % 2][:, qt * P:(qt + 1) * P],
                                     ksb[h % 2][:, kc * 512:kc * 512 + cw],
                                     start=True, stop=True).then_inc(ssem, 1)
                            smm += 1
                    t.wait_ge(zsem, rec["zsc1"])
                    ktc = rec["kt0"]
                    for kt in range(4 * g + 4):
                        i0 = max(0, kt - 4 * g)
                        if ktc >= 2:
                            t.wait_ge(ptsem, ktc - 1)
                        last = None
                        for i in range(i0, 4):
                            last = t.transpose(pt_ps[ktc % 2][:, i * P:(i + 1) * P],
                                               psb[:, i * T + kt * P: i * T + (kt + 1) * P],
                                               idbf[:, :])
                        last.then_inc(tsem, 1)
                        t.wait_ge(ptsem, ktc + 1)
                        if kt == 0 and rec["oev0"] > 0:
                            t.wait_ge(osem, rec["oev0"])
                        t.matmul(o_ps[:, i0 * P:512],
                                 vsb[h % 2][:, kt * HD:(kt + 1) * HD],
                                 ptsb[ktc % 2][:, i0 * P:512],
                                 start=(kt == 0), stop=(kt == 4 * g + 3)).then_inc(avsem, 1)
                        ktc += 1

        @block.vector
        def _(v):
            nvself = [0]
            v.wait_ge(cld, 32)
            for h in range(NH):
                for g in range(4):
                    rec = sched[h * 4 + g]
                    smm = rec["smm0"]
                    zc = rec["zsc0"]
                    for i in range(4):
                        qt = 4 * g + i
                        chs = chunks_of(qt)
                        for (kc, cw) in chs:
                            if kc == len(chs) - 1:
                                v.wait_ge(ssem, smm + 1)
                                v.tensor_tensor(ps_s[smm % 2][:, cw - P:cw],
                                                ps_s[smm % 2][:, cw - P:cw],
                                                mneg[:, :], ALU.add).then_inc(msem, 1)
                            smm += 1
                        v.wait_ge(esem, smm)
                        nkc = len(chs)
                        zq = rec["zsc0"] + i
                        if zq > 0:
                            v.wait_ge(zsem, zq)
                        v.reduce_sum(zinv[:, i:i + 1], zp[:, i * 4:i * 4 + nkc],
                                     AX.X).then_inc(vself, 1)
                        nvself[0] += 1
                        v.wait_ge(vself, nvself[0])
                        v.reciprocal(zinv[:, i:i + 1], zinv[:, i:i + 1]).then_inc(vself, 1)
                        nvself[0] += 1
                        v.wait_ge(vself, nvself[0])
                        v.tensor_scalar_mul(psb[:, i * T: i * T + (qt + 1) * P],
                                            psb[:, i * T: i * T + (qt + 1) * P],
                                            zinv[:, i:i + 1]).then_inc(zsem, 1)
                    ktc = rec["kt0"]
                    for kt in range(4 * g + 4):
                        i0 = max(0, kt - 4 * g)
                        v.wait_ge(tsem, ktc + 1)
                        v.tensor_copy(ptsb[ktc % 2][:, i0 * P:512],
                                      pt_ps[ktc % 2][:, i0 * P:512]).then_inc(ptsem, 1)
                        ktc += 1

        @block.scalar
        def _(s):
            s.wait_ge(cst, 1)
            for h in range(NH):
                for g in range(4):
                    idx = h * 4 + g
                    rec = sched[idx]
                    if idx > 0:
                        prev = sched[idx - 1]
                        s.wait_ge(tsem, prev["kt1"])
                        s.wait_ge(zsem, prev["zsc1"])
                    if h >= 2 and g == 0:
                        s.wait_ge(sto[h % 2], 16 * (h // 2))
                    smm = rec["smm0"]
                    mk = rec["mask0"]
                    for i in range(4):
                        qt = 4 * g + i
                        chs = chunks_of(qt)
                        for (kc, cw) in chs:
                            if kc == len(chs) - 1:
                                mk += 1
                                s.wait_ge(msem, mk)
                            else:
                                s.wait_ge(ssem, smm + 1)
                            s.activation(psb[:, i * T + kc * 512: i * T + kc * 512 + cw],
                                         ps_s[smm % 2][:, 0:cw], AF.Exp,
                                         bias=shcol[:, :], scale=SCALE,
                                         accum_out=zp[:, i * 4 + kc:i * 4 + kc + 1]
                                         ).then_inc(esem, 1)
                            smm += 1
                    s.wait_ge(avsem, rec["kt1"])
                    s.copy(osb[h % 2][:, g * 512:(g + 1) * 512], o_ps[:, :]).then_inc(osem, 1)

        @block.gpsimd
        def _(gq):
            gq.memset(shcol[:, :], SHIFT).then_inc(cst, 1)
            for h in range(NH):
                gq.wait_ge(osem, 4 * (h + 1))
                if h >= 2:
                    gq.wait_ge(sto[h % 2], 16 * (h // 2))
                gq.dma_start(out=o_f[h * HD:(h + 1) * HD, :], in_=osb[h % 2][:, :]).then_inc(sto[h % 2], 16)
            gq.wait_ge(sto[0], 16 * (NH // 2))
            gq.wait_ge(sto[1], 16 * (NH // 2))

    return nc


def build_mid():
    nc = bass.Bass("TRN2", target_bir_lowering=False)
    x_t = nc.dram_tensor("x_t", [TOK, D], F32, kind="ExternalInput")
    o_fb = nc.dram_tensor("o_fb", [D, TOK], BF16, kind="ExternalInput")
    w_proj = nc.dram_tensor("w_proj", [D, D], F32, kind="ExternalInput")
    b_proj = nc.dram_tensor("b_proj", [D], F32, kind="ExternalInput")
    gate_w = nc.dram_tensor("gate_w", [D, 8], F32, kind="ExternalInput")
    w1 = nc.dram_tensor("w1", [8, D, 768], F32, kind="ExternalInput")
    w2 = nc.dram_tensor("w2", [8, 768, D], F32, kind="ExternalInput")
    w3 = nc.dram_tensor("w3", [8, D, 768], F32, kind="ExternalInput")
    id32 = nc.dram_tensor("id32", [P, P], F32, kind="ExternalInput")
    idbf = nc.dram_tensor("idbf", [P, P], BF16, kind="ExternalInput")
    x2 = nc.dram_tensor("x2", [TOK, D], F32, kind="ExternalOutput")

    NE = 8
    PIECES = [(0, 512), (512, 256)]
    EPS = 1e-5
    NGOP = 9

    with contextlib.ExitStack() as ctx:
        E = ctx.enter_context
        stream = E(nc.sbuf_tensor("stream", [P, NTT * D], F32))
        x_f = E(nc.sbuf_tensor("x_f", [P, NC_CHUNKS * TOK], BF16))
        bprow = E(nc.sbuf_tensor("bprow", [1, D], BF16))
        ones1 = E(nc.sbuf_tensor("ones1", [1, P], BF16))
        gwsb = E(nc.sbuf_tensor("gwsb", [P, NC_CHUNKS * 8], BF16))
        i32sb = E(nc.sbuf_tensor("i32sb", [P, P], F32))
        ibfsb = E(nc.sbuf_tensor("ibfsb", [P, P], BF16))
        sq = E(nc.sbuf_tensor("sq", [P, NTT], F32))
        sums = E(nc.sbuf_tensor("sums", [P, NTT], F32))
        mu = E(nc.sbuf_tensor("mu", [P, NTT], F32))
        varv = E(nc.sbuf_tensor("varv", [P, NTT], F32))
        rsd = E(nc.sbuf_tensor("rsd", [P, NTT], F32))
        epsc = E(nc.sbuf_tensor("epsc", [P, 1], F32))
        scr = E(nc.sbuf_tensor("scr", [P, D], F32))
        gexp = E(nc.sbuf_tensor("gexp", [P, NTT * 8], F32))
        comb = E(nc.sbuf_tensor("comb", [P, NTT * 8], F32))
        e8a = E(nc.sbuf_tensor("e8a", [P, 8], F32))
        t2a = E(nc.sbuf_tensor("t2a", [P, 1], F32))
        t2b = E(nc.sbuf_tensor("t2b", [P, 1], F32))
        ctx2 = contextlib.ExitStack()
        ofb = ctx2.enter_context(nc.sbuf_tensor("ofb", [P, NC_CHUNKS * TOK], BF16))
        wpb = [ctx2.enter_context(nc.sbuf_tensor(f"wpb{i}", [P, NC_CHUNKS * 512], BF16)) for i in range(2)]
        ctx2.close()
        w13b = [E(nc.sbuf_tensor(f"w13b{i}", [P, NC_CHUNKS * 512], BF16)) for i in range(4)]
        w2b = [E(nc.sbuf_tensor(f"w2b{i}", [P, 6 * 512], BF16)) for i in range(2)]
        gs = [E(nc.sbuf_tensor(f"gs{i}", [P, 512], F32)) for i in range(2)]
        a2t = [E(nc.sbuf_tensor(f"a2t{i}", [P, 512], BF16)) for i in range(2)]
        a2f = E(nc.sbuf_tensor("a2f", [P, NTT * 6 * P], BF16))
        ps = [E(nc.psum_tensor(f"ps{i}", [P, 512], F32)) for i in range(2)]
        pg = [E(nc.psum_tensor(f"pg{i}", [P, 512], F32)) for i in range(2)]
        pu = [E(nc.psum_tensor(f"pu{i}", [P, 512], F32)) for i in range(2)]
        ptr = E(nc.psum_tensor("ptr", [P, 1024], BF16))
        pxf = pu

        ldx = E(nc.semaphore("ldx"))
        ldw = [E(nc.semaphore(f"ldw{i}")) for i in range(2)]
        ldc = E(nc.semaphore("ldc"))
        ldm = E(nc.semaphore("ldm"))
        lde = [E(nc.semaphore(f"lde{i}")) for i in range(2)]
        ld2 = [E(nc.semaphore(f"ld2{i}")) for i in range(2)]
        mm = E(nc.semaphore("mm"))
        evs = E(nc.semaphore("evs"))
        lnv = E(nc.semaphore("lnv"))
        lns = E(nc.semaphore("lns"))
        trs = E(nc.semaphore("trs"))
        xfs = E(nc.semaphore("xfs"))
        gsem = E(nc.semaphore("gsem"))
        vch = E(nc.semaphore("vch"))
        sil = E(nc.semaphore("sil"))
        a2s = E(nc.semaphore("a2s"))
        a2e = E(nc.semaphore("a2e"))
        dmm = E(nc.semaphore("dmm"))
        dev = E(nc.semaphore("dev"))
        sto = E(nc.semaphore("sto"))
        block = E(nc.Block())

        wp_r = w_proj.ap().rearrange("(c p) o -> p c o", p=P)
        x_r = x_t.ap().rearrange("(tt p) d -> p tt d", p=P)
        x2_r = x2.ap().rearrange("(tt p) d -> p tt d", p=P)
        o_r = o_fb.ap().rearrange("(c p) t -> p c t", p=P)
        gw_r = gate_w.ap().rearrange("(c p) e -> p c e", p=P)

        def mm_gu(gidx):
            return 40 + gidx + 1

        @block.sync
        def _(sy):
            sy.dma_start(out=stream[:, :].rearrange("p (tt d) -> p tt d", tt=NTT),
                         in_=x_r).then_inc(ldx, 16)
            sy.dma_start(out=ofb[:, :].rearrange("p (c t) -> p c t", c=NC_CHUNKS),
                         in_=o_r).then_inc(ldx, 16)
            sy.dma_start(out=i32sb[:, :], in_=id32[:, :]).then_inc(ldx, 16)
            sy.dma_start(out=ibfsb[:, :], in_=idbf[:, :]).then_inc(ldx, 16)
            sy.dma_start(out=i32sb[:, :], in_=id32[:, :]).then_inc(ldx, 16)

        @block.gpsimd
        def _(g):
            g.dma_start(out=bprow[:, :], in_=b_proj.ap().unsqueeze(0)).then_inc(ldc, 16)
            g.dma_start(out=gwsb[:, :].rearrange("p (c e) -> p c e", c=NC_CHUNKS),
                        in_=gw_r).then_inc(ldc, 16)
            g.memset(ones1[:, :], 1.0)
            g.memset(epsc[:, :], EPS).then_inc(ldm, 1)
            for ob in range(4):
                if ob >= 2:
                    g.wait_ge(mm, 8 * (ob - 1))
                g.dma_start(out=wpb[ob % 2][:, :].rearrange("p (c o) -> p c o", c=NC_CHUNKS),
                            in_=wp_r[:, :, ob * 512:(ob + 1) * 512]).then_inc(ldw[ob % 2], 16)
            g.wait_ge(mm, 32)
            for e in range(NE):
                for piece, (foff, fw) in enumerate(PIECES):
                    pig = e * 2 + piece
                    par = pig % 2
                    for (mat, slot) in [(w1, 2 * par), (w3, 2 * par + 1)]:
                        if pig >= 2:
                            pe_, pp_ = divmod(pig - 2, 2)
                            g.wait_ge(mm, mm_gu(pe_ * 16 + pp_ * 8 + 7))
                        g.dma_start(
                            out=w13b[slot][:, 0:NC_CHUNKS * fw].rearrange(
                                "p (c f) -> p c f", c=NC_CHUNKS),
                            in_=mat[e, :, foff:foff + fw].rearrange("(c p) f -> p c f", p=P)
                        ).then_inc(lde[par], 16)
                for db in range(4):
                    j = e * 4 + db
                    if j >= 2:
                        g.wait_ge(dmm, 48 * (j - 1))
                    g.dma_start(
                        out=w2b[j % 2][:, :].rearrange("p (c d) -> p c d", c=6),
                        in_=w2[e, :, db * 512:(db + 1) * 512].rearrange("(c p) d -> p c d", p=P)
                    ).then_inc(ld2[j % 2], 16)

        @block.tensor
        def _(t):
            t.wait_ge(ldx, 64)
            t.wait_ge(ldc, 32)
            t.wait_ge(ldm, 1)
            gi = 0
            for ob in range(4):
                t.wait_ge(ldw[ob % 2], 16 * (ob // 2 + 1))
                for tt in range(NTT):
                    pb = ps[gi % 2]
                    if gi >= 2:
                        t.wait_ge(evs, gi - 1)
                    for c in range(NC_CHUNKS):
                        t.matmul(pb[:, :],
                                 ofb[:, c * TOK + tt * P: c * TOK + (tt + 1) * P],
                                 wpb[ob % 2][:, c * 512:(c + 1) * 512],
                                 start=(c == 0), stop=False)
                    t.matmul(pb[:, :], ones1[:, :],
                             bprow[0:1, ob * 512:(ob + 1) * 512],
                             start=False, stop=True).then_inc(mm, 1)
                    gi += 1
            ntr = 0
            for tt in range(NTT):
                t.wait_ge(lnv, 7 * (tt + 1))
                for cg in range(4):
                    if ntr >= 2:
                        t.wait_ge(xfs, ntr - 1)
                    last = None
                    for c4 in range(4):
                        c = cg * 4 + c4
                        last = t.transpose(pxf[ntr % 2][:, c4 * P:(c4 + 1) * P],
                                           stream[:, tt * D + c * P: tt * D + (c + 1) * P],
                                           i32sb[:, :])
                    last.then_inc(trs, 1)
                    ntr += 1
            for tt in range(NTT):
                t.wait_ge(xfs, 4 * (tt + 1))
                if tt > 0:
                    t.wait_ge(gsem, tt)
                for c in range(NC_CHUNKS):
                    t.matmul(pg[0][:, 0:8],
                             x_f[:, c * TOK + tt * P: c * TOK + (tt + 1) * P],
                             gwsb[:, c * 8:(c + 1) * 8],
                             start=(c == 0), stop=(c == NC_CHUNKS - 1)
                             ).then_inc(mm, 1 if c == NC_CHUNKS - 1 else 0)
            t.wait_ge(gsem, 8)
            t.wait_ge(evs, 32)
            t.wait_ge(xfs, 32)
            for e in range(NE):
                pi = 0
                for piece, (foff, fw) in enumerate(PIECES):
                    pig = e * 2 + piece
                    par = pig % 2
                    t.wait_ge(lde[par], 32 * (pig // 2 + 1))
                    for tt in range(NTT):
                        gidx = e * 16 + pi
                        if gidx >= 2:
                            t.wait_ge(sil, gidx - 1)
                            t.wait_ge(a2s, gidx - 1)
                        for c in range(NC_CHUNKS):
                            t.matmul(pg[gidx % 2][:, 0:fw],
                                     x_f[:, c * TOK + tt * P: c * TOK + (tt + 1) * P],
                                     w13b[2 * par][:, c * fw:(c + 1) * fw],
                                     start=(c == 0), stop=(c == NC_CHUNKS - 1))
                        inst = None
                        for c in range(NC_CHUNKS):
                            inst = t.matmul(pu[gidx % 2][:, 0:fw],
                                            x_f[:, c * TOK + tt * P: c * TOK + (tt + 1) * P],
                                            w13b[2 * par + 1][:, c * fw:(c + 1) * fw],
                                            start=(c == 0), stop=(c == NC_CHUNKS - 1))
                        inst.then_inc(mm, 1)
                        t.wait_ge(a2s, gidx + 1)
                        nt = fw // P
                        last = None
                        for c4 in range(nt):
                            last = t.transpose(ptr[:, c4 * P:(c4 + 1) * P],
                                               a2t[gidx % 2][:, c4 * P:(c4 + 1) * P],
                                               ibfsb[:, :])
                        last.then_inc(trs, 1)
                        t.wait_ge(a2e, gidx + 1)
                        pi += 1
                for db in range(4):
                    j = e * 4 + db
                    t.wait_ge(ld2[j % 2], 16 * (j // 2 + 1))
                    for tt in range(NTT):
                        di = e * 32 + db * 8 + tt
                        if di >= 2:
                            t.wait_ge(dev, di - 1)
                        t.wait_ge(a2e, e * 16 + 8 + tt + 1)
                        inst = None
                        for fc in range(6):
                            inst = t.matmul(ps[di % 2][:, :],
                                            a2f[:, tt * 6 * P + fc * P: tt * 6 * P + (fc + 1) * P],
                                            w2b[j % 2][:, fc * 512:(fc + 1) * 512],
                                            start=(fc == 0), stop=(fc == 5))
                        inst.then_inc(dmm, 6)

        @block.vector
        def _(v):
            gi = 0
            for ob in range(4):
                for tt in range(NTT):
                    v.wait_ge(mm, gi + 1)
                    v.tensor_tensor(stream[:, tt * D + ob * 512: tt * D + (ob + 1) * 512],
                                    ps[gi % 2][:, :],
                                    stream[:, tt * D + ob * 512: tt * D + (ob + 1) * 512],
                                    ALU.add).then_inc(evs, 1)
                    gi += 1

            def ln_vector(tt, lnv_base, lns_base):
                sl = slice(tt * D, (tt + 1) * D)
                v.reduce_sum(sums[:, tt:tt + 1], stream[:, sl], AX.X).then_inc(lnv, 1)
                v.wait_ge(lns, lns_base + 2 * tt + 1)
                v.wait_ge(lnv, lnv_base + 7 * tt + 1)
                v.tensor_scalar_mul(mu[:, tt:tt + 1], sums[:, tt:tt + 1], 1.0 / D).then_inc(lnv, 1)
                v.wait_ge(lnv, lnv_base + 7 * tt + 2)
                v.tensor_scalar_mul(varv[:, tt:tt + 1], sq[:, tt:tt + 1], 1.0 / D).then_inc(lnv, 1)
                v.wait_ge(lnv, lnv_base + 7 * tt + 3)
                v.scalar_tensor_tensor(varv[:, tt:tt + 1], mu[:, tt:tt + 1], mu[:, tt:tt + 1],
                                       varv[:, tt:tt + 1], ALU.mult, ALU.subtract).then_inc(lnv, 1)
                v.wait_ge(lns, lns_base + 2 * tt + 2)
                v.reciprocal(rsd[:, tt:tt + 1], rsd[:, tt:tt + 1]).then_inc(lnv, 1)
                v.wait_ge(lnv, lnv_base + 7 * tt + 5)
                v.tensor_scalar(stream[:, sl], stream[:, sl],
                                mu[:, tt:tt + 1], rsd[:, tt:tt + 1],
                                ALU.subtract, ALU.mult).then_inc(lnv, 2)

            v.wait_ge(evs, 32)
            for tt in range(NTT):
                ln_vector(tt, 0, 0)
            for tt in range(NTT):
                g8 = slice(tt * 8, (tt + 1) * 8)
                v.wait_ge(gsem, tt + 1)
                if tt > 0:
                    v.wait_ge(vch, NGOP * tt)
                v.reduce_max(t2a[:, :], gexp[:, g8], AX.X).then_inc(vch, 1)
                v.wait_ge(vch, NGOP * tt + 1)
                v.tensor_scalar(e8a[:, :], gexp[:, g8], t2a[:, 0:1], None,
                                ALU.is_equal).then_inc(vch, 1)
                v.wait_ge(vch, NGOP * tt + 2)
                v.scalar_tensor_tensor(e8a[:, :], e8a[:, :], -1e30, gexp[:, g8],
                                       ALU.mult, ALU.add).then_inc(vch, 1)
                v.wait_ge(vch, NGOP * tt + 3)
                v.reduce_max(t2b[:, :], e8a[:, :], AX.X).then_inc(vch, 1)
                v.wait_ge(vch, NGOP * tt + 4)
                v.tensor_scalar(e8a[:, :], gexp[:, g8], t2b[:, 0:1], None,
                                ALU.is_ge).then_inc(vch, 1)
                v.wait_ge(vch, NGOP * tt + 5)
                v.tensor_tensor(e8a[:, :], e8a[:, :], gexp[:, g8], ALU.mult).then_inc(vch, 1)
                v.tensor_tensor(t2a[:, :], t2a[:, :], t2b[:, :], ALU.add).then_inc(vch, 1)
                v.wait_ge(vch, NGOP * tt + 7)
                v.reciprocal(t2a[:, :], t2a[:, :]).then_inc(vch, 1)
                v.wait_ge(vch, NGOP * tt + 8)
                v.tensor_scalar_mul(comb[:, g8], e8a[:, :], t2a[:, 0:1]).then_inc(vch, 1)
            v.wait_ge(vch, NGOP * NTT)
            for e in range(NE):
                pi = 0
                for (foff, fw) in PIECES:
                    for tt in range(NTT):
                        gidx = e * 16 + pi
                        v.wait_ge(sil, gidx + 1)
                        if gidx >= 2:
                            v.wait_ge(trs, 32 + gidx - 1)
                        v.scalar_tensor_tensor(a2t[gidx % 2][:, 0:fw],
                                               pu[gidx % 2][:, 0:fw],
                                               comb[:, tt * 8 + e: tt * 8 + e + 1],
                                               gs[gidx % 2][:, 0:fw],
                                               ALU.mult, ALU.mult).then_inc(a2s, 1)
                        pi += 1
                for db in range(4):
                    for tt in range(NTT):
                        di = e * 32 + db * 8 + tt
                        v.wait_ge(dmm, 6 * (di + 1))
                        v.tensor_tensor(stream[:, tt * D + db * 512: tt * D + (db + 1) * 512],
                                        ps[di % 2][:, :],
                                        stream[:, tt * D + db * 512: tt * D + (db + 1) * 512],
                                        ALU.add).then_inc(dev, 1)
            v.wait_ge(dev, NE * 32)
            for tt in range(NTT):
                ln_vector(tt, 56, 16)

        @block.scalar
        def _(s):
            def ln_scalar(tt, lnv_base, lns_base):
                sl = slice(tt * D, (tt + 1) * D)
                s.wait_ge(lns, lns_base + 2 * tt)
                s.activation(scr[:, :], stream[:, sl], AF.Square,
                             accum_out=sq[:, tt:tt + 1]).then_inc(lns, 1)
                s.wait_ge(lnv, lnv_base + 7 * tt + 4)
                s.activation(rsd[:, tt:tt + 1], varv[:, tt:tt + 1], AF.Sqrt,
                             bias=epsc[:, :], scale=-1.0).then_inc(lns, 1)

            s.wait_ge(evs, 32)
            for tt in range(NTT):
                ln_scalar(tt, 0, 0)
            ntr = 0
            for tt in range(NTT):
                for cg in range(4):
                    s.wait_ge(trs, ntr + 1)
                    s.copy(x_f[:, :].rearrange("p (c t) -> p c t", c=NC_CHUNKS)
                           [:, cg * 4:(cg + 1) * 4, tt * P:(tt + 1) * P],
                           pxf[ntr % 2][:, :].rearrange("p (c4 t) -> p c4 t", c4=4)
                           ).then_inc(xfs, 1)
                    ntr += 1
            for tt in range(NTT):
                s.wait_ge(mm, 32 + tt + 1)
                s.activation(gexp[:, tt * 8:(tt + 1) * 8], pg[0][:, 0:8],
                             AF.Exp).then_inc(gsem, 1)
            for e in range(NE):
                pi = 0
                for (foff, fw) in PIECES:
                    for tt in range(NTT):
                        gidx = e * 16 + pi
                        s.wait_ge(mm, mm_gu(gidx))
                        if gidx >= 2:
                            s.wait_ge(a2s, gidx - 1)
                        s.activation(gs[gidx % 2][:, 0:fw], pg[gidx % 2][:, 0:fw],
                                     AF.Silu).then_inc(sil, 1)
                        s.wait_ge(trs, 32 + gidx + 1)
                        if e >= 1 and pi == 0:
                            s.wait_ge(dmm, 48 * 4 * e)
                        nt = fw // P
                        s.copy(a2f[:, tt * 6 * P + (foff // P) * P:
                                   tt * 6 * P + (foff // P) * P + nt * P],
                               ptr[:, 0:nt * P]).then_inc(a2e, 1)
                        pi += 1
            s.wait_ge(dev, NE * 32)
            for tt in range(NTT):
                ln_scalar(tt, 56, 16)

        @block.sync
        def _(sy):
            for tt in range(NTT):
                sy.wait_ge(lnv, 56 + 7 * (tt + 1))
                sy.dma_start(out=x2_r[:, tt, :],
                             in_=stream[:, tt * D:(tt + 1) * D]).then_inc(sto, 16)
            sy.wait_ge(sto, 16 * NTT)

    return nc


def build_l1qkv():
    nc = bass.Bass("TRN2", target_bir_lowering=False)
    x2_f = nc.dram_tensor("x2_f", [D, TOK], F32, kind="ExternalInput")
    WqT = nc.dram_tensor("WqT", [D, D], F32, kind="ExternalInput")
    WkT = nc.dram_tensor("WkT", [D, D], F32, kind="ExternalInput")
    WvT = nc.dram_tensor("WvT", [D, D], F32, kind="ExternalInput")
    WgkT = nc.dram_tensor("WgkT", [D, 32], F32, kind="ExternalInput")
    rope_cos = nc.dram_tensor("rope_cos", [TOK, 64], F32, kind="ExternalInput")
    rope_sin = nc.dram_tensor("rope_sin", [TOK, 64], F32, kind="ExternalInput")
    qkv1 = nc.dram_tensor("qkv1", [TOK, 3 * D], BF16, kind="ExternalOutput")
    gk = nc.dram_tensor("gk", [TOK, 32], F32, kind="ExternalOutput")

    MATS = [WqT, WkT, WvT]

    with contextlib.ExitStack() as ctx:
        E = ctx.enter_context
        xfb = E(nc.sbuf_tensor("xfb", [P, NC_CHUNKS * TOK], BF16))
        wblk = [E(nc.sbuf_tensor(f"wblk{i}", [P, NC_CHUNKS * 512], BF16)) for i in range(2)]
        wgk = E(nc.sbuf_tensor("wgk", [P, NC_CHUNKS * 32], BF16))
        stage = E(nc.sbuf_tensor("stage", [P, NTT * D], BF16))
        cosb = E(nc.sbuf_tensor("cosb", [P, NTT * 64], BF16))
        sinb = E(nc.sbuf_tensor("sinb", [P, NTT * 64], BF16))
        tmp = [E(nc.sbuf_tensor(f"tmp{i}", [P, 1024], BF16)) for i in range(4)]
        gks = E(nc.sbuf_tensor("gks", [P, NTT * 32], F32))
        ps = [E(nc.psum_tensor(f"ps{i}", [P, 512], F32)) for i in range(2)]
        pgk = E(nc.psum_tensor("pgk", [P, 32], F32))
        ldx = E(nc.semaphore("ldx"))
        ldw = [E(nc.semaphore(f"ldw{i}")) for i in range(2)]
        mm = E(nc.semaphore("mm"))
        mgk = E(nc.semaphore("mgk"))
        esem = E(nc.semaphore("esem"))
        gsig = E(nc.semaphore("gsig"))
        vch = E(nc.semaphore("vch"))
        sto = E(nc.semaphore("sto"))
        stg = E(nc.semaphore("stg"))
        block = E(nc.Block())

        x_r = x2_f.ap().rearrange("(c p) t -> p c t", p=P)
        qkv_r = qkv1.ap().rearrange("(tt p) o -> p tt o", p=P)
        gk_r = gk.ap().rearrange("(tt p) e -> p tt e", p=P)

        @block.gpsimd
        def _(g):
            g.dma_start(out=xfb[:, :].rearrange("p (c t) -> p c t", c=NC_CHUNKS),
                        in_=x_r).then_inc(ldx, 16)
            g.dma_start(out=wgk[:, :].rearrange("p (c e) -> p c e", c=NC_CHUNKS),
                        in_=WgkT.ap().rearrange("(c p) e -> p c e", p=P)).then_inc(ldx, 16)
            g.dma_start(out=cosb[:, :].rearrange("p (tt f) -> p tt f", tt=NTT),
                        in_=rope_cos.ap().rearrange("(tt p) f -> p tt f", p=P)).then_inc(ldx, 16)
            g.dma_start(out=sinb[:, :].rearrange("p (tt f) -> p tt f", tt=NTT),
                        in_=rope_sin.ap().rearrange("(tt p) f -> p tt f", p=P)).then_inc(ldx, 16)
            bi = 0
            for mi in range(3):
                for ob in range(4):
                    if bi >= 2:
                        g.wait_ge(mm, 8 * (bi - 1))
                    g.dma_start(out=wblk[bi % 2][:, :].rearrange("p (c o) -> p c o", c=NC_CHUNKS),
                                in_=MATS[mi].ap().rearrange("(c p) o -> p c o", p=P)
                                [:, :, ob * 512:(ob + 1) * 512]).then_inc(ldw[bi % 2], 16)
                    bi += 1

        @block.tensor
        def _(t):
            t.wait_ge(ldx, 64)
            gi = 0
            for mi in range(3):
                for ob in range(4):
                    t.wait_ge(ldw[gi % 2], 16 * (gi // 2 + 1))
                    for tt in range(NTT):
                        grp = gi * 8 + tt
                        pb = ps[grp % 2]
                        if grp >= 2:
                            t.wait_ge(esem, grp - 1)
                        inst = None
                        for c in range(NC_CHUNKS):
                            inst = t.matmul(pb[:, :],
                                            xfb[:, c * TOK + tt * P: c * TOK + (tt + 1) * P],
                                            wblk[gi % 2][:, c * 512:(c + 1) * 512],
                                            start=(c == 0), stop=(c == NC_CHUNKS - 1))
                        inst.then_inc(mm, 1)
                    gi += 1
            for tt in range(NTT):
                if tt > 0:
                    t.wait_ge(gsig, tt)
                inst = None
                for c in range(NC_CHUNKS):
                    inst = t.matmul(pgk[:, :],
                                    xfb[:, c * TOK + tt * P: c * TOK + (tt + 1) * P],
                                    wgk[:, c * 32:(c + 1) * 32],
                                    start=(c == 0), stop=(c == NC_CHUNKS - 1))
                inst.then_inc(mgk, 1)

        @block.scalar
        def _(s):
            gi = 0
            for mi in range(3):
                for ob in range(4):
                    for tt in range(NTT):
                        grp = gi * 8 + tt
                        s.wait_ge(mm, grp + 1)
                        if mi > 0 and ob == 0:
                            s.wait_ge(sto, 16 * mi)
                        s.copy(stage[:, tt * D + ob * 512: tt * D + (ob + 1) * 512],
                               ps[grp % 2][:, :]).then_inc(esem, 1)
                    gi += 1
            for tt in range(NTT):
                s.wait_ge(mgk, tt + 1)
                s.activation(gks[:, tt * 32:(tt + 1) * 32], pgk[:, :],
                             AF.Sigmoid).then_inc(gsig, 1)
            for tt in range(NTT):
                s.wait_ge(gsig, NTT + tt)
                s.activation(gks[:, tt * 32:(tt + 1) * 32], gks[:, tt * 32:(tt + 1) * 32],
                             AF.Ln).then_inc(gsig, 1)

        @block.vector
        def _(v):
            nv = [0]
            for mi in range(2):
                for tt in range(NTT):
                    v.wait_ge(esem, mi * 32 + 24 + tt + 1)
                    v.wait_ge(vch, nv[0])
                    st3 = stage[:, tt * D:(tt + 1) * D].rearrange("p (h f) -> p h f", h=32)
                    x1 = st3[:, :, 0:32]
                    x2v = st3[:, :, 32:64]
                    cc = cosb[:, tt * 64: tt * 64 + 32].unsqueeze(1).broadcast_to([P, 32, 32])
                    ss = sinb[:, tt * 64: tt * 64 + 32].unsqueeze(1).broadcast_to([P, 32, 32])
                    t3 = [tmp[i][:, :].rearrange("p (h f) -> p h f", h=32) for i in range(4)]
                    v.tensor_tensor(t3[0], x1, cc, ALU.mult).then_inc(vch, 1); nv[0] += 1
                    v.tensor_tensor(t3[1], x2v, ss, ALU.mult).then_inc(vch, 1); nv[0] += 1
                    v.tensor_tensor(t3[2], x1, ss, ALU.mult).then_inc(vch, 1); nv[0] += 1
                    v.tensor_tensor(t3[3], x2v, cc, ALU.mult).then_inc(vch, 1); nv[0] += 1
                    v.wait_ge(vch, nv[0])
                    v.tensor_tensor(x1, t3[0], t3[1], ALU.subtract).then_inc(vch, 1); nv[0] += 1
                    v.tensor_tensor(x2v, t3[2], t3[3], ALU.add).then_inc(vch, 1); nv[0] += 1

        @block.sync
        def _(sy):
            for mi in range(3):
                if mi < 2:
                    sy.wait_ge(vch, 48 * (mi + 1))
                else:
                    sy.wait_ge(esem, 96)
                sy.dma_start(out=qkv_r[:, :, mi * D:(mi + 1) * D],
                             in_=stage[:, :].rearrange("p (tt o) -> p tt o", tt=NTT)
                             ).then_inc(sto, 16)
            sy.wait_ge(gsig, 16)
            sy.dma_start(out=gk_r, in_=gks[:, :].rearrange("p (tt e) -> p tt e", tt=NTT)
                         ).then_inc(sto, 16)
            sy.wait_ge(sto, 64)

    return nc


def build_l1attn():
    nc = bass.Bass("TRN2", target_bir_lowering=False)
    NP_ = 16
    T = 2048
    HD = 64
    q_f = nc.dram_tensor("q_f", [NP_, HD, T], BF16, kind="ExternalInput")
    k_f = nc.dram_tensor("k_f", [NP_, HD, T], BF16, kind="ExternalInput")
    v_t = nc.dram_tensor("v_t", [NP_, T, HD], BF16, kind="ExternalInput")
    gk_f = nc.dram_tensor("gk_f", [NP_, T], F32, kind="ExternalInput")
    maskbf = nc.dram_tensor("maskbf", [P, P], BF16, kind="ExternalInput")
    idbf = nc.dram_tensor("idbf", [P, P], BF16, kind="ExternalInput")
    id32 = nc.dram_tensor("id32", [P, P], F32, kind="ExternalInput")
    o_f = nc.dram_tensor("o_f", [NP_ * HD, T], BF16, kind="ExternalOutput")
    gdram = nc.dram_tensor("gdram", [NP_, T], F32)

    def chunks_of(qt):
        klen = (qt + 1) * P
        out = []
        kc = 0
        while kc * 512 < klen:
            out.append((kc, min(512, klen - kc * 512)))
            kc += 1
        return out

    sched = []
    c_smm = c_exp = c_ast = c_kt = c_oev = 0
    for p in range(NP_):
        for g in range(4):
            rec = {"smm0": c_smm, "exp0": c_exp, "ast0": c_ast, "kt0": c_kt, "oev0": c_oev}
            for i in range(4):
                qt = 4 * g + i
                n = len(chunks_of(qt))
                c_smm += n
                c_exp += n
                c_ast += n
            c_kt += 4 * g + 4
            c_oev += 1
            rec.update({"smm1": c_smm, "exp1": c_exp, "ast1": c_ast, "kt1": c_kt, "oev1": c_oev})
            sched.append(rec)

    with contextlib.ExitStack() as ctx:
        E = ctx.enter_context
        qsb = [E(nc.sbuf_tensor(f"qsb{i}", [HD, T], BF16)) for i in range(2)]
        ksb = [E(nc.sbuf_tensor(f"ksb{i}", [HD, T], BF16)) for i in range(2)]
        vsb = [E(nc.sbuf_tensor(f"vsb{i}", [P, 16 * HD], BF16)) for i in range(2)]
        mbf = E(nc.sbuf_tensor("mbf", [P, P], BF16))
        idb = E(nc.sbuf_tensor("idb", [P, P], BF16))
        id3 = E(nc.sbuf_tensor("id3", [P, P], F32))
        gsb = E(nc.sbuf_tensor("gsb", [NP_, T], F32))
        gcum = E(nc.sbuf_tensor("gcum", [NP_, T], F32))
        gz = E(nc.sbuf_tensor("gz", [NP_, T], F32))
        gt16 = E(nc.sbuf_tensor("gt16", [P, 16 * NP_], F32))
        grow = [E(nc.sbuf_tensor(f"grow{i}", [1, T], F32)) for i in range(2)]
        onec = E(nc.sbuf_tensor("onec", [1, P], F32))
        gkbc = E(nc.sbuf_tensor("gkbc", [P, T], F32))
        esb = [E(nc.sbuf_tensor(f"esb{i}", [P, 512], BF16)) for i in range(2)]
        asb = E(nc.sbuf_tensor("asb", [P, 4 * T], BF16))
        ptsb = [E(nc.sbuf_tensor(f"ptsb{i}", [P, 512], BF16)) for i in range(2)]
        osb = [E(nc.sbuf_tensor(f"osb{i}", [HD, T], BF16)) for i in range(2)]
        ps_s = [E(nc.psum_tensor(f"ps_s{i}", [P, 512], F32)) for i in range(2)]
        pt_ps = [E(nc.psum_tensor(f"pt_ps{i}", [P, 1024], BF16)) for i in range(2)]
        o_ps = E(nc.psum_tensor("o_ps", [P, 512], F32))
        pbc = E(nc.psum_tensor("pbc", [P, 512], F32))
        cld = E(nc.semaphore("cld"))
        qld = E(nc.semaphore("qld"))
        gld = [E(nc.semaphore(f"gld{i}")) for i in range(2)]
        gsc = E(nc.semaphore("gsc"))
        gtr = E(nc.semaphore("gtr"))
        bcs = E(nc.semaphore("bcs"))
        bce = E(nc.semaphore("bce"))
        ssem = E(nc.semaphore("ssem"))
        esem = E(nc.semaphore("esem"))
        asem = E(nc.semaphore("asem"))
        tsem = E(nc.semaphore("tsem"))
        ptsem = E(nc.semaphore("ptsem"))
        avsem = E(nc.semaphore("avsem"))
        osem = E(nc.semaphore("osem"))
        sto = [E(nc.semaphore(f"sto{i}")) for i in range(2)]
        gst = E(nc.semaphore("gst"))
        vsf = E(nc.semaphore("vsf"))
        block = E(nc.Block())

        @block.sync
        def _(sy):
            sy.dma_start(out=mbf[:, :], in_=maskbf[:, :]).then_inc(cld, 16)
            sy.dma_start(out=idb[:, :], in_=idbf[:, :]).then_inc(cld, 16)
            sy.dma_start(out=id3[:, :], in_=id32[:, :]).then_inc(cld, 16)
            sy.dma_start(out=gsb[:, :], in_=gk_f[:, :]).then_inc(cld, 16)
            for p in range(NP_):
                if p >= 1:
                    sy.wait_ge(qld, 48 * p)
                if p >= 2:
                    sy.wait_ge(avsem, sched[(p - 2) * 4 + 3]["kt1"])
                sy.dma_start(out=qsb[p % 2][:, :], in_=q_f[p, :, :]).then_inc(qld, 16)
                sy.dma_start(out=ksb[p % 2][:, :], in_=k_f[p, :, :]).then_inc(qld, 16)
                sy.dma_start(out=vsb[p % 2][:, :].rearrange("p (kt d) -> p kt d", kt=16),
                             in_=v_t[p, :, :].rearrange("(kt p) d -> p kt d", p=P)
                             ).then_inc(qld, 16)

        @block.gpsimd
        def _(g):
            g.memset(gz[:, :], 0.0)
            g.memset(onec[:, :], 1.0).then_inc(cld, 1)
            g.wait_ge(gsc, 1)
            g.dma_start(out=gdram[:, :], in_=gcum[:, :]).then_inc(gst, 16)
            g.wait_ge(gst, 16)
            for p in range(NP_):
                if p >= 2:
                    g.wait_ge(bcs, 4 * (p - 1))
                g.dma_start(out=grow[p % 2][:, :], in_=gdram[p:p + 1, :]).then_inc(gld[p % 2], 16)

        @block.vector
        def _(v):
            nv = [0]
            v.wait_ge(cld, 65)
            v.tensor_tensor_scan(gcum[:, :], gsb[:, :], gz[:, :], 0.0,
                                 ALU.add, ALU.add).then_inc(gsc, 1)
            for p in range(NP_):
                for g in range(4):
                    rec = sched[p * 4 + g]
                    smm = rec["smm0"]
                    ast = rec["ast0"]
                    for i in range(4):
                        qt = 4 * g + i
                        chs = chunks_of(qt)
                        for (kc, cw) in chs:
                            isdiag = (kc == len(chs) - 1)
                            v.wait_ge(esem, smm + 1)
                            if isdiag:
                                v.tensor_tensor(esb[smm % 2][:, cw - P:cw],
                                                esb[smm % 2][:, cw - P:cw],
                                                mbf[:, :], ALU.mult).then_inc(vsf, 1)
                                nv[0] += 1
                                v.wait_ge(vsf, nv[0])
                            v.wait_ge(ssem, smm + 1)
                            v.scalar_tensor_tensor(asb[:, i * T + kc * 512: i * T + kc * 512 + cw],
                                                   ps_s[smm % 2][:, 0:cw], 0.125,
                                                   esb[smm % 2][:, 0:cw],
                                                   ALU.mult, ALU.mult).then_inc(asem, 1)
                            smm += 1
                            ast += 1
                    ktc = rec["kt0"]
                    for kt in range(4 * g + 4):
                        i0 = max(0, kt - 4 * g)
                        v.wait_ge(tsem, ktc + 1)
                        v.tensor_copy(ptsb[ktc % 2][:, i0 * P:512],
                                      pt_ps[ktc % 2][:, i0 * P:512]).then_inc(ptsem, 1)
                        ktc += 1

        @block.scalar
        def _(s):
            s.wait_ge(cld, 65)
            for kt in range(16):
                s.wait_ge(gtr, 2 * kt + 1)
                s.activation(gt16[:, kt * NP_:(kt + 1) * NP_], pbc[:, 0:NP_],
                             AF.Copy, scale=1.0 / 16.0).then_inc(gtr, 1)
            for p in range(NP_):
                rec0 = sched[p * 4]
                for j in range(4):
                    s.wait_ge(bcs, 4 * p + j + 1)
                    s.copy(gkbc[:, j * 512:(j + 1) * 512], pbc[:, :]).then_inc(bce, 1)
                for g in range(4):
                    rec = sched[p * 4 + g]
                    smm = rec["smm0"]
                    if g == 0:
                        s.wait_ge(bce, 4 * (p + 1))
                        if p >= 2:
                            s.wait_ge(sto[p % 2], 16 * (p // 2))
                    for i in range(4):
                        qt = 4 * g + i
                        for (kc, cw) in chunks_of(qt):
                            if smm >= 2:
                                s.wait_ge(asem, smm - 1)
                            s.activation(esb[smm % 2][:, 0:cw], gkbc[:, kc * 512:kc * 512 + cw],
                                         AF.Exp, bias=gt16[:, qt * NP_ + p: qt * NP_ + p + 1],
                                         scale=-1.0 / 16.0).then_inc(esem, 1)
                            smm += 1
                    s.wait_ge(avsem, rec["kt1"])
                    s.copy(osb[p % 2][:, g * 512:(g + 1) * 512], o_ps[0:HD, :]).then_inc(osem, 1)
                s.wait_ge(osem, 4 * (p + 1))
                s.dma_start(out=o_f[p * HD:(p + 1) * HD, :],
                            in_=osb[p % 2][:, :]).then_inc(sto[p % 2], 16)
            s.wait_ge(sto[0], 16 * (NP_ // 2))
            s.wait_ge(sto[1], 16 * (NP_ // 2))

        @block.tensor
        def _(t):
            t.wait_ge(cld, 65)
            t.wait_ge(gsc, 1)
            for kt in range(16):
                if kt >= 1:
                    t.wait_ge(gtr, 2 * kt)
                t.transpose(pbc[:, 0:NP_], gcum[0:NP_, kt * P:(kt + 1) * P],
                            id3[0:NP_, 0:NP_]).then_inc(gtr, 1)
            t.wait_ge(gtr, 32)
            for p in range(NP_):
                t.wait_ge(qld, 48 * (p + 1))
                t.wait_ge(gld[p % 2], 16 * (p // 2 + 1))
                if p >= 1:
                    t.wait_ge(esem, sched[(p - 1) * 4 + 3]["exp1"])
                for j in range(4):
                    if p > 0 or j > 0:
                        t.wait_ge(bce, 4 * p + j)
                    t.matmul(pbc[:, :], onec[:, :], grow[p % 2][0:1, j * 512:(j + 1) * 512],
                             start=True, stop=True).then_inc(bcs, 1)
                t.wait_ge(bce, 4 * (p + 1))
                for g in range(4):
                    rec = sched[p * 4 + g]
                    smm = rec["smm0"]
                    for i in range(4):
                        qt = 4 * g + i
                        for (kc, cw) in chunks_of(qt):
                            if smm >= 2:
                                t.wait_ge(asem, smm - 1)
                            t.matmul(ps_s[smm % 2][:, 0:cw],
                                     qsb[p % 2][:, qt * P:(qt + 1) * P],
                                     ksb[p % 2][:, kc * 512:kc * 512 + cw],
                                     start=True, stop=True).then_inc(ssem, 1)
                            smm += 1
                    t.wait_ge(asem, rec["ast1"])
                    ktc = rec["kt0"]
                    for kt in range(4 * g + 4):
                        i0 = max(0, kt - 4 * g)
                        if ktc >= 2:
                            t.wait_ge(ptsem, ktc - 1)
                        last = None
                        for i in range(i0, 4):
                            last = t.transpose(pt_ps[ktc % 2][:, i * P:(i + 1) * P],
                                               asb[:, i * T + kt * P: i * T + (kt + 1) * P],
                                               idb[:, :])
                        last.then_inc(tsem, 1)
                        t.wait_ge(ptsem, ktc + 1)
                        if kt == 0 and rec["oev0"] > 0:
                            t.wait_ge(osem, rec["oev0"])
                        t.matmul(o_ps[0:HD, i0 * P:512],
                                 vsb[p % 2][:, kt * HD:(kt + 1) * HD],
                                 ptsb[ktc % 2][:, i0 * P:512],
                                 start=(kt == 0), stop=(kt == 4 * g + 3)).then_inc(avsem, 1)
                        ktc += 1

    return nc


def build_post():
    nc = bass.Bass("TRN2", target_bir_lowering=False)
    HID = 5632
    NHC = HID // P
    x2_t = nc.dram_tensor("x2_t", [TOK, D], F32, kind="ExternalInput")
    x2_f = nc.dram_tensor("x2_f", [D, TOK], F32, kind="ExternalInput")
    o_fb = nc.dram_tensor("o_fb", [D, TOK], BF16, kind="ExternalInput")
    WgT = nc.dram_tensor("WgT", [D, D], F32, kind="ExternalInput")
    WoT = nc.dram_tensor("WoT", [D, D], F32, kind="ExternalInput")
    wqgT = nc.dram_tensor("wqgT", [D, HID], I32, kind="ExternalInput")
    wquT = nc.dram_tensor("wquT", [D, HID], I32, kind="ExternalInput")
    wqdT = nc.dram_tensor("wqdT", [HID, D], I32, kind="ExternalInput")
    sg2d = nc.dram_tensor("sg2d", [P, NHC], F32, kind="ExternalInput")
    su2d = nc.dram_tensor("su2d", [P, NHC], F32, kind="ExternalInput")
    sd2d = nc.dram_tensor("sd2d", [P, NC_CHUNKS], F32, kind="ExternalInput")
    idbf = nc.dram_tensor("idbf", [P, P], BF16, kind="ExternalInput")
    id32 = nc.dram_tensor("id32", [P, P], F32, kind="ExternalInput")
    y = nc.dram_tensor("y", [TOK, D], F32, kind="ExternalOutput")

    EPS_LN = 1e-5
    EPS_RMS = 1e-6

    with contextlib.ExitStack() as ctx:
        E = ctx.enter_context
        stream = E(nc.sbuf_tensor("stream", [P, NTT * D], F32))
        xfb = E(nc.sbuf_tensor("xfb", [P, NC_CHUNKS * TOK], BF16))
        ibfsb = E(nc.sbuf_tensor("ibfsb", [P, P], BF16))
        i32sb = E(nc.sbuf_tensor("i32sb", [P, P], F32))
        sgc = E(nc.sbuf_tensor("sgc", [P, NHC], F32))
        suc = E(nc.sbuf_tensor("suc", [P, NHC], F32))
        sdc = E(nc.sbuf_tensor("sdc", [P, NC_CHUNKS], F32))
        ones1 = E(nc.sbuf_tensor("ones1", [1, P], BF16))
        nones = E(nc.sbuf_tensor("nones", [P, 1], BF16))
        sq = E(nc.sbuf_tensor("sq", [P, NTT], F32))
        sums = E(nc.sbuf_tensor("sums", [P, NTT], F32))
        mu = E(nc.sbuf_tensor("mu", [P, NTT], F32))
        varv = E(nc.sbuf_tensor("varv", [P, NTT], F32))
        rsd = E(nc.sbuf_tensor("rsd", [P, NTT], F32))
        epsln = E(nc.sbuf_tensor("epsln", [P, 1], F32))
        epsrm = E(nc.sbuf_tensor("epsrm", [P, 1], F32))
        scr = E(nc.sbuf_tensor("scr", [P, D], F32))
        hvar = E(nc.sbuf_tensor("hvar", [P, 32], F32))
        rsrow = E(nc.sbuf_tensor("rsrow", [1, TOK], BF16))
        csrow = E(nc.sbuf_tensor("csrow", [1, TOK], BF16))
        ctxA = contextlib.ExitStack()
        ogf = ctxA.enter_context(nc.sbuf_tensor("ogf", [P, NC_CHUNKS * TOK], BF16))
        sg_sb = ctxA.enter_context(nc.sbuf_tensor("sg_sb", [P, NTT * D], BF16))
        otb = [ctxA.enter_context(nc.sbuf_tensor(f"otb{i}", [P, D], BF16)) for i in range(2)]
        wb = [ctxA.enter_context(nc.sbuf_tensor(f"wb{i}", [P, NC_CHUNKS * 256], BF16)) for i in range(2)]
        ctxA.close()
        a2sb = E(nc.sbuf_tensor("a2sb", [P, NHC * 512], BF16))
        wqg_b = [E(nc.sbuf_tensor(f"wqg_b{i}", [P, NC_CHUNKS * P], BF16)) for i in range(2)]
        wqu_b = [E(nc.sbuf_tensor(f"wqu_b{i}", [P, NC_CHUNKS * P], BF16)) for i in range(2)]
        wqd_b = [E(nc.sbuf_tensor(f"wqd_b{i}", [P, NHC * P], BF16)) for i in range(2)]
        sgh = [E(nc.sbuf_tensor(f"sgh{i}", [P, 512], BF16)) for i in range(2)]
        yfb = [E(nc.sbuf_tensor(f"yfb{i}", [P, 512], BF16)) for i in range(2)]
        ps = [E(nc.psum_tensor(f"ps{i}", [P, 512], F32)) for i in range(2)]
        pu = [E(nc.psum_tensor(f"pu{i}", [P, 512], F32)) for i in range(2)]
        ptrb = [E(nc.psum_tensor(f"ptr{i}", [P, 1024], BF16)) for i in range(2)]
        prow = E(nc.psum_tensor("prow", [P, 512], F32))

        ldx = E(nc.semaphore("ldx"))
        ldg = E(nc.semaphore("ldg"))
        ldo = [E(nc.semaphore(f"ldo{i}")) for i in range(2)]
        ldw = [E(nc.semaphore(f"ldw{i}")) for i in range(2)]
        ldq = [E(nc.semaphore(f"ldq{i}")) for i in range(2)]
        ldd = [E(nc.semaphore(f"ldd{i}")) for i in range(2)]
        ldm = E(nc.semaphore("ldm"))
        mm = E(nc.semaphore("mm"))
        evs = E(nc.semaphore("evs"))
        lnv = E(nc.semaphore("lnv"))
        lns = E(nc.semaphore("lns"))
        onr = E(nc.semaphore("onr"))
        sil = E(nc.semaphore("sil"))
        trs = E(nc.semaphore("trs"))
        xfs = E(nc.semaphore("xfs"))
        rsm = E(nc.semaphore("rsm"))
        rse = E(nc.semaphore("rse"))
        gup = E(nc.semaphore("gup"))
        a2s = E(nc.semaphore("a2s"))
        dmm = E(nc.semaphore("dmm"))
        dse = E(nc.semaphore("dse"))
        dtr = E(nc.semaphore("dtr"))
        dev = E(nc.semaphore("dev"))
        sto = E(nc.semaphore("sto"))
        block = E(nc.Block())

        x_r = x2_t.ap().rearrange("(tt p) d -> p tt d", p=P)
        y_r = y.ap().rearrange("(tt p) d -> p tt d", p=P)

        NGP = 64
        NWO = 64

        @block.sync
        def _(sy):
            sy.dma_start(out=stream[:, :].rearrange("p (tt d) -> p tt d", tt=NTT),
                         in_=x_r).then_inc(ldx, 16)
            sy.dma_start(out=ibfsb[:, :], in_=idbf[:, :]).then_inc(ldx, 16)
            sy.dma_start(out=i32sb[:, :], in_=id32[:, :]).then_inc(ldx, 16)
            sy.dma_start(out=sgc[:, :], in_=sg2d[:, :]).then_inc(ldx, 16)
            sy.dma_start(out=suc[:, :], in_=su2d[:, :]).then_inc(ldx, 16)
            sy.dma_start(out=sdc[:, :], in_=sd2d[:, :]).then_inc(ldx, 16)
            for tt in range(NTT):
                if tt >= 2:
                    sy.wait_ge(onr, 5 * (tt - 1))
                sy.dma_start(out=otb[tt % 2][:, :],
                             in_=o_fb[:, tt * P:(tt + 1) * P],
                             transpose=True).then_inc(ldo[tt % 2], 16)

        @block.gpsimd
        def _(g):
            g.memset(ones1[:, :], 1.0)
            g.memset(nones[:, :], -128.0)
            g.memset(epsln[:, :], EPS_LN)
            g.memset(epsrm[:, :], EPS_RMS).then_inc(ldm, 1)
            g.dma_start(out=xfb[:, :].rearrange("p (c t) -> p c t", c=NC_CHUNKS),
                        in_=x2_f.ap().rearrange("(c p) t -> p c t", p=P)).then_inc(ldg, 16)
            bi = 0
            for mat in (WgT, WoT):
                for ob in range(8):
                    if bi >= 2:
                        g.wait_ge(mm, 8 * (bi - 1))
                    g.dma_start(out=wb[bi % 2][:, :].rearrange("p (c o) -> p c o", c=NC_CHUNKS),
                                in_=mat.ap().rearrange("(c p) o -> p c o", p=P)
                                [:, :, ob * 256:(ob + 1) * 256]).then_inc(ldw[bi % 2], 16)
                    bi += 1
            qi = 0
            for th in range(2):
                for hc in range(NHC):
                    if qi >= 2:
                        g.wait_ge(gup, 2 * (qi - 1))
                    else:
                        g.wait_ge(mm, NGP + NWO)
                    g.dma_start(out=wqg_b[qi % 2][:, :].rearrange("p (c h) -> p c h", c=NC_CHUNKS),
                                in_=wqgT.ap().rearrange("(c p) h -> p c h", p=P)
                                [:, :, hc * P:(hc + 1) * P]).then_inc(ldq[qi % 2], 16)
                    g.dma_start(out=wqu_b[qi % 2][:, :].rearrange("p (c h) -> p c h", c=NC_CHUNKS),
                                in_=wquT.ap().rearrange("(c p) h -> p c h", p=P)
                                [:, :, hc * P:(hc + 1) * P]).then_inc(ldq[qi % 2], 16)
                    qi += 1
                for dc in range(NC_CHUNKS):
                    di = th * NC_CHUNKS + dc
                    if di >= 2:
                        g.wait_ge(dse, di - 1)
                    else:
                        g.wait_ge(mm, NGP + NWO)
                    g.dma_start(out=wqd_b[di % 2][:, :].rearrange("p (c d) -> p c d", c=NHC),
                                in_=wqdT.ap().rearrange("(c p) d -> p c d", p=P)
                                [:, :, dc * P:(dc + 1) * P]).then_inc(ldd[di % 2], 16)

        @block.tensor
        def _(t):
            t.wait_ge(ldx, 96)
            t.wait_ge(ldg, 16)
            t.wait_ge(ldm, 1)
            gi = 0
            for ob in range(8):
                t.wait_ge(ldw[ob % 2], 16 * (ob // 2 + 1))
                for tt in range(NTT):
                    pb = ps[gi % 2]
                    if gi >= 2:
                        t.wait_ge(sil, gi - 1)
                    inst = None
                    for c in range(NC_CHUNKS):
                        inst = t.matmul(pb[:, 0:256],
                                        xfb[:, c * TOK + tt * P: c * TOK + (tt + 1) * P],
                                        wb[ob % 2][:, c * 256:(c + 1) * 256],
                                        start=(c == 0), stop=(c == NC_CHUNKS - 1))
                    inst.then_inc(mm, 1)
                    gi += 1
            ntr = 0
            for tt in range(NTT):
                t.wait_ge(onr, 5 * tt + 5)
                for cg in range(4):
                    if ntr >= 2:
                        t.wait_ge(xfs, ntr - 1)
                    last = None
                    for c4 in range(4):
                        c = cg * 4 + c4
                        last = t.transpose(ptrb[ntr % 2][:, c4 * P:(c4 + 1) * P],
                                           sg_sb[:, tt * D + c * P: tt * D + (c + 1) * P],
                                           ibfsb[:, :])
                    last.then_inc(trs, 1)
                    ntr += 1
            gi = 0
            for ob in range(8):
                bi = 8 + ob
                t.wait_ge(ldw[bi % 2], 16 * (bi // 2 + 1))
                for tt in range(NTT):
                    grp = NGP + gi
                    pb = ps[grp % 2]
                    if gi >= 2:
                        t.wait_ge(evs, gi - 1)
                    else:
                        t.wait_ge(sil, NGP)
                    t.wait_ge(xfs, 4 * (tt + 1))
                    inst = None
                    for c in range(NC_CHUNKS):
                        inst = t.matmul(pb[:, 0:256],
                                        ogf[:, c * TOK + tt * P: c * TOK + (tt + 1) * P],
                                        wb[bi % 2][:, c * 256:(c + 1) * 256],
                                        start=(c == 0), stop=(c == NC_CHUNKS - 1))
                    inst.then_inc(mm, 1)
                    gi += 1
            ntr = 32
            for tt in range(NTT):
                t.wait_ge(lnv, 7 * (tt + 1))
                for cg in range(4):
                    if ntr >= 34:
                        t.wait_ge(xfs, ntr - 1)
                    else:
                        t.wait_ge(evs, NWO)
                    last = None
                    for c4 in range(4):
                        c = cg * 4 + c4
                        last = t.transpose(ps[ntr % 2][:, c4 * P:(c4 + 1) * P],
                                           stream[:, tt * D + c * P: tt * D + (c + 1) * P],
                                           i32sb[:, :])
                    last.then_inc(trs, 1)
                    ntr += 1
            for th in range(2):
                tsl = slice(th * 512, (th + 1) * 512)
                t.wait_ge(xfs, 64)
                if th == 1:
                    t.wait_ge(rse, 2)
                inst = None
                for c in range(NC_CHUNKS):
                    inst = t.matmul(prow[0:1, :], nones[:, :],
                                    xfb[:, c * TOK + th * 512: c * TOK + (th + 1) * 512],
                                    start=(c == 0), stop=(c == NC_CHUNKS - 1))
                inst.then_inc(rsm, 1)
                t.wait_ge(rse, 2 * th + 1)
                for hc in range(NHC):
                    qi = th * NHC + hc
                    t.wait_ge(ldq[qi % 2], 32 * (qi // 2 + 1))
                    if qi >= 1:
                        t.wait_ge(gup, 2 * qi)
                    for c in range(NC_CHUNKS):
                        t.matmul(ps[0][:, :],
                                 wqg_b[qi % 2][:, c * P:(c + 1) * P],
                                 xfb[:, c * TOK + th * 512: c * TOK + (th + 1) * 512],
                                 start=(c == 0), stop=False)
                    t.matmul(ps[0][:, :], ones1[:, :], rsrow[0:1, tsl],
                             start=False, stop=True)
                    inst = None
                    for c in range(NC_CHUNKS):
                        t.matmul(pu[0][:, :],
                                 wqu_b[qi % 2][:, c * P:(c + 1) * P],
                                 xfb[:, c * TOK + th * 512: c * TOK + (th + 1) * 512],
                                 start=(c == 0), stop=False)
                    inst = t.matmul(pu[0][:, :], ones1[:, :], rsrow[0:1, tsl],
                                    start=False, stop=True)
                    inst.then_inc(mm, 1)
                t.wait_ge(a2s, NHC * (th + 1))
                inst = None
                for hc in range(NHC):
                    inst = t.matmul(prow[0:1, :], nones[:, :], a2sb[:, hc * 512:(hc + 1) * 512],
                                    start=(hc == 0), stop=(hc == NHC - 1))
                inst.then_inc(rsm, 1)
                t.wait_ge(rse, 2 * th + 2)
                for dc in range(NC_CHUNKS):
                    di = th * NC_CHUNKS + dc
                    t.wait_ge(ldd[di % 2], 16 * (di // 2 + 1))
                    if di >= 1:
                        t.wait_ge(dse, di)
                    for hc in range(NHC):
                        t.matmul(pu[1][:, :],
                                 wqd_b[di % 2][:, hc * P:(hc + 1) * P],
                                 a2sb[:, hc * 512:(hc + 1) * 512],
                                 start=(hc == 0), stop=False)
                    inst = t.matmul(pu[1][:, :], ones1[:, :], csrow[0:1, tsl],
                                    start=False, stop=True)
                    inst.then_inc(dmm, 1)
                    t.wait_ge(dse, di + 1)
                    if di >= 2:
                        t.wait_ge(dev, di - 1)
                    last = None
                    for j in range(4):
                        last = t.transpose(ptrb[di % 2][:, j * P:(j + 1) * P],
                                           yfb[di % 2][:, j * P:(j + 1) * P],
                                           ibfsb[:, :])
                    last.then_inc(dtr, 1)

        @block.vector
        def _(v):
            nv = [0]

            def ln_vector(tt, lnv_base, lns_base):
                sl = slice(tt * D, (tt + 1) * D)
                v.reduce_sum(sums[:, tt:tt + 1], stream[:, sl], AX.X).then_inc(lnv, 1)
                v.wait_ge(lns, lns_base + 2 * tt + 1)
                v.wait_ge(lnv, lnv_base + 7 * tt + 1)
                v.tensor_scalar_mul(mu[:, tt:tt + 1], sums[:, tt:tt + 1], 1.0 / D).then_inc(lnv, 1)
                v.wait_ge(lnv, lnv_base + 7 * tt + 2)
                v.tensor_scalar_mul(varv[:, tt:tt + 1], sq[:, tt:tt + 1], 1.0 / D).then_inc(lnv, 1)
                v.wait_ge(lnv, lnv_base + 7 * tt + 3)
                v.scalar_tensor_tensor(varv[:, tt:tt + 1], mu[:, tt:tt + 1], mu[:, tt:tt + 1],
                                       varv[:, tt:tt + 1], ALU.mult, ALU.subtract).then_inc(lnv, 1)
                v.wait_ge(lns, lns_base + 2 * tt + 2)
                v.reciprocal(rsd[:, tt:tt + 1], rsd[:, tt:tt + 1]).then_inc(lnv, 1)
                v.wait_ge(lnv, lnv_base + 7 * tt + 5)
                v.tensor_scalar(stream[:, sl], stream[:, sl],
                                mu[:, tt:tt + 1], rsd[:, tt:tt + 1],
                                ALU.subtract, ALU.mult).then_inc(lnv, 2)

            for tt in range(NTT):
                v.wait_ge(ldo[tt % 2], 16 * (tt // 2 + 1))
                v.wait_ge(onr, 5 * tt)
                v.wait_ge(lns, 2 * tt + 1)
                o3 = otb[tt % 2][:, :].rearrange("p (h f) -> p h f", h=32)
                v.reduce_sum(hvar[:, :].unsqueeze(2), scr[:, :].rearrange("p (h f) -> p h f", h=32),
                             AX.X).then_inc(onr, 1)
                v.wait_ge(onr, 5 * tt + 1)
                v.wait_ge(lns, 2 * tt + 2)
                v.reciprocal(hvar[:, :], hvar[:, :]).then_inc(onr, 1)
                v.wait_ge(onr, 5 * tt + 2)
                v.tensor_tensor(otb[tt % 2][:, :].rearrange("p (h f) -> p h f", h=32),
                                o3, hvar[:, :].unsqueeze(2).broadcast_to([P, 32, 64]),
                                ALU.mult).then_inc(onr, 1)
                v.wait_ge(onr, 5 * tt + 3)
                v.wait_ge(sil, NGP)
                v.tensor_tensor(sg_sb[:, tt * D:(tt + 1) * D],
                                sg_sb[:, tt * D:(tt + 1) * D],
                                otb[tt % 2][:, :], ALU.mult).then_inc(onr, 2)
            gi = 0
            for ob in range(8):
                for tt in range(NTT):
                    grp = NGP + gi
                    v.wait_ge(mm, grp + 1)
                    v.tensor_tensor(stream[:, tt * D + ob * 256: tt * D + (ob + 1) * 256],
                                    ps[grp % 2][:, 0:256],
                                    stream[:, tt * D + ob * 256: tt * D + (ob + 1) * 256],
                                    ALU.add).then_inc(evs, 1)
                    gi += 1
            v.wait_ge(evs, NWO)
            for tt in range(NTT):
                ln_vector(tt, 0, 16)
            for th in range(2):
                for hc in range(NHC):
                    qi = th * NHC + hc
                    v.wait_ge(gup, 2 * (qi + 1))
                    v.tensor_tensor(a2sb[:, hc * 512:(hc + 1) * 512],
                                    sgh[qi % 2][:, :],
                                    a2sb[:, hc * 512:(hc + 1) * 512],
                                    ALU.mult).then_inc(a2s, 1)
                for dc in range(NC_CHUNKS):
                    di = th * NC_CHUNKS + dc
                    v.wait_ge(dtr, di + 1)
                    for j in range(4):
                        tt = th * 4 + j
                        inst = v.tensor_tensor(stream[:, tt * D + dc * P: tt * D + (dc + 1) * P],
                                               ptrb[di % 2][:, j * P:(j + 1) * P],
                                               stream[:, tt * D + dc * P: tt * D + (dc + 1) * P],
                                               ALU.add)
                        if j == 3:
                            inst.then_inc(dev, 1)
            v.wait_ge(dev, 32)
            for tt in range(NTT):
                ln_vector(tt, 56, 32)

        @block.scalar
        def _(s):
            def ln_scalar(tt, lnv_base, lns_base):
                sl = slice(tt * D, (tt + 1) * D)
                s.wait_ge(lns, lns_base + 2 * tt)
                s.activation(scr[:, :], stream[:, sl], AF.Square,
                             accum_out=sq[:, tt:tt + 1]).then_inc(lns, 1)
                s.wait_ge(lnv, lnv_base + 7 * tt + 4)
                s.activation(rsd[:, tt:tt + 1], varv[:, tt:tt + 1], AF.Sqrt,
                             bias=epsln[:, :], scale=-1.0).then_inc(lns, 1)

            s.wait_ge(ldm, 1)
            gi = 0
            for ob in range(8):
                for tt in range(NTT):
                    s.wait_ge(mm, gi + 1)
                    s.activation(sg_sb[:, tt * D + ob * 256: tt * D + (ob + 1) * 256],
                                 ps[gi % 2][:, 0:256], AF.Silu).then_inc(sil, 1)
                    gi += 1
            for tt in range(NTT):
                s.wait_ge(ldo[tt % 2], 16 * (tt // 2 + 1))
                if tt >= 1:
                    s.wait_ge(onr, 5 * (tt - 1) + 1)
                s.activation(scr[:, :], otb[tt % 2][:, :], AF.Square).then_inc(lns, 1)
                s.wait_ge(onr, 5 * tt + 1)
                s.activation(hvar[:, :], hvar[:, :], AF.Sqrt,
                             bias=epsrm[:, :], scale=1.0 / 64.0).then_inc(lns, 1)
            ntr = 0
            for tt in range(NTT):
                for cg in range(4):
                    s.wait_ge(trs, ntr + 1)
                    s.copy(ogf[:, :].rearrange("p (c t) -> p c t", c=NC_CHUNKS)
                           [:, cg * 4:(cg + 1) * 4, tt * P:(tt + 1) * P],
                           ptrb[ntr % 2][:, 0:512].rearrange("p (c4 t) -> p c4 t", c4=4)
                           ).then_inc(xfs, 1)
                    ntr += 1
            s.wait_ge(evs, NWO)
            for tt in range(NTT):
                ln_scalar(tt, 0, 16)
            for tt in range(NTT):
                for cg in range(4):
                    s.wait_ge(trs, ntr + 1)
                    s.copy(xfb[:, :].rearrange("p (c t) -> p c t", c=NC_CHUNKS)
                           [:, cg * 4:(cg + 1) * 4, tt * P:(tt + 1) * P],
                           ps[ntr % 2][:, :].rearrange("p (c4 t) -> p c4 t", c4=4)
                           ).then_inc(xfs, 1)
                    ntr += 1
            for th in range(2):
                tsl = slice(th * 512, (th + 1) * 512)
                s.wait_ge(rsm, 2 * th + 1)
                s.copy(rsrow[0:1, tsl], prow[0:1, :]).then_inc(rse, 1)
                for hc in range(NHC):
                    qi = th * NHC + hc
                    s.wait_ge(mm, NGP + NWO + qi + 1)
                    if qi >= 2:
                        s.wait_ge(a2s, qi - 1)
                    s.activation(sgh[qi % 2][:, :], ps[0][:, :], AF.Silu,
                                 scale=sgc[:, hc:hc + 1]).then_inc(gup, 1)
                    s.activation(a2sb[:, hc * 512:(hc + 1) * 512], pu[0][:, :], AF.Copy,
                                 scale=suc[:, hc:hc + 1]).then_inc(gup, 1)
                s.wait_ge(rsm, 2 * th + 2)
                s.copy(csrow[0:1, tsl], prow[0:1, :]).then_inc(rse, 1)
                for dc in range(NC_CHUNKS):
                    di = th * NC_CHUNKS + dc
                    s.wait_ge(dmm, di + 1)
                    if di >= 2:
                        s.wait_ge(dtr, di - 1)
                    s.activation(yfb[di % 2][:, :], pu[1][:, :], AF.Copy,
                                 scale=sdc[:, dc:dc + 1]).then_inc(dse, 1)
            s.wait_ge(dev, 32)
            for tt in range(NTT):
                ln_scalar(tt, 56, 32)

        @block.sync
        def _(sy):
            for tt in range(NTT):
                sy.wait_ge(lnv, 56 + 7 * (tt + 1))
                sy.dma_start(out=y_r[:, tt, :],
                             in_=stream[:, tt * D:(tt + 1) * D]).then_inc(sto, 16)
            sy.wait_ge(sto, 16 * NTT)

    return nc


from concourse.bass_utils import run_bass_kernel_spmd
import ml_dtypes

BF = ml_dtypes.bfloat16
_NC = 8


def _cc(a):
    return np.ascontiguousarray(a)


def _planar_perm():
    perm = np.empty(2048, np.int64)
    for h in range(32):
        for i in range(64):
            perm[h * 64 + i] = h * 64 + (2 * i if i < 32 else 2 * (i - 32) + 1)
    return perm


def kernel(L0_attn_x, L0_attn_w_attn, L0_attn_b_attn, L0_attn_w_proj, L0_attn_b_proj,
           L0_ffn_gate_w, L0_ffn_w1, L0_ffn_w2, L0_ffn_w3,
           L1_attn_Wq, L1_attn_Wk, L1_attn_Wv, L1_attn_Wg, L1_attn_Wgk, L1_attn_Wo,
           L1_ffn_wq_gate, L1_ffn_s_gate, L1_ffn_wq_up, L1_ffn_s_up,
           L1_ffn_wq_down, L1_ffn_s_down):
    x = np.asarray(L0_attn_x, np.float32)
    w_attn = np.asarray(L0_attn_w_attn, np.float32)
    b_attn = np.asarray(L0_attn_b_attn, np.float32)
    w_proj = np.asarray(L0_attn_w_proj, np.float32)
    b_proj = np.asarray(L0_attn_b_proj, np.float32)
    gate_w = np.asarray(L0_ffn_gate_w, np.float32)
    w1 = np.asarray(L0_ffn_w1, np.float32)
    w2 = np.asarray(L0_ffn_w2, np.float32)
    w3 = np.asarray(L0_ffn_w3, np.float32)

    perm = _planar_perm()
    WqT = _cc(np.asarray(L1_attn_Wq, np.float32).T[:, perm])
    WkT = _cc(np.asarray(L1_attn_Wk, np.float32).T[:, perm])
    WvT = _cc(np.asarray(L1_attn_Wv, np.float32).T)
    WgT = _cc(np.asarray(L1_attn_Wg, np.float32).T)
    WgkT = _cc(np.asarray(L1_attn_Wgk, np.float32).T)
    WoT = _cc(np.asarray(L1_attn_Wo, np.float32).T)
    wqgT = _cc(np.asarray(L1_ffn_wq_gate, np.int32).T)
    wquT = _cc(np.asarray(L1_ffn_wq_up, np.int32).T)
    wqdT = _cc(np.asarray(L1_ffn_wq_down, np.int32).T)
    sg2d = _cc(np.asarray(L1_ffn_s_gate, np.float32).reshape(44, 128).T)
    su2d = _cc(np.asarray(L1_ffn_s_up, np.float32).reshape(44, 128).T)
    sd2d = _cc(np.asarray(L1_ffn_s_down, np.float32).reshape(16, 128).T)

    ident32 = np.eye(128, dtype=np.float32)
    identbf = ident32.astype(BF)
    maskneg = np.triu(np.full((128, 128), -1e30, np.float32), 1)
    maskbf = np.tril(np.ones((128, 128), np.float32)).astype(BF)

    cores = list(range(_NC))
    sl = [slice((c % 2) * 1024, (c % 2) * 1024 + 1024) for c in cores]
    bat = [c // 2 for c in cores]

    nc1 = build_l0qkv()
    im = [{"x_f": _cc(x[bat[c], sl[c]].T), "w_attn": w_attn, "b_attn": b_attn}
          for c in cores]
    r1 = run_bass_kernel_spmd(nc1, im, core_ids=cores)
    qkv = [r1.results[c]["qkv"] for c in cores]

    nc2 = build_l0attn()
    im = []
    for c in cores:
        pair = np.concatenate([qkv[2 * bat[c]], qkv[2 * bat[c] + 1]], 0)
        hb = 8 * (c % 2)
        q_f = np.stack([_cc(pair[:, (hb + h) * 128:(hb + h + 1) * 128].T) for h in range(8)])
        k_f = np.stack([_cc(pair[:, 2048 + (hb + h) * 128:2048 + (hb + h + 1) * 128].T)
                        for h in range(8)])
        v_t = np.stack([_cc(pair[:, 4096 + (hb + h) * 128:4096 + (hb + h + 1) * 128])
                        for h in range(8)])
        im.append({"q_f": q_f, "k_f": k_f, "v_t": v_t,
                   "maskneg": maskneg, "ident_bf": identbf})
    r2 = run_bass_kernel_spmd(nc2, im, core_ids=cores)
    o0 = [r2.results[c]["o_f"] for c in cores]

    nc3 = build_mid()
    im = []
    for c in cores:
        opair = np.concatenate([o0[2 * bat[c]], o0[2 * bat[c] + 1]], 0)
        im.append({"x_t": _cc(x[bat[c], sl[c]]), "o_fb": _cc(opair[:, sl[c]]),
                   "w_proj": w_proj, "b_proj": b_proj, "gate_w": gate_w,
                   "w1": w1, "w2": w2, "w3": w3, "id32": ident32, "idbf": identbf})
    r3 = run_bass_kernel_spmd(nc3, im, core_ids=cores)
    x2 = [r3.results[c]["x2"] for c in cores]

    nc4 = build_l1qkv()
    inv = 1.0 / 10000.0 ** (np.arange(0, 64, 2) / 64.0)
    im = []
    for c in cores:
        pos = np.arange((c % 2) * 1024, (c % 2) * 1024 + 1024, dtype=np.float64)
        fr = pos[:, None] * inv[None, :]
        cosm = np.concatenate([np.cos(fr), np.cos(fr)], 1).astype(np.float32)
        sinm = np.concatenate([np.sin(fr), np.sin(fr)], 1).astype(np.float32)
        im.append({"x2_f": _cc(x2[c].T), "WqT": WqT, "WkT": WkT, "WvT": WvT,
                   "WgkT": WgkT, "rope_cos": cosm, "rope_sin": sinm})
    r4 = run_bass_kernel_spmd(nc4, im, core_ids=cores)
    qkv1 = [r4.results[c]["qkv1"] for c in cores]
    gkv = [r4.results[c]["gk"] for c in cores]

    nc5 = build_l1attn()
    im = []
    for c in cores:
        pair = np.concatenate([qkv1[2 * bat[c]], qkv1[2 * bat[c] + 1]], 0)
        gpair = np.concatenate([gkv[2 * bat[c]], gkv[2 * bat[c] + 1]], 0)
        hb = 16 * (c % 2)
        q_f = np.stack([_cc(pair[:, (hb + p) * 64:(hb + p + 1) * 64].T) for p in range(16)])
        k_f = np.stack([_cc(pair[:, 2048 + (hb + p) * 64:2048 + (hb + p + 1) * 64].T)
                        for p in range(16)])
        v_t = np.stack([_cc(pair[:, 4096 + (hb + p) * 64:4096 + (hb + p + 1) * 64])
                        for p in range(16)])
        gk_f = _cc(gpair[:, hb:hb + 16].T)
        im.append({"q_f": q_f, "k_f": k_f, "v_t": v_t, "gk_f": gk_f,
                   "maskbf": maskbf, "idbf": identbf, "id32": ident32})
    r5 = run_bass_kernel_spmd(nc5, im, core_ids=cores)
    o1 = [r5.results[c]["o_f"] for c in cores]

    nc6 = build_post()
    im = []
    for c in cores:
        opair = np.concatenate([o1[2 * bat[c]], o1[2 * bat[c] + 1]], 0)
        im.append({"x2_t": x2[c], "x2_f": _cc(x2[c].T), "o_fb": _cc(opair[:, sl[c]]),
                   "WgT": WgT, "WoT": WoT, "wqgT": wqgT, "wquT": wquT, "wqdT": wqdT,
                   "sg2d": sg2d, "su2d": su2d, "sd2d": sd2d,
                   "idbf": identbf, "id32": ident32})
    r6 = run_bass_kernel_spmd(nc6, im, core_ids=cores)

    out = np.empty((4, 2048, 2048), np.float32)
    for c in cores:
        out[bat[c], sl[c]] = r6.results[c]["y"]
    return out


# version 04
# speedup vs baseline: 1.0618x; 1.0618x over previous
import contextlib
from types import SimpleNamespace as NS
import numpy as np
import concourse.bass as bass
import concourse.mybir as mybir

F32 = mybir.dt.float32
BF16 = mybir.dt.bfloat16
I32 = mybir.dt.int32
AF = mybir.ActivationFunctionType
ALU = mybir.AluOpType
AX = mybir.AxisListType

P = 128
D = 2048
NCH = 16
TOK = 1024
NTT = 8
T = 2048
EPS_LN = 1e-5
EPS_RMS = 1e-6
PIECES = [(0, 512), (512, 256)]


def chunks_of(qt):
    klen = (qt + 1) * P
    out = []
    kc = 0
    while kc * 512 < klen:
        out.append((kc, min(512, klen - kc * 512)))
        kc += 1
    return out


def attn_sched(npairs):
    sched = []
    c_smm = c_mask = c_exp = c_zsc = c_kt = c_oev = 0
    for p in range(npairs):
        for g in range(4):
            rec = {"smm0": c_smm, "mask0": c_mask, "exp0": c_exp,
                   "zsc0": c_zsc, "kt0": c_kt, "oev0": c_oev}
            for i in range(4):
                qt = 4 * g + i
                n = len(chunks_of(qt))
                c_smm += n
                c_exp += n
                c_mask += 1
                c_zsc += 1
            c_kt += 4 * g + 4
            c_oev += 1
            rec.update({"smm1": c_smm, "mask1": c_mask, "exp1": c_exp,
                        "zsc1": c_zsc, "kt1": c_kt, "oev1": c_oev})
            sched.append(rec)
    return sched


def build_fused():
    nc = bass.Bass("TRN2", target_bir_lowering=False)

    x_f_in = nc.dram_tensor("x_f", [D, TOK], F32, kind="ExternalInput")
    x_t_in = nc.dram_tensor("x_t", [TOK, D], F32, kind="ExternalInput")
    coreinfo = nc.dram_tensor("coreinfo", [1, 4], I32, kind="ExternalInput")
    w_attn = nc.dram_tensor("w_attn", [D, 3 * D], F32, kind="ExternalInput")
    b_attn = nc.dram_tensor("b_attn", [3 * D], F32, kind="ExternalInput")
    w_proj = nc.dram_tensor("w_proj", [D, D], F32, kind="ExternalInput")
    b_proj = nc.dram_tensor("b_proj", [D], F32, kind="ExternalInput")
    gate_w = nc.dram_tensor("gate_w", [D, 8], F32, kind="ExternalInput")
    w1 = nc.dram_tensor("w1", [8, D, 768], F32, kind="ExternalInput")
    w2 = nc.dram_tensor("w2", [8, 768, D], F32, kind="ExternalInput")
    w3 = nc.dram_tensor("w3", [8, D, 768], F32, kind="ExternalInput")
    WqT = nc.dram_tensor("WqT", [D, D], F32, kind="ExternalInput")
    WkT = nc.dram_tensor("WkT", [D, D], F32, kind="ExternalInput")
    WvT = nc.dram_tensor("WvT", [D, D], F32, kind="ExternalInput")
    WgkT = nc.dram_tensor("WgkT", [D, 32], F32, kind="ExternalInput")
    WgT = nc.dram_tensor("WgT", [D, D], F32, kind="ExternalInput")
    WoT = nc.dram_tensor("WoT", [D, D], F32, kind="ExternalInput")
    wqgT = nc.dram_tensor("wqgT", [D, 5632], I32, kind="ExternalInput")
    wquT = nc.dram_tensor("wquT", [D, 5632], I32, kind="ExternalInput")
    wqdT = nc.dram_tensor("wqdT", [5632, D], I32, kind="ExternalInput")
    sg2d = nc.dram_tensor("sg2d", [P, 44], F32, kind="ExternalInput")
    su2d = nc.dram_tensor("su2d", [P, 44], F32, kind="ExternalInput")
    sd2d = nc.dram_tensor("sd2d", [P, NCH], F32, kind="ExternalInput")
    rope_cos = nc.dram_tensor("rope_cos", [TOK, 64], F32, kind="ExternalInput")
    rope_sin = nc.dram_tensor("rope_sin", [TOK, 64], F32, kind="ExternalInput")
    id32_in = nc.dram_tensor("id32", [P, P], F32, kind="ExternalInput")
    idbf_in = nc.dram_tensor("idbf", [P, P], BF16, kind="ExternalInput")
    maskneg_in = nc.dram_tensor("maskneg", [P, P], F32, kind="ExternalInput")
    maskbf_in = nc.dram_tensor("maskbf", [P, P], BF16, kind="ExternalInput")
    y_out = nc.dram_tensor("y", [TOK, D], F32, kind="ExternalOutput")

    A0own = nc.dram_tensor("A0own", [TOK, 3 * D], BF16)
    A0o = nc.dram_tensor("A0o", [T, 3 * D], BF16)
    B0own = nc.dram_tensor("B0own", [TOK, D], BF16)
    B0o = nc.dram_tensor("B0o", [T, D], BF16)
    A1own = nc.dram_tensor("A1own", [TOK, 3 * D], BF16)
    A1o = nc.dram_tensor("A1o", [T, 3 * D], BF16)
    Agown = nc.dram_tensor("Agown", [TOK, 32], F32)
    Ago = nc.dram_tensor("Ago", [T, 32], F32)
    B1own = nc.dram_tensor("B1own", [TOK, D], BF16)
    B1o = nc.dram_tensor("B1o", [T, D], BF16)
    gdram = nc.dram_tensor("gdram", [32, T], F32)
    A0loc = nc.dram_tensor("A0loc", [T, 3 * TOK], BF16)
    A1loc = nc.dram_tensor("A1loc", [T, 3 * TOK], BF16)
    B1loc = nc.dram_tensor("B1loc", [T, TOK], BF16)
    gloc = nc.dram_tensor("gloc", [16, T], F32)
    RG = [[0, 1], [2, 3], [4, 5], [6, 7]]

    ctx = contextlib.ExitStack()
    E = ctx.enter_context

    stream = E(nc.sbuf_tensor("stream", [P, NTT * D], F32))
    x_f = E(nc.sbuf_tensor("x_fb", [P, NCH * TOK], BF16))
    isb = E(nc.sbuf_tensor("isb", [1, 4], I32))
    i32sb = E(nc.sbuf_tensor("i32sb", [P, P], F32))
    ibfsb = E(nc.sbuf_tensor("ibfsb", [P, P], BF16))
    mnegsb = E(nc.sbuf_tensor("mnegsb", [P, P], F32))
    mbfsb = E(nc.sbuf_tensor("mbfsb", [P, P], BF16))
    ones1 = E(nc.sbuf_tensor("ones1", [1, P], BF16))
    onec32 = E(nc.sbuf_tensor("onec32", [1, P], F32))
    nones = E(nc.sbuf_tensor("nones", [P, 1], BF16))
    epsln = E(nc.sbuf_tensor("epsln", [P, 1], F32))
    epsrm = E(nc.sbuf_tensor("epsrm", [P, 1], F32))
    shcol = E(nc.sbuf_tensor("shcol", [P, 1], F32))
    sq = E(nc.sbuf_tensor("sq", [P, NTT], F32))
    sums = E(nc.sbuf_tensor("sums", [P, NTT], F32))
    mu = E(nc.sbuf_tensor("mu", [P, NTT], F32))
    varv = E(nc.sbuf_tensor("varv", [P, NTT], F32))
    rsd = E(nc.sbuf_tensor("rsd", [P, NTT], F32))
    scr = E(nc.sbuf_tensor("scr", [P, D], F32))
    sgc_p = E(nc.sbuf_tensor("sgc_p", [P, 44], F32))
    suc_p = E(nc.sbuf_tensor("suc_p", [P, 44], F32))
    sdc_p = E(nc.sbuf_tensor("sdc_p", [P, NCH], F32))

    cc0 = E(nc.semaphore("cc0"))
    cc1 = E(nc.semaphore("cc1"))
    cc2 = E(nc.semaphore("cc2"))
    cc3 = E(nc.semaphore("cc3"))
    cst = E(nc.semaphore("cst"))

    sc = contextlib.ExitStack()
    p1 = NS()
    p1.wblk = [sc.enter_context(nc.sbuf_tensor(f"p1_wblk{i}", [P, NCH * 512], BF16)) for i in range(2)]
    p1.ev = [sc.enter_context(nc.sbuf_tensor(f"p1_ev{i}", [P, NTT * 512], BF16)) for i in range(2)]
    p1.brow = sc.enter_context(nc.sbuf_tensor("p1_brow", [1, 3 * D], BF16))
    p1.ps = [sc.enter_context(nc.psum_tensor(f"p1_ps{i}", [P, 512], F32)) for i in range(2)]
    _s = E(nc.semaphore("p1_ldw"))
    p1.ldw = [_s, _s]
    p1.ldx = E(nc.semaphore("p1_ldx"))
    p1.mm = E(nc.semaphore("p1_mm"))
    p1.evs = E(nc.semaphore("p1_evs"))
    _s = E(nc.semaphore("p1_so"))
    p1.so = [_s, _s]
    sc.close()
    sc = contextlib.ExitStack()
    p2 = NS()
    p2.qsb = [sc.enter_context(nc.sbuf_tensor(f"p2_qsb{i}", [P, T], BF16)) for i in range(2)]
    p2.ksb = [sc.enter_context(nc.sbuf_tensor(f"p2_ksb{i}", [P, T], BF16)) for i in range(2)]
    p2.vsb = [sc.enter_context(nc.sbuf_tensor(f"p2_vsb{i}", [P, 16 * P], BF16)) for i in range(2)]
    p2.psb = [sc.enter_context(nc.sbuf_tensor(f"p2_psb{i}", [P, 4 * T], BF16)) for i in range(2)]
    p2.ptsb = [sc.enter_context(nc.sbuf_tensor(f"p2_ptsb{i}", [P, 512], BF16)) for i in range(2)]
    p2.zp = sc.enter_context(nc.sbuf_tensor("p2_zp", [P, 16], F32))
    p2.zinv = sc.enter_context(nc.sbuf_tensor("p2_zinv", [P, 4], F32))
    p2.osb = [sc.enter_context(nc.sbuf_tensor(f"p2_osb{i}", [P, T], BF16)) for i in range(2)]
    p2.ps_s = [sc.enter_context(nc.psum_tensor(f"p2_ps_s{i}", [P, 512], F32)) for i in range(2)]
    p2.pt_ps = [sc.enter_context(nc.psum_tensor(f"p2_pt_ps{i}", [P, 1024], BF16)) for i in range(2)]
    p2.o_ps = sc.enter_context(nc.psum_tensor("p2_o_ps", [P, 512], F32))
    for nm in ["qld", "ssem", "msem", "esem", "zsem", "tsem", "ptsem", "avsem",
               "osem", "vself"]:
        setattr(p2, nm, E(nc.semaphore(f"p2_{nm}")))
    _s = E(nc.semaphore("p2_sto"))
    p2.sto = [_s, _s]
    p2.sched = attn_sched(8)
    sc.close()
    sc = contextlib.ExitStack()
    p3 = NS()
    p3.gwsb = sc.enter_context(nc.sbuf_tensor("p3_gwsb", [P, NCH * 8], BF16))
    p3.bprow = sc.enter_context(nc.sbuf_tensor("p3_bprow", [1, D], BF16))
    p3.gexp = sc.enter_context(nc.sbuf_tensor("p3_gexp", [P, NTT * 8], F32))
    p3.comb = sc.enter_context(nc.sbuf_tensor("p3_comb", [P, NTT * 8], F32))
    p3.e8a = sc.enter_context(nc.sbuf_tensor("p3_e8a", [P, 8], F32))
    p3.t2a = sc.enter_context(nc.sbuf_tensor("p3_t2a", [P, 1], F32))
    p3.t2b = sc.enter_context(nc.sbuf_tensor("p3_t2b", [P, 1], F32))
    scb = contextlib.ExitStack()
    p3.ofb = scb.enter_context(nc.sbuf_tensor("p3_ofb", [P, NCH * TOK], BF16))
    p3.wpb = [scb.enter_context(nc.sbuf_tensor(f"p3_wpb{i}", [P, NCH * 512], BF16)) for i in range(2)]
    scb.close()
    p3.w13b = [sc.enter_context(nc.sbuf_tensor(f"p3_w13b{i}", [P, NCH * 512], BF16)) for i in range(4)]
    p3.w2b = [sc.enter_context(nc.sbuf_tensor(f"p3_w2b{i}", [P, 6 * 512], BF16)) for i in range(2)]
    p3.gs = [sc.enter_context(nc.sbuf_tensor(f"p3_gs{i}", [P, 512], F32)) for i in range(2)]
    p3.a2t = [sc.enter_context(nc.sbuf_tensor(f"p3_a2t{i}", [P, 512], BF16)) for i in range(2)]
    p3.a2f = sc.enter_context(nc.sbuf_tensor("p3_a2f", [P, NTT * 6 * P], BF16))
    p3.ps = [sc.enter_context(nc.psum_tensor(f"p3_ps{i}", [P, 512], F32)) for i in range(2)]
    p3.pg = [sc.enter_context(nc.psum_tensor(f"p3_pg{i}", [P, 512], F32)) for i in range(2)]
    p3.pu = [sc.enter_context(nc.psum_tensor(f"p3_pu{i}", [P, 512], F32)) for i in range(2)]
    p3.ptr = [sc.enter_context(nc.psum_tensor(f"p3_ptr{i}", [P, 1024], BF16)) for i in range(2)]
    p3.pxf = p3.pu
    for nm in ["ldx", "ldc", "mm", "evs", "lnv", "lns", "trs", "xfs", "gsem",
               "vch", "sil", "a2s", "a2e", "dmm", "dev"]:
        setattr(p3, nm, E(nc.semaphore(f"p3_{nm}")))
    _s = E(nc.semaphore("p3_ldw")); p3.ldw = [_s, _s]
    _s = E(nc.semaphore("p3_lde")); p3.lde = [_s, _s]
    _s = E(nc.semaphore("p3_ld2")); p3.ld2 = [_s, _s]
    sc.close()
    sc = contextlib.ExitStack()
    p4 = NS()
    p4.wblk = [sc.enter_context(nc.sbuf_tensor(f"p4_wblk{i}", [P, NCH * 512], BF16)) for i in range(2)]
    p4.wgk = sc.enter_context(nc.sbuf_tensor("p4_wgk", [P, NCH * 32], BF16))
    p4.stage = sc.enter_context(nc.sbuf_tensor("p4_stage", [P, NTT * D], BF16))
    p4.cosb = sc.enter_context(nc.sbuf_tensor("p4_cosb", [P, NTT * 64], BF16))
    p4.sinb = sc.enter_context(nc.sbuf_tensor("p4_sinb", [P, NTT * 64], BF16))
    p4.tmp = [sc.enter_context(nc.sbuf_tensor(f"p4_tmp{i}", [P, 1024], BF16)) for i in range(4)]
    p4.gks = sc.enter_context(nc.sbuf_tensor("p4_gks", [P, NTT * 32], F32))
    p4.ps = [sc.enter_context(nc.psum_tensor(f"p4_ps{i}", [P, 512], F32)) for i in range(2)]
    p4.pgk = sc.enter_context(nc.psum_tensor("p4_pgk", [P, 32], F32))
    for nm in ["ldc", "mm", "mgk", "esem", "gsig", "vch", "sto", "trs", "xfs"]:
        setattr(p4, nm, E(nc.semaphore(f"p4_{nm}")))
    _s = E(nc.semaphore("p4_ldw")); p4.ldw = [_s, _s]
    sc.close()
    sc = contextlib.ExitStack()
    p5 = NS()
    HD5 = 64
    p5.qsb = [sc.enter_context(nc.sbuf_tensor(f"p5_qsb{i}", [P, T], BF16)) for i in range(2)]
    p5.ksb = [sc.enter_context(nc.sbuf_tensor(f"p5_ksb{i}", [P, T], BF16)) for i in range(2)]
    p5.vsb = [sc.enter_context(nc.sbuf_tensor(f"p5_vsb{i}", [P, 16 * P], BF16)) for i in range(2)]
    p5.gsb = sc.enter_context(nc.sbuf_tensor("p5_gsb", [P, 16 * 32], F32))
    p5.gz = sc.enter_context(nc.sbuf_tensor("p5_gz", [32, 1], F32))
    p5.grow = [sc.enter_context(nc.sbuf_tensor(f"p5_grow{i}", [1, T], F32)) for i in range(2)]
    p5.gq16 = sc.enter_context(nc.sbuf_tensor("p5_gq16", [P, 16], F32))
    p5.gkbc = sc.enter_context(nc.sbuf_tensor("p5_gkbc", [P, T], F32))
    p5.esb = [sc.enter_context(nc.sbuf_tensor(f"p5_esb{i}", [P, 512], BF16)) for i in range(2)]
    p5.ptsb = [sc.enter_context(nc.sbuf_tensor(f"p5_ptsb{i}", [P, 512], BF16)) for i in range(2)]
    p5.osb = [sc.enter_context(nc.sbuf_tensor(f"p5_osb{i}", [HD5, T], BF16)) for i in range(2)]
    _scg = contextlib.ExitStack()
    p5.gf = _scg.enter_context(nc.sbuf_tensor("p5_gf", [32, T], F32))
    p5.gcum = _scg.enter_context(nc.sbuf_tensor("p5_gcum", [32, T], F32))
    _scg.close()
    p5.asb = [sc.enter_context(nc.sbuf_tensor(f"p5_asb{i}", [P, 4 * T], BF16)) for i in range(2)]
    p5.ps_s = [sc.enter_context(nc.psum_tensor(f"p5_ps_s{i}", [P, 512], F32)) for i in range(2)]
    p5.pt_ps = [sc.enter_context(nc.psum_tensor(f"p5_pt_ps{i}", [P, 1024], BF16)) for i in range(2)]
    p5.o_ps = sc.enter_context(nc.psum_tensor("p5_o_ps", [P, 512], F32))
    p5.pbc = sc.enter_context(nc.psum_tensor("p5_pbc", [P, 512], F32))
    for nm in ["qld", "gsc", "gtr", "gqt", "gqe", "bcs", "bce", "ssem", "esem",
               "asem", "tsem", "ptsem", "avsem", "osem", "gst", "vsf", "gqs"]:
        setattr(p5, nm, E(nc.semaphore(f"p5_{nm}")))
    _s = E(nc.semaphore("p5_gld")); p5.gld = [_s, _s]
    _s = E(nc.semaphore("p5_sto")); p5.sto = [_s, _s]
    p5.sched = attn_sched(16)
    sc.close()
    sc = contextlib.ExitStack()
    p6 = NS()
    NHC = 44
    p6.sgc = sgc_p
    p6.suc = suc_p
    p6.sdc = sdc_p
    p6.hvar = sc.enter_context(nc.sbuf_tensor("p6_hvar", [P, 32], F32))
    p6.rsrow = sc.enter_context(nc.sbuf_tensor("p6_rsrow", [1, TOK], BF16))
    p6.csrow = sc.enter_context(nc.sbuf_tensor("p6_csrow", [1, TOK], BF16))
    scb = contextlib.ExitStack()
    p6.ogf = scb.enter_context(nc.sbuf_tensor("p6_ogf", [P, NCH * TOK], BF16))
    p6.sg_sb = scb.enter_context(nc.sbuf_tensor("p6_sg_sb", [P, NTT * D], BF16))
    p6.otb = [scb.enter_context(nc.sbuf_tensor(f"p6_otb{i}", [P, D], BF16)) for i in range(2)]
    p6.wb = [scb.enter_context(nc.sbuf_tensor(f"p6_wb{i}", [P, NCH * 256], BF16)) for i in range(2)]
    scb.close()
    p6.a2sb = sc.enter_context(nc.sbuf_tensor("p6_a2sb", [P, NHC * 512], BF16))
    p6.wqg_b = [sc.enter_context(nc.sbuf_tensor(f"p6_wqg_b{i}", [P, NCH * P], BF16)) for i in range(2)]
    p6.wqu_b = [sc.enter_context(nc.sbuf_tensor(f"p6_wqu_b{i}", [P, NCH * P], BF16)) for i in range(2)]
    p6.wqd_b = [sc.enter_context(nc.sbuf_tensor(f"p6_wqd_b{i}", [P, NHC * P], BF16)) for i in range(2)]
    p6.sgh = [sc.enter_context(nc.sbuf_tensor(f"p6_sgh{i}", [P, 512], BF16)) for i in range(2)]
    p6.yfb = [sc.enter_context(nc.sbuf_tensor(f"p6_yfb{i}", [P, 512], BF16)) for i in range(2)]
    p6.ps = [sc.enter_context(nc.psum_tensor(f"p6_ps{i}", [P, 512], F32)) for i in range(2)]
    p6.pu = [sc.enter_context(nc.psum_tensor(f"p6_pu{i}", [P, 512], F32)) for i in range(2)]
    p6.ptrb = [sc.enter_context(nc.psum_tensor(f"p6_ptr{i}", [P, 1024], BF16)) for i in range(2)]
    p6.prow = sc.enter_context(nc.psum_tensor("p6_prow", [P, 512], F32))
    for nm in ["ldm", "mm", "evs", "lnv", "lns", "onr", "sil", "trs", "xfs",
               "rsm", "rse", "gup", "a2s", "dmm", "dse", "dtr", "dev", "sto"]:
        setattr(p6, nm, E(nc.semaphore(f"p6_{nm}")))
    _s = E(nc.semaphore("p6_ldo")); p6.ldo = [_s, _s]
    _s = E(nc.semaphore("p6_ldw")); p6.ldw = [_s, _s]
    _s = E(nc.semaphore("p6_ldq")); p6.ldq = [_s, _s]
    _s = E(nc.semaphore("p6_ldd")); p6.ldd = [_s, _s]
    sc.close()

    NGP6 = 64
    NWO6 = 64

    def mm_gu3(gidx):
        return 40 + gidx + 1

    NHC = 44
    with nc.Block() as block:

        @block.sync
        def _(sy):
            sy.dma_start(out=isb[:, :], in_=coreinfo[:, :]).then_inc(cst, 16)
            sy.dma_start(out=i32sb[:, :], in_=id32_in[:, :]).then_inc(cst, 16)
            sy.dma_start(out=ibfsb[:, :], in_=idbf_in[:, :]).then_inc(cst, 16)
            sy.dma_start(out=mnegsb[:, :], in_=maskneg_in[:, :]).then_inc(cst, 16)
            sy.dma_start(out=mbfsb[:, :], in_=maskbf_in[:, :]).then_inc(cst, 16)
            sy.dma_start(out=stream[:, :].rearrange("p (tt d) -> p tt d", tt=NTT),
                         in_=x_t_in.ap().rearrange("(tt p) d -> p tt d", p=P)
                         ).then_inc(p3.ldx, 16)
            sy.dma_start(out=p6.sgc[:, :], in_=sg2d[:, :]).then_inc(p6.ldm, 16)
            sy.dma_start(out=p6.suc[:, :], in_=su2d[:, :]).then_inc(p6.ldm, 16)
            sy.dma_start(out=p6.sdc[:, :], in_=sd2d[:, :]).then_inc(p6.ldm, 16)
            sy.wait_ge(cst, 81)
            rhb0 = sy.alloc_register("rhb0")
            rhb1 = sy.alloc_register("rhb1")
            rtok = sy.alloc_register("rtok")
            sy.load(rhb0, isb[0:1, 0:1])
            sy.load(rhb1, isb[0:1, 1:2])
            sy.load(rtok, isb[0:1, 2:3])
            vhb0 = sy.snap(rhb0, min_val=0, max_val=1024)
            vhb1 = sy.snap(rhb1, min_val=0, max_val=1024)
            vtok = sy.snap(rtok, min_val=0, max_val=1024)

            A0r = A0own.ap().rearrange("(tt p) o -> p tt o", p=P)
            for ob in range(12):
                sy.wait_ge(p1.evs, 8 * (ob + 1))
                if ob >= 1:
                    sy.wait_ge(p1.so[0], 16 * ob)
                sy.dma_start(out=A0r[:, :, ob * 512:(ob + 1) * 512],
                             in_=p1.ev[ob % 2][:, :].rearrange("p (tt o) -> p tt o", tt=NTT)
                             ).then_inc(p1.so[0], 16)

            sy.wait_ge(cc0, 1)
            sy.dma_start(out=A0loc[:, 0:TOK],
                         in_=A0o[:, bass.ds(vhb0, TOK)]).then_inc(cc0, 16)
            sy.dma_start(out=A0loc[:, TOK:2 * TOK],
                         in_=A0o[:, bass.ds(vhb0 + 2048, TOK)]).then_inc(cc0, 16)
            sy.dma_start(out=A0loc[:, 2 * TOK:3 * TOK],
                         in_=A0o[:, bass.ds(vhb0 + 4096, TOK)]).then_inc(cc0, 16)
            for h in range(8):
                if h >= 1:
                    sy.wait_ge(p2.qld, 48 * h)
                if h >= 2:
                    sy.wait_ge(p2.avsem, p2.sched[(h - 2) * 4 + 3]["kt1"])
                else:
                    sy.wait_ge(cc0, 49)
                sy.dma_start(out=p2.qsb[h % 2][:, :],
                             in_=A0loc[:, h * P:(h + 1) * P],
                             transpose=True).then_inc(p2.qld, 16)
                sy.dma_start(out=p2.ksb[h % 2][:, :],
                             in_=A0loc[:, TOK + h * P:TOK + (h + 1) * P],
                             transpose=True).then_inc(p2.qld, 16)
                sy.dma_start(out=p2.vsb[h % 2][:, :].rearrange("p (kt d) -> p kt d", kt=16),
                             in_=A0loc.ap().rearrange("(kt p) o -> p kt o", p=P)
                             [:, :, 2 * TOK + h * P:2 * TOK + (h + 1) * P]
                             ).then_inc(p2.qld, 16)

            sy.wait_ge(cc1, 1)
            sy.dma_start(out=p3.ofb[:, :].rearrange("p (c t) -> p c t", c=NCH),
                         in_=B0o.ap().rearrange("(c p) t -> p c t", p=P)
                         [:, :, bass.ds(vtok, TOK)]).then_inc(p3.ldx, 16)

            A1r = A1own.ap().rearrange("(tt p) o -> p tt o", p=P)
            for mi in range(3):
                if mi < 2:
                    sy.wait_ge(p4.vch, 48 * (mi + 1))
                else:
                    sy.wait_ge(p4.esem, 96)
                if mi >= 1:
                    sy.wait_ge(p4.sto, 16 * mi)
                sy.dma_start(out=A1r[:, :, mi * D:(mi + 1) * D],
                             in_=p4.stage[:, :].rearrange("p (tt o) -> p tt o", tt=NTT)
                             ).then_inc(p4.sto, 16)
            sy.wait_ge(p4.gsig, 16)
            sy.dma_start(out=Agown.ap().rearrange("(tt p) e -> p tt e", p=P),
                         in_=p4.gks[:, :].rearrange("p (tt e) -> p tt e", tt=NTT)
                         ).then_inc(p4.sto, 16)

            sy.wait_ge(cc2, 2)
            sy.dma_start(out=A1loc[:, 0:TOK],
                         in_=A1o[:, bass.ds(vhb1, TOK)]).then_inc(cc2, 16)
            sy.dma_start(out=A1loc[:, TOK:2 * TOK],
                         in_=A1o[:, bass.ds(vhb1 + 2048, TOK)]).then_inc(cc2, 16)
            sy.dma_start(out=A1loc[:, 2 * TOK:3 * TOK],
                         in_=A1o[:, bass.ds(vhb1 + 4096, TOK)]).then_inc(cc2, 16)
            for prr in range(8):
                if prr >= 1:
                    sy.wait_ge(p5.qld, 48 * prr)
                if prr >= 2:
                    sy.wait_ge(p5.avsem, p5.sched[(2 * (prr - 2) + 1) * 4 + 3]["kt1"])
                else:
                    sy.wait_ge(cc2, 50)
                sy.dma_start(out=p5.qsb[prr % 2][:, :],
                             in_=A1loc[:, prr * P:(prr + 1) * P],
                             transpose=True).then_inc(p5.qld, 16)
                sy.dma_start(out=p5.ksb[prr % 2][:, :],
                             in_=A1loc[:, TOK + prr * P:TOK + (prr + 1) * P],
                             transpose=True).then_inc(p5.qld, 16)
                sy.dma_start(out=p5.vsb[prr % 2][:, :].rearrange("p (kt d) -> p kt d", kt=16),
                             in_=A1loc.ap().rearrange("(kt p) o -> p kt o", p=P)
                             [:, :, 2 * TOK + prr * P:2 * TOK + (prr + 1) * P]
                             ).then_inc(p5.qld, 16)

            sy.wait_ge(cc3, 1)
            sy.dma_start(out=B1loc[:, :],
                         in_=B1o[:, bass.ds(vtok, TOK)]).then_inc(cc3, 16)
            for tt in range(NTT):
                if tt >= 1:
                    sy.wait_ge(p6.ldo[0], 16 * tt)
                if tt >= 2:
                    sy.wait_ge(p6.onr, 5 * (tt - 1))
                else:
                    sy.wait_ge(cc3, 17)
                sy.dma_start(out=p6.otb[tt % 2][:, :],
                             in_=B1loc[:, tt * P:(tt + 1) * P],
                             transpose=True).then_inc(p6.ldo[0], 16)
            for tt in range(NTT):
                sy.wait_ge(p6.lnv, 56 + 7 * (tt + 1))
                sy.dma_start(out=y_out.ap().rearrange("(tt p) d -> p tt d", p=P)[:, tt, :],
                             in_=stream[:, tt * D:(tt + 1) * D]).then_inc(p6.sto, 16)
            sy.wait_ge(p6.sto, 16 * NTT)

        @block.gpsimd
        def _(g):
            g.memset(ones1[:, :], 1.0)
            g.memset(onec32[:, :], 1.0)
            g.memset(nones[:, :], -128.0)
            g.memset(epsln[:, :], EPS_LN)
            g.memset(epsrm[:, :], EPS_RMS)
            g.memset(p5.gz[:, :], 0.0)
            g.memset(shcol[:, :], -16.0).then_inc(cst, 1)
            g.dma_start(out=x_f[:, :].rearrange("p (c t) -> p c t", c=NCH),
                        in_=x_f_in.ap().rearrange("(c p) t -> p c t", p=P)
                        ).then_inc(p1.ldx, 16)
            g.dma_start(out=p1.brow[:, :], in_=b_attn.ap().unsqueeze(0)).then_inc(p1.ldx, 16)
            for ob in range(12):
                if ob >= 1:
                    g.wait_ge(p1.ldw[0], 16 * ob)
                if ob >= 2:
                    g.wait_ge(p1.mm, 8 * (ob - 1))
                g.dma_start(out=p1.wblk[ob % 2][:, :].rearrange("p (c o) -> p c o", c=NCH),
                            in_=w_attn.ap().rearrange("(c p) o -> p c o", p=P)
                            [:, :, ob * 512:(ob + 1) * 512]).then_inc(p1.ldw[0], 16)
            g.wait_ge(p1.so[0], 16 * 12)
            g.collective_compute("AllGather", ALU.bypass, replica_groups=RG,
                                 ins=[A0own.ap().opt()], outs=[A0o.ap().opt()]
                                 ).then_inc(cc0, 1)
            for h in range(8):
                g.wait_ge(p2.osem, 4 * (h + 1))
                if h >= 1:
                    g.wait_ge(p2.sto[0], 16 * h)
                g.dma_start(out=B0own[h * P:(h + 1) * P, :],
                            in_=p2.osb[h % 2][:, :]).then_inc(p2.sto[0], 16)
            g.wait_ge(p2.sto[0], 16 * 8)
            g.collective_compute("AllGather", ALU.bypass, replica_groups=RG,
                                 ins=[B0own.ap().opt()], outs=[B0o.ap().opt()]
                                 ).then_inc(cc1, 1)
            g.wait_ge(cc1, 1)
            g.dma_start(out=p3.bprow[:, :], in_=b_proj.ap().unsqueeze(0)).then_inc(p3.ldc, 16)
            g.dma_start(out=p3.gwsb[:, :].rearrange("p (c e) -> p c e", c=NCH),
                        in_=gate_w.ap().rearrange("(c p) e -> p c e", p=P)).then_inc(p3.ldc, 16)
            for ob in range(4):
                if ob >= 1:
                    g.wait_ge(p3.ldw[0], 16 * ob)
                if ob >= 2:
                    g.wait_ge(p3.mm, 8 * (ob - 1))
                g.dma_start(out=p3.wpb[ob % 2][:, :].rearrange("p (c o) -> p c o", c=NCH),
                            in_=w_proj.ap().rearrange("(c p) o -> p c o", p=P)
                            [:, :, ob * 512:(ob + 1) * 512]).then_inc(p3.ldw[0], 16)
            g.wait_ge(p3.mm, 32)
            for e in range(8):
                for piece, (foff, fw) in enumerate(PIECES):
                    pig = e * 2 + piece
                    par = pig % 2
                    if pig >= 1:
                        g.wait_ge(p3.lde[0], 32 * pig)
                    for (mat, slot) in [(w1, 2 * par), (w3, 2 * par + 1)]:
                        if pig >= 2:
                            pe_, pp_ = divmod(pig - 2, 2)
                            g.wait_ge(p3.mm, mm_gu3(pe_ * 16 + pp_ * 8 + 7))
                        g.dma_start(
                            out=p3.w13b[slot][:, 0:NCH * fw].rearrange(
                                "p (c f) -> p c f", c=NCH),
                            in_=mat[e, :, foff:foff + fw].rearrange("(c p) f -> p c f", p=P)
                        ).then_inc(p3.lde[0], 16)
                for db in range(4):
                    j = e * 4 + db
                    if j >= 1:
                        g.wait_ge(p3.ld2[0], 16 * j)
                    if j >= 2:
                        g.wait_ge(p3.dmm, 48 * (j - 1))
                    g.dma_start(
                        out=p3.w2b[j % 2][:, :].rearrange("p (c d) -> p c d", c=6),
                        in_=w2[e, :, db * 512:(db + 1) * 512].rearrange("(c p) d -> p c d", p=P)
                    ).then_inc(p3.ld2[0], 16)
            g.wait_ge(p3.lnv, 112)
            g.dma_start(out=p4.wgk[:, :].rearrange("p (c e) -> p c e", c=NCH),
                        in_=WgkT.ap().rearrange("(c p) e -> p c e", p=P)).then_inc(p4.ldc, 16)
            g.dma_start(out=p4.cosb[:, :].rearrange("p (tt f) -> p tt f", tt=NTT),
                        in_=rope_cos.ap().rearrange("(tt p) f -> p tt f", p=P)
                        ).then_inc(p4.ldc, 16)
            g.dma_start(out=p4.sinb[:, :].rearrange("p (tt f) -> p tt f", tt=NTT),
                        in_=rope_sin.ap().rearrange("(tt p) f -> p tt f", p=P)
                        ).then_inc(p4.ldc, 16)
            bi = 0
            for mi, mat in enumerate([WqT, WkT, WvT]):
                for ob in range(4):
                    if bi >= 1:
                        g.wait_ge(p4.ldw[0], 16 * bi)
                    if bi >= 2:
                        g.wait_ge(p4.mm, 8 * (bi - 1))
                    g.dma_start(out=p4.wblk[bi % 2][:, :].rearrange("p (c o) -> p c o", c=NCH),
                                in_=mat.ap().rearrange("(c p) o -> p c o", p=P)
                                [:, :, ob * 512:(ob + 1) * 512]).then_inc(p4.ldw[0], 16)
                    bi += 1
            g.wait_ge(p4.sto, 64)
            g.collective_compute("AllGather", ALU.bypass, replica_groups=RG,
                                 ins=[A1own.ap().opt()], outs=[A1o.ap().opt()]
                                 ).then_inc(cc2, 1)
            g.collective_compute("AllGather", ALU.bypass, replica_groups=RG,
                                 ins=[Agown.ap().opt()], outs=[Ago.ap().opt()]
                                 ).then_inc(cc2, 1)
            g.wait_ge(cc2, 2)
            g.dma_start(out=p5.gsb[:, :].rearrange("p (tt h) -> p tt h", tt=16),
                        in_=Ago.ap().rearrange("(tt p) h -> p tt h", p=P)
                        ).then_inc(p5.gqs, 16)
            g.wait_ge(p5.gsc, 1)
            g.dma_start(out=gdram[:, :], in_=p5.gcum[:, :]).then_inc(p5.gst, 16)
            g.wait_ge(p5.gst, 16)
            rp5 = g.alloc_register("rp5")
            g.load(rp5, isb[0:1, 3:4])
            offv = g.snap(rp5, min_val=0, max_val=16)
            g.dma_start(out=gloc[:, :],
                        in_=gdram[bass.ds(offv, 16), :]).then_inc(p5.gst, 16)
            g.wait_ge(p5.gst, 32)

            def p5_store(pr):
                g.wait_ge(p5.osem, 4 * (pr + 1))
                if pr >= 1:
                    g.wait_ge(p5.sto[0], 16 * pr)
                g.dma_start(out=B1own[pr * 64:(pr + 1) * 64, :],
                            in_=p5.osb[pr % 2][:, :]).then_inc(p5.sto[0], 16)

            for pr in range(16):
                if pr >= 1:
                    g.wait_ge(p5.gld[0], 16 * pr)
                if pr >= 2:
                    g.wait_ge(p5.bcs, 4 * (pr - 1))
                g.dma_start(out=p5.grow[pr % 2][:, :],
                            in_=gloc[pr:pr + 1, :]).then_inc(p5.gld[0], 16)
                if pr >= 2:
                    p5_store(pr - 2)
            p5_store(14)
            p5_store(15)
            g.wait_ge(p5.sto[0], 16 * 16)
            g.collective_compute("AllGather", ALU.bypass, replica_groups=RG,
                                 ins=[B1own.ap().opt()], outs=[B1o.ap().opt()]
                                 ).then_inc(cc3, 1)
            g.wait_ge(cc3, 1)
            bi = 0
            for mat in (WgT, WoT):
                for ob in range(8):
                    if bi >= 1:
                        g.wait_ge(p6.ldw[0], 16 * bi)
                    if bi >= 2:
                        g.wait_ge(p6.mm, 8 * (bi - 1))
                    g.dma_start(out=p6.wb[bi % 2][:, :].rearrange("p (c o) -> p c o", c=NCH),
                                in_=mat.ap().rearrange("(c p) o -> p c o", p=P)
                                [:, :, ob * 256:(ob + 1) * 256]).then_inc(p6.ldw[0], 16)
                    bi += 1
            qi = 0
            for th in range(2):
                for hc in range(NHC):
                    if qi >= 1:
                        g.wait_ge(p6.ldq[0], 32 * qi)
                    if qi >= 2:
                        g.wait_ge(p6.gup, 2 * (qi - 1))
                    else:
                        g.wait_ge(p6.mm, NGP6 + NWO6)
                    g.dma_start(out=p6.wqg_b[qi % 2][:, :].rearrange("p (c h) -> p c h", c=NCH),
                                in_=wqgT.ap().rearrange("(c p) h -> p c h", p=P)
                                [:, :, hc * P:(hc + 1) * P]).then_inc(p6.ldq[0], 16)
                    g.dma_start(out=p6.wqu_b[qi % 2][:, :].rearrange("p (c h) -> p c h", c=NCH),
                                in_=wquT.ap().rearrange("(c p) h -> p c h", p=P)
                                [:, :, hc * P:(hc + 1) * P]).then_inc(p6.ldq[0], 16)
                    qi += 1
                for dc in range(NCH):
                    di = th * NCH + dc
                    if di >= 1:
                        g.wait_ge(p6.ldd[0], 16 * di)
                    if di >= 2:
                        g.wait_ge(p6.dse, di - 1)
                    else:
                        g.wait_ge(p6.mm, NGP6 + NWO6)
                    g.dma_start(out=p6.wqd_b[di % 2][:, :].rearrange("p (c d) -> p c d", c=NHC),
                                in_=wqdT.ap().rearrange("(c p) d -> p c d", p=P)
                                [:, :, dc * P:(dc + 1) * P]).then_inc(p6.ldd[0], 16)

        @block.tensor
        def _(t):
            t.wait_ge(cst, 81)
            t.wait_ge(p1.ldx, 32)
            gi = 0
            for ob in range(12):
                t.wait_ge(p1.ldw[0], 16 * (ob + 1))
                for tt in range(NTT):
                    pb = p1.ps[gi % 2]
                    if gi >= 2:
                        t.wait_ge(p1.evs, gi - 1)
                    for c in range(NCH):
                        t.matmul(pb[:, :],
                                 x_f[:, c * TOK + tt * P: c * TOK + (tt + 1) * P],
                                 p1.wblk[ob % 2][:, c * 512:(c + 1) * 512],
                                 start=(c == 0), stop=False)
                    t.matmul(pb[:, :], ones1[:, :],
                             p1.brow[0:1, ob * 512:(ob + 1) * 512],
                             start=False, stop=True).then_inc(p1.mm, 1)
                    gi += 1
            def p2_smms(idx):
                h, gq = divmod(idx, 4)
                rec = p2.sched[idx]
                smm = rec["smm0"]
                for i in range(4):
                    qt = 4 * gq + i
                    for (kc, cw) in chunks_of(qt):
                        if smm >= 2:
                            t.wait_ge(p2.esem, smm - 1)
                        t.matmul(p2.ps_s[smm % 2][:, 0:cw],
                                 p2.qsb[h % 2][:, qt * P:(qt + 1) * P],
                                 p2.ksb[h % 2][:, kc * 512:kc * 512 + cw],
                                 start=True, stop=True).then_inc(p2.ssem, 1)
                        smm += 1

            def p2_tav(idx):
                h, gq = divmod(idx, 4)
                rec = p2.sched[idx]
                t.wait_ge(p2.zsem, rec["zsc1"])
                ktc = rec["kt0"]
                for kt in range(4 * gq + 4):
                    i0 = max(0, kt - 4 * gq)
                    if ktc >= 2:
                        t.wait_ge(p2.ptsem, ktc - 1)
                    last = None
                    for i in range(i0, 4):
                        last = t.transpose(p2.pt_ps[ktc % 2][:, i * P:(i + 1) * P],
                                           p2.psb[idx % 2][:, i * T + kt * P: i * T + (kt + 1) * P],
                                           ibfsb[:, :])
                    last.then_inc(p2.tsem, 1)
                    t.wait_ge(p2.ptsem, ktc + 1)
                    if kt == 0 and rec["oev0"] > 0:
                        t.wait_ge(p2.osem, rec["oev0"])
                    t.matmul(p2.o_ps[:, i0 * P:512],
                             p2.vsb[h % 2][:, kt * P:(kt + 1) * P],
                             p2.ptsb[ktc % 2][:, i0 * P:512],
                             start=(kt == 0), stop=(kt == 4 * gq + 3)
                             ).then_inc(p2.avsem, 1)
                    ktc += 1

            for idx in range(32):
                if idx % 4 == 0:
                    t.wait_ge(p2.qld, 48 * (idx // 4 + 1))
                p2_smms(idx)
                if idx >= 1:
                    p2_tav(idx - 1)
            p2_tav(31)
            t.wait_ge(p3.ldx, 32)
            t.wait_ge(p3.ldc, 32)
            gi = 0
            for ob in range(4):
                t.wait_ge(p3.ldw[0], 16 * (ob + 1))
                for tt in range(NTT):
                    pb = p3.ps[gi % 2]
                    if gi >= 2:
                        t.wait_ge(p3.evs, gi - 1)
                    for c in range(NCH):
                        t.matmul(pb[:, :],
                                 p3.ofb[:, c * TOK + tt * P: c * TOK + (tt + 1) * P],
                                 p3.wpb[ob % 2][:, c * 512:(c + 1) * 512],
                                 start=(c == 0), stop=False)
                    t.matmul(pb[:, :], ones1[:, :],
                             p3.bprow[0:1, ob * 512:(ob + 1) * 512],
                             start=False, stop=True).then_inc(p3.mm, 1)
                    gi += 1
            ntr = 0
            for tt in range(NTT):
                t.wait_ge(p3.lnv, 7 * (tt + 1))
                for cg in range(4):
                    if ntr >= 2:
                        t.wait_ge(p3.xfs, ntr - 1)
                    last = None
                    for c4 in range(4):
                        c = cg * 4 + c4
                        last = t.transpose(p3.pxf[ntr % 2][:, c4 * P:(c4 + 1) * P],
                                           stream[:, tt * D + c * P: tt * D + (c + 1) * P],
                                           i32sb[:, :])
                    last.then_inc(p3.trs, 1)
                    ntr += 1
            for tt in range(NTT):
                t.wait_ge(p3.xfs, 4 * (tt + 1))
                if tt > 0:
                    t.wait_ge(p3.gsem, tt)
                for c in range(NCH):
                    t.matmul(p3.pg[0][:, 0:8],
                             x_f[:, c * TOK + tt * P: c * TOK + (tt + 1) * P],
                             p3.gwsb[:, c * 8:(c + 1) * 8],
                             start=(c == 0), stop=(c == NCH - 1)
                             ).then_inc(p3.mm, 1 if c == NCH - 1 else 0)
            t.wait_ge(p3.gsem, 8)
            t.wait_ge(p3.evs, 32)
            t.wait_ge(p3.xfs, 32)
            for e in range(8):
                pi = 0
                for piece, (foff, fw) in enumerate(PIECES):
                    pig = e * 2 + piece
                    par = pig % 2
                    t.wait_ge(p3.lde[0], 32 * (pig + 1))
                    for tt in range(NTT):
                        gidx = e * 16 + pi
                        if gidx >= 2:
                            t.wait_ge(p3.sil, gidx - 1)
                            t.wait_ge(p3.a2s, gidx - 1)
                        for c in range(NCH):
                            t.matmul(p3.pg[gidx % 2][:, 0:fw],
                                     x_f[:, c * TOK + tt * P: c * TOK + (tt + 1) * P],
                                     p3.w13b[2 * par][:, c * fw:(c + 1) * fw],
                                     start=(c == 0), stop=(c == NCH - 1))
                        inst = None
                        for c in range(NCH):
                            inst = t.matmul(p3.pu[gidx % 2][:, 0:fw],
                                            x_f[:, c * TOK + tt * P: c * TOK + (tt + 1) * P],
                                            p3.w13b[2 * par + 1][:, c * fw:(c + 1) * fw],
                                            start=(c == 0), stop=(c == NCH - 1))
                        inst.then_inc(p3.mm, 1)
                        t.wait_ge(p3.a2s, gidx + 1)
                        if gidx >= 2:
                            t.wait_ge(p3.a2e, gidx - 1)
                        nt = fw // P
                        last = None
                        for c4 in range(nt):
                            last = t.transpose(p3.ptr[gidx % 2][:, c4 * P:(c4 + 1) * P],
                                               p3.a2t[gidx % 2][:, c4 * P:(c4 + 1) * P],
                                               ibfsb[:, :])
                        last.then_inc(p3.trs, 1)
                        pi += 1
                for db in range(4):
                    j = e * 4 + db
                    t.wait_ge(p3.ld2[0], 16 * (j + 1))
                    for tt in range(NTT):
                        di = e * 32 + db * 8 + tt
                        if di >= 2:
                            t.wait_ge(p3.dev, di - 1)
                        t.wait_ge(p3.a2e, e * 16 + 8 + tt + 1)
                        inst = None
                        for fc in range(6):
                            inst = t.matmul(p3.ps[di % 2][:, :],
                                            p3.a2f[:, tt * 6 * P + fc * P: tt * 6 * P + (fc + 1) * P],
                                            p3.w2b[j % 2][:, fc * 512:(fc + 1) * 512],
                                            start=(fc == 0), stop=(fc == 5))
                        inst.then_inc(p3.dmm, 6)
            t.wait_ge(p3.lnv, 112)
            t.wait_ge(p3.mm, 168)
            ntr = 0
            for tt in range(NTT):
                for cg in range(4):
                    if ntr >= 2:
                        t.wait_ge(p4.xfs, ntr - 1)
                    last = None
                    for c4 in range(4):
                        c = cg * 4 + c4
                        last = t.transpose(p4.ps[ntr % 2][:, c4 * P:(c4 + 1) * P],
                                           stream[:, tt * D + c * P: tt * D + (c + 1) * P],
                                           i32sb[:, :])
                    last.then_inc(p4.trs, 1)
                    ntr += 1
            t.wait_ge(p4.ldc, 48)
            t.wait_ge(p4.xfs, 32)
            gi = 0
            for mi in range(3):
                for ob in range(4):
                    t.wait_ge(p4.ldw[0], 16 * (gi + 1))
                    for tt in range(NTT):
                        grp = gi * 8 + tt
                        pb = p4.ps[grp % 2]
                        if grp >= 2:
                            t.wait_ge(p4.esem, grp - 1)
                        inst = None
                        for c in range(NCH):
                            inst = t.matmul(pb[:, :],
                                            x_f[:, c * TOK + tt * P: c * TOK + (tt + 1) * P],
                                            p4.wblk[gi % 2][:, c * 512:(c + 1) * 512],
                                            start=(c == 0), stop=(c == NCH - 1))
                        inst.then_inc(p4.mm, 1)
                    gi += 1
            for tt in range(NTT):
                if tt > 0:
                    t.wait_ge(p4.gsig, tt)
                inst = None
                for c in range(NCH):
                    inst = t.matmul(p4.pgk[:, :],
                                    x_f[:, c * TOK + tt * P: c * TOK + (tt + 1) * P],
                                    p4.wgk[:, c * 32:(c + 1) * 32],
                                    start=(c == 0), stop=(c == NCH - 1))
                inst.then_inc(p4.mgk, 1)
            t.wait_ge(p4.esem, 96)
            t.wait_ge(p4.gsig, 16)
            t.wait_ge(p5.gqs, 16)
            for kt in range(16):
                if kt >= 1:
                    t.wait_ge(p5.gtr, 2 * kt)
                t.transpose(p5.pbc[0:32, 0:128], p5.gsb[:, kt * 32:(kt + 1) * 32],
                            i32sb[:, :]).then_inc(p5.gtr, 1)
            t.wait_ge(p5.gtr, 32)
            for pr in range(16):
                buf5 = (pr // 2) % 2
                sub5 = (pr % 2) * 64
                t.wait_ge(p5.qld, 48 * (pr // 2 + 1))
                t.wait_ge(p5.gld[0], 16 * (pr + 1))
                if pr >= 1:
                    t.wait_ge(p5.esem, p5.sched[(pr - 1) * 4 + 3]["exp1"])
                    t.wait_ge(p5.bce, 4 * pr)
                last = None
                for j in range(16):
                    last = t.transpose(p5.pbc[:, j:j + 1],
                                       p5.grow[pr % 2][0:1, j * P:(j + 1) * P],
                                       i32sb[0:1, 0:1])
                last.then_inc(p5.gqt, 1)
                t.wait_ge(p5.gqe, pr + 1)
                for j in range(4):
                    if pr > 0 or j > 0:
                        t.wait_ge(p5.bce, 4 * pr + j)
                    t.matmul(p5.pbc[:, :], onec32[:, :],
                             p5.grow[pr % 2][0:1, j * 512:(j + 1) * 512],
                             start=True, stop=True).then_inc(p5.bcs, 1)
                t.wait_ge(p5.bce, 4 * (pr + 1))

                def p5_smms(idx):
                    gq = idx % 4
                    rec = p5.sched[idx]
                    smm = rec["smm0"]
                    for i in range(4):
                        qt = 4 * gq + i
                        for (kc, cw) in chunks_of(qt):
                            if smm >= 2:
                                t.wait_ge(p5.asem, smm - 1)
                            t.matmul(p5.ps_s[smm % 2][:, 0:cw],
                                     p5.qsb[buf5][sub5:sub5 + 64, qt * P:(qt + 1) * P],
                                     p5.ksb[buf5][sub5:sub5 + 64, kc * 512:kc * 512 + cw],
                                     start=True, stop=True).then_inc(p5.ssem, 1)
                            smm += 1

                def p5_tav(idx):
                    gq = idx % 4
                    rec = p5.sched[idx]
                    t.wait_ge(p5.asem, rec["smm1"])
                    ktc = rec["kt0"]
                    for kt in range(4 * gq + 4):
                        i0 = max(0, kt - 4 * gq)
                        if ktc >= 2:
                            t.wait_ge(p5.ptsem, ktc - 1)
                        last = None
                        for i in range(i0, 4):
                            last = t.transpose(p5.pt_ps[ktc % 2][:, i * P:(i + 1) * P],
                                               p5.asb[idx % 2][:, i * T + kt * P: i * T + (kt + 1) * P],
                                               ibfsb[:, :])
                        last.then_inc(p5.tsem, 1)
                        t.wait_ge(p5.ptsem, ktc + 1)
                        if kt == 0 and rec["oev0"] > 0:
                            t.wait_ge(p5.osem, rec["oev0"])
                        t.matmul(p5.o_ps[0:64, i0 * P:512],
                                 p5.vsb[buf5][:, kt * P + sub5: kt * P + sub5 + 64],
                                 p5.ptsb[ktc % 2][:, i0 * P:512],
                                 start=(kt == 0), stop=(kt == 4 * gq + 3)
                                 ).then_inc(p5.avsem, 1)
                        ktc += 1

                for gq in range(4):
                    p5_smms(pr * 4 + gq)
                    if gq >= 1:
                        p5_tav(pr * 4 + gq - 1)
                p5_tav(pr * 4 + 3)
            t.wait_ge(p5.osem, 64)
            gi = 0
            for ob in range(8):
                t.wait_ge(p6.ldw[0], 16 * (ob + 1))
                for tt in range(NTT):
                    pb = p6.ps[gi % 2]
                    if gi >= 2:
                        t.wait_ge(p6.sil, gi - 1)
                    inst = None
                    for c in range(NCH):
                        inst = t.matmul(pb[:, 0:256],
                                        x_f[:, c * TOK + tt * P: c * TOK + (tt + 1) * P],
                                        p6.wb[ob % 2][:, c * 256:(c + 1) * 256],
                                        start=(c == 0), stop=(c == NCH - 1))
                    inst.then_inc(p6.mm, 1)
                    gi += 1
            ntr = 0
            for tt in range(NTT):
                t.wait_ge(p6.onr, 5 * tt + 5)
                for cg in range(4):
                    if ntr >= 2:
                        t.wait_ge(p6.xfs, ntr - 1)
                    last = None
                    for c4 in range(4):
                        c = cg * 4 + c4
                        last = t.transpose(p6.ptrb[ntr % 2][:, c4 * P:(c4 + 1) * P],
                                           p6.sg_sb[:, tt * D + c * P: tt * D + (c + 1) * P],
                                           ibfsb[:, :])
                    last.then_inc(p6.trs, 1)
                    ntr += 1
            gi = 0
            for ob in range(8):
                bi = 8 + ob
                t.wait_ge(p6.ldw[0], 16 * (bi + 1))
                for tt in range(NTT):
                    grp = NGP6 + gi
                    pb = p6.ps[grp % 2]
                    if gi >= 2:
                        t.wait_ge(p6.evs, gi - 1)
                    else:
                        t.wait_ge(p6.sil, NGP6)
                    t.wait_ge(p6.xfs, 4 * (tt + 1))
                    inst = None
                    for c in range(NCH):
                        inst = t.matmul(pb[:, 0:256],
                                        p6.ogf[:, c * TOK + tt * P: c * TOK + (tt + 1) * P],
                                        p6.wb[bi % 2][:, c * 256:(c + 1) * 256],
                                        start=(c == 0), stop=(c == NCH - 1))
                    inst.then_inc(p6.mm, 1)
                    gi += 1
            ntr = 32
            for tt in range(NTT):
                t.wait_ge(p6.lnv, 7 * (tt + 1))
                for cg in range(4):
                    if ntr >= 34:
                        t.wait_ge(p6.xfs, ntr - 1)
                    else:
                        t.wait_ge(p6.evs, NWO6)
                    last = None
                    for c4 in range(4):
                        c = cg * 4 + c4
                        last = t.transpose(p6.ps[ntr % 2][:, c4 * P:(c4 + 1) * P],
                                           stream[:, tt * D + c * P: tt * D + (c + 1) * P],
                                           i32sb[:, :])
                    last.then_inc(p6.trs, 1)
                    ntr += 1
            for th in range(2):
                tsl = slice(th * 512, (th + 1) * 512)
                t.wait_ge(p6.xfs, 64)
                if th == 1:
                    t.wait_ge(p6.rse, 2)
                inst = None
                for c in range(NCH):
                    inst = t.matmul(p6.prow[0:1, :], nones[:, :],
                                    x_f[:, c * TOK + th * 512: c * TOK + (th + 1) * 512],
                                    start=(c == 0), stop=(c == NCH - 1))
                inst.then_inc(p6.rsm, 1)
                t.wait_ge(p6.rse, 2 * th + 1)
                for hc in range(NHC):
                    qi = th * NHC + hc
                    t.wait_ge(p6.ldq[0], 32 * (qi + 1))
                    if qi >= 2:
                        t.wait_ge(p6.gup, 2 * (qi - 1))
                    for c in range(NCH):
                        t.matmul(p6.ps[qi % 2][:, :],
                                 p6.wqg_b[qi % 2][:, c * P:(c + 1) * P],
                                 x_f[:, c * TOK + th * 512: c * TOK + (th + 1) * 512],
                                 start=(c == 0), stop=False)
                    t.matmul(p6.ps[qi % 2][:, :], ones1[:, :], p6.rsrow[0:1, tsl],
                             start=False, stop=True)
                    inst = None
                    for c in range(NCH):
                        t.matmul(p6.pu[qi % 2][:, :],
                                 p6.wqu_b[qi % 2][:, c * P:(c + 1) * P],
                                 x_f[:, c * TOK + th * 512: c * TOK + (th + 1) * 512],
                                 start=(c == 0), stop=False)
                    inst = t.matmul(p6.pu[qi % 2][:, :], ones1[:, :], p6.rsrow[0:1, tsl],
                                    start=False, stop=True)
                    inst.then_inc(p6.mm, 1)
                t.wait_ge(p6.a2s, NHC * (th + 1))
                inst = None
                for hc in range(NHC):
                    inst = t.matmul(p6.prow[0:1, :], nones[:, :],
                                    p6.a2sb[:, hc * 512:(hc + 1) * 512],
                                    start=(hc == 0), stop=(hc == NHC - 1))
                inst.then_inc(p6.rsm, 1)
                t.wait_ge(p6.rse, 2 * th + 2)
                for dc in range(NCH):
                    di = th * NCH + dc
                    t.wait_ge(p6.ldd[0], 16 * (di + 1))
                    if di >= 1:
                        t.wait_ge(p6.dse, di)
                    for hc in range(NHC):
                        t.matmul(p6.pu[1][:, :],
                                 p6.wqd_b[di % 2][:, hc * P:(hc + 1) * P],
                                 p6.a2sb[:, hc * 512:(hc + 1) * 512],
                                 start=(hc == 0), stop=False)
                    inst = t.matmul(p6.pu[1][:, :], ones1[:, :], p6.csrow[0:1, tsl],
                                    start=False, stop=True)
                    inst.then_inc(p6.dmm, 1)
                    t.wait_ge(p6.dse, di + 1)
                    if di >= 2:
                        t.wait_ge(p6.dev, di - 1)
                    last = None
                    for j in range(4):
                        last = t.transpose(p6.ptrb[di % 2][:, j * P:(j + 1) * P],
                                           p6.yfb[di % 2][:, j * P:(j + 1) * P],
                                           ibfsb[:, :])
                    last.then_inc(p6.dtr, 1)

        @block.vector
        def _(v):
            v.wait_ge(cst, 81)
            gi = 0
            for ob in range(12):
                for tt in range(NTT):
                    v.wait_ge(p1.mm, gi + 1)
                    if ob >= 2:
                        v.wait_ge(p1.so[0], 16 * (ob - 1))
                    v.tensor_copy(p1.ev[ob % 2][:, tt * 512:(tt + 1) * 512],
                                  p1.ps[gi % 2][:, :]).then_inc(p1.evs, 1)
                    gi += 1
            nvself = [0]

            def p2_v_softmax(idx):
                h, gq = divmod(idx, 4)
                rec = p2.sched[idx]
                smm = rec["smm0"]
                for i in range(4):
                    qt = 4 * gq + i
                    chs = chunks_of(qt)
                    for (kc, cw) in chs:
                        if kc == len(chs) - 1:
                            v.wait_ge(p2.ssem, smm + 1)
                            v.tensor_tensor(p2.ps_s[smm % 2][:, cw - P:cw],
                                            p2.ps_s[smm % 2][:, cw - P:cw],
                                            mnegsb[:, :], ALU.add).then_inc(p2.msem, 1)
                        smm += 1
                    v.wait_ge(p2.esem, smm)
                    nkc = len(chs)
                    zq = rec["zsc0"] + i
                    if zq > 0:
                        v.wait_ge(p2.zsem, zq)
                    v.reduce_sum(p2.zinv[:, i:i + 1], p2.zp[:, i * 4:i * 4 + nkc],
                                 AX.X).then_inc(p2.vself, 1)
                    nvself[0] += 1
                    v.wait_ge(p2.vself, nvself[0])
                    v.reciprocal(p2.zinv[:, i:i + 1], p2.zinv[:, i:i + 1]
                                 ).then_inc(p2.vself, 1)
                    nvself[0] += 1
                    v.wait_ge(p2.vself, nvself[0])
                    v.tensor_scalar_mul(p2.psb[idx % 2][:, i * T: i * T + (qt + 1) * P],
                                        p2.psb[idx % 2][:, i * T: i * T + (qt + 1) * P],
                                        p2.zinv[:, i:i + 1]).then_inc(p2.zsem, 1)

            def p2_v_evacs(idx):
                h, gq = divmod(idx, 4)
                rec = p2.sched[idx]
                ktc = rec["kt0"]
                for kt in range(4 * gq + 4):
                    i0 = max(0, kt - 4 * gq)
                    v.wait_ge(p2.tsem, ktc + 1)
                    v.tensor_copy(p2.ptsb[ktc % 2][:, i0 * P:512],
                                  p2.pt_ps[ktc % 2][:, i0 * P:512]).then_inc(p2.ptsem, 1)
                    ktc += 1
                v.wait_ge(p2.avsem, rec["kt1"])
                if h >= 2 and gq == 0:
                    v.wait_ge(p2.sto[0], 16 * (h - 1))
                v.tensor_copy(p2.osb[h % 2][:, gq * 512:(gq + 1) * 512],
                              p2.o_ps[:, :]).then_inc(p2.osem, 1)

            for idx in range(32):
                p2_v_softmax(idx)
                if idx >= 1:
                    p2_v_evacs(idx - 1)
            p2_v_evacs(31)
            gi = 0
            for ob in range(4):
                for tt in range(NTT):
                    v.wait_ge(p3.mm, gi + 1)
                    v.tensor_tensor(stream[:, tt * D + ob * 512: tt * D + (ob + 1) * 512],
                                    p3.ps[gi % 2][:, :],
                                    stream[:, tt * D + ob * 512: tt * D + (ob + 1) * 512],
                                    ALU.add).then_inc(p3.evs, 1)
                    gi += 1

            def ln_vector(ph, tt, lnv_base, lns_base):
                sl = slice(tt * D, (tt + 1) * D)
                v.reduce_sum(sums[:, tt:tt + 1], stream[:, sl], AX.X).then_inc(ph.lnv, 1)
                v.wait_ge(ph.lns, lns_base + 2 * tt + 1)
                v.wait_ge(ph.lnv, lnv_base + 7 * tt + 1)
                v.tensor_scalar_mul(mu[:, tt:tt + 1], sums[:, tt:tt + 1], 1.0 / D
                                    ).then_inc(ph.lnv, 1)
                v.wait_ge(ph.lnv, lnv_base + 7 * tt + 2)
                v.tensor_scalar_mul(varv[:, tt:tt + 1], sq[:, tt:tt + 1], 1.0 / D
                                    ).then_inc(ph.lnv, 1)
                v.wait_ge(ph.lnv, lnv_base + 7 * tt + 3)
                v.scalar_tensor_tensor(varv[:, tt:tt + 1], mu[:, tt:tt + 1], mu[:, tt:tt + 1],
                                       varv[:, tt:tt + 1], ALU.mult, ALU.subtract
                                       ).then_inc(ph.lnv, 1)
                v.wait_ge(ph.lns, lns_base + 2 * tt + 2)
                v.reciprocal(rsd[:, tt:tt + 1], rsd[:, tt:tt + 1]).then_inc(ph.lnv, 1)
                v.wait_ge(ph.lnv, lnv_base + 7 * tt + 5)
                v.tensor_scalar(stream[:, sl], stream[:, sl],
                                mu[:, tt:tt + 1], rsd[:, tt:tt + 1],
                                ALU.subtract, ALU.mult).then_inc(ph.lnv, 2)

            v.wait_ge(p3.evs, 32)
            for tt in range(NTT):
                ln_vector(p3, tt, 0, 0)
            NGOP = 9
            for tt in range(NTT):
                g8 = slice(tt * 8, (tt + 1) * 8)
                v.wait_ge(p3.gsem, tt + 1)
                if tt > 0:
                    v.wait_ge(p3.vch, NGOP * tt)
                v.reduce_max(p3.t2a[:, :], p3.gexp[:, g8], AX.X).then_inc(p3.vch, 1)
                v.wait_ge(p3.vch, NGOP * tt + 1)
                v.tensor_scalar(p3.e8a[:, :], p3.gexp[:, g8], p3.t2a[:, 0:1], None,
                                ALU.is_equal).then_inc(p3.vch, 1)
                v.wait_ge(p3.vch, NGOP * tt + 2)
                v.scalar_tensor_tensor(p3.e8a[:, :], p3.e8a[:, :], -1e30, p3.gexp[:, g8],
                                       ALU.mult, ALU.add).then_inc(p3.vch, 1)
                v.wait_ge(p3.vch, NGOP * tt + 3)
                v.reduce_max(p3.t2b[:, :], p3.e8a[:, :], AX.X).then_inc(p3.vch, 1)
                v.wait_ge(p3.vch, NGOP * tt + 4)
                v.tensor_scalar(p3.e8a[:, :], p3.gexp[:, g8], p3.t2b[:, 0:1], None,
                                ALU.is_ge).then_inc(p3.vch, 1)
                v.wait_ge(p3.vch, NGOP * tt + 5)
                v.tensor_tensor(p3.e8a[:, :], p3.e8a[:, :], p3.gexp[:, g8],
                                ALU.mult).then_inc(p3.vch, 1)
                v.tensor_tensor(p3.t2a[:, :], p3.t2a[:, :], p3.t2b[:, :],
                                ALU.add).then_inc(p3.vch, 1)
                v.wait_ge(p3.vch, NGOP * tt + 7)
                v.reciprocal(p3.t2a[:, :], p3.t2a[:, :]).then_inc(p3.vch, 1)
                v.wait_ge(p3.vch, NGOP * tt + 8)
                v.tensor_scalar_mul(p3.comb[:, g8], p3.e8a[:, :], p3.t2a[:, 0:1]
                                    ).then_inc(p3.vch, 1)
            v.wait_ge(p3.vch, NGOP * NTT)
            for e in range(8):
                pi = 0
                for (foff, fw) in PIECES:
                    for tt in range(NTT):
                        gidx = e * 16 + pi
                        v.wait_ge(p3.sil, gidx + 1)
                        if gidx >= 2:
                            v.wait_ge(p3.trs, 32 + gidx - 1)
                        v.scalar_tensor_tensor(p3.a2t[gidx % 2][:, 0:fw],
                                               p3.pu[gidx % 2][:, 0:fw],
                                               p3.comb[:, tt * 8 + e: tt * 8 + e + 1],
                                               p3.gs[gidx % 2][:, 0:fw],
                                               ALU.mult, ALU.mult).then_inc(p3.a2s, 1)
                        pi += 1
                for db in range(4):
                    for tt in range(NTT):
                        di = e * 32 + db * 8 + tt
                        v.wait_ge(p3.dmm, 6 * (di + 1))
                        v.tensor_tensor(stream[:, tt * D + db * 512: tt * D + (db + 1) * 512],
                                        p3.ps[di % 2][:, :],
                                        stream[:, tt * D + db * 512: tt * D + (db + 1) * 512],
                                        ALU.add).then_inc(p3.dev, 1)
            v.wait_ge(p3.dev, 256)
            for tt in range(NTT):
                ln_vector(p3, tt, 56, 16)
            nv4 = [0]
            v.wait_ge(p4.ldc, 48)
            for mi in range(2):
                for tt in range(NTT):
                    v.wait_ge(p4.esem, mi * 32 + 24 + tt + 1)
                    v.wait_ge(p4.vch, nv4[0])
                    st3 = p4.stage[:, tt * D:(tt + 1) * D].rearrange("p (h f) -> p h f", h=32)
                    x1 = st3[:, :, 0:32]
                    x2v = st3[:, :, 32:64]
                    ccb = p4.cosb[:, tt * 64: tt * 64 + 32].unsqueeze(1).broadcast_to([P, 32, 32])
                    ssb = p4.sinb[:, tt * 64: tt * 64 + 32].unsqueeze(1).broadcast_to([P, 32, 32])
                    t3 = [p4.tmp[i][:, :].rearrange("p (h f) -> p h f", h=32) for i in range(4)]
                    v.tensor_tensor(t3[0], x1, ccb, ALU.mult).then_inc(p4.vch, 1); nv4[0] += 1
                    v.tensor_tensor(t3[1], x2v, ssb, ALU.mult).then_inc(p4.vch, 1); nv4[0] += 1
                    v.tensor_tensor(t3[2], x1, ssb, ALU.mult).then_inc(p4.vch, 1); nv4[0] += 1
                    v.tensor_tensor(t3[3], x2v, ccb, ALU.mult).then_inc(p4.vch, 1); nv4[0] += 1
                    v.wait_ge(p4.vch, nv4[0])
                    v.tensor_tensor(x1, t3[0], t3[1], ALU.subtract).then_inc(p4.vch, 1); nv4[0] += 1
                    v.tensor_tensor(x2v, t3[2], t3[3], ALU.add).then_inc(p4.vch, 1); nv4[0] += 1
            v.wait_ge(p5.gtr, 32)
            v.tensor_tensor_scan(p5.gcum[:, :], p5.gf[:, :], p5.gf[:, :], 0.0,
                                 ALU.add, ALU.bypass).then_inc(p5.gsc, 1)
            nv5 = [0]
            v.wait_ge(p5.gst, 16)

            def p5_v_amul(idx):
                gq = idx % 4
                rec = p5.sched[idx]
                smm = rec["smm0"]
                if idx >= 2:
                    v.wait_ge(p5.tsem, p5.sched[idx - 2]["kt1"])
                for i in range(4):
                    qt = 4 * gq + i
                    chs = chunks_of(qt)
                    for (kc, cw) in chs:
                        isdiag = (kc == len(chs) - 1)
                        v.wait_ge(p5.esem, smm + 1)
                        if isdiag:
                            v.tensor_tensor(p5.esb[smm % 2][:, cw - P:cw],
                                            p5.esb[smm % 2][:, cw - P:cw],
                                            mbfsb[:, :], ALU.mult).then_inc(p5.vsf, 1)
                            nv5[0] += 1
                            v.wait_ge(p5.vsf, nv5[0])
                        v.wait_ge(p5.ssem, smm + 1)
                        v.scalar_tensor_tensor(
                            p5.asb[idx % 2][:, i * T + kc * 512: i * T + kc * 512 + cw],
                            p5.ps_s[smm % 2][:, 0:cw], 0.125,
                            p5.esb[smm % 2][:, 0:cw],
                            ALU.mult, ALU.mult).then_inc(p5.asem, 1)
                        smm += 1

            def p5_v_evacs(idx):
                pr, gq = divmod(idx, 4)
                rec = p5.sched[idx]
                ktc = rec["kt0"]
                for kt in range(4 * gq + 4):
                    i0 = max(0, kt - 4 * gq)
                    v.wait_ge(p5.tsem, ktc + 1)
                    v.tensor_copy(p5.ptsb[ktc % 2][:, i0 * P:512],
                                  p5.pt_ps[ktc % 2][:, i0 * P:512]).then_inc(p5.ptsem, 1)
                    ktc += 1
                v.wait_ge(p5.avsem, rec["kt1"])
                if pr >= 2 and gq == 0:
                    v.wait_ge(p5.sto[0], 16 * (pr - 1))
                v.tensor_copy(p5.osb[pr % 2][:, gq * 512:(gq + 1) * 512],
                              p5.o_ps[0:64, :]).then_inc(p5.osem, 1)

            for idx5 in range(64):
                if idx5 >= 1 and idx5 % 4 == 0:
                    p5_v_evacs(idx5 - 1)
                p5_v_amul(idx5)
                if idx5 >= 1 and idx5 % 4 != 0:
                    p5_v_evacs(idx5 - 1)
            p5_v_evacs(63)

            v.wait_ge(p3.lnv, 112)
            for tt in range(NTT):
                v.wait_ge(p6.ldo[0], 16 * (tt + 1))
                v.wait_ge(p6.onr, 5 * tt)
                v.wait_ge(p6.lns, 2 * tt + 1)
                o3 = p6.otb[tt % 2][:, :].rearrange("p (h f) -> p h f", h=32)
                v.reduce_sum(p6.hvar[:, :].unsqueeze(2),
                             scr[:, :].rearrange("p (h f) -> p h f", h=32),
                             AX.X).then_inc(p6.onr, 1)
                v.wait_ge(p6.onr, 5 * tt + 1)
                v.wait_ge(p6.lns, 2 * tt + 2)
                v.reciprocal(p6.hvar[:, :], p6.hvar[:, :]).then_inc(p6.onr, 1)
                v.wait_ge(p6.onr, 5 * tt + 2)
                v.tensor_tensor(p6.otb[tt % 2][:, :].rearrange("p (h f) -> p h f", h=32),
                                o3, p6.hvar[:, :].unsqueeze(2).broadcast_to([P, 32, 64]),
                                ALU.mult).then_inc(p6.onr, 1)
                v.wait_ge(p6.onr, 5 * tt + 3)
                v.wait_ge(p6.sil, NGP6)
                v.tensor_tensor(p6.sg_sb[:, tt * D:(tt + 1) * D],
                                p6.sg_sb[:, tt * D:(tt + 1) * D],
                                p6.otb[tt % 2][:, :], ALU.mult).then_inc(p6.onr, 2)
            gi = 0
            for ob in range(8):
                for tt in range(NTT):
                    grp = NGP6 + gi
                    v.wait_ge(p6.mm, grp + 1)
                    v.tensor_tensor(stream[:, tt * D + ob * 256: tt * D + (ob + 1) * 256],
                                    p6.ps[grp % 2][:, 0:256],
                                    stream[:, tt * D + ob * 256: tt * D + (ob + 1) * 256],
                                    ALU.add).then_inc(p6.evs, 1)
                    gi += 1
            v.wait_ge(p6.evs, NWO6)
            for tt in range(NTT):
                ln_vector(p6, tt, 0, 16)
            for th in range(2):
                for hc in range(NHC):
                    qi = th * NHC + hc
                    v.wait_ge(p6.gup, 2 * (qi + 1))
                    v.tensor_tensor(p6.a2sb[:, hc * 512:(hc + 1) * 512],
                                    p6.sgh[qi % 2][:, :],
                                    p6.a2sb[:, hc * 512:(hc + 1) * 512],
                                    ALU.mult).then_inc(p6.a2s, 1)
                for dc in range(NCH):
                    di = th * NCH + dc
                    v.wait_ge(p6.dtr, di + 1)
                    for j in range(4):
                        tt = th * 4 + j
                        inst = v.tensor_tensor(
                            stream[:, tt * D + dc * P: tt * D + (dc + 1) * P],
                            p6.ptrb[di % 2][:, j * P:(j + 1) * P],
                            stream[:, tt * D + dc * P: tt * D + (dc + 1) * P],
                            ALU.add)
                        if j == 3:
                            inst.then_inc(p6.dev, 1)
            v.wait_ge(p6.dev, 32)
            for tt in range(NTT):
                ln_vector(p6, tt, 56, 32)

        @block.scalar
        def _(s):
            s.wait_ge(cst, 81)
            SCALE2 = 128.0 ** -0.5
            for h in range(8):
                for gq in range(4):
                    idx = h * 4 + gq
                    rec = p2.sched[idx]
                    if idx > 1:
                        prev = p2.sched[idx - 2]
                        s.wait_ge(p2.tsem, prev["kt1"])
                    if idx > 0:
                        s.wait_ge(p2.zsem, p2.sched[idx - 1]["zsc1"])
                    smm = rec["smm0"]
                    mk = rec["mask0"]
                    for i in range(4):
                        qt = 4 * gq + i
                        chs = chunks_of(qt)
                        for (kc, cw) in chs:
                            if kc == len(chs) - 1:
                                mk += 1
                                s.wait_ge(p2.msem, mk)
                            else:
                                s.wait_ge(p2.ssem, smm + 1)
                            s.activation(p2.psb[idx % 2][:, i * T + kc * 512: i * T + kc * 512 + cw],
                                         p2.ps_s[smm % 2][:, 0:cw], AF.Exp,
                                         bias=shcol[:, :], scale=SCALE2,
                                         accum_out=p2.zp[:, i * 4 + kc:i * 4 + kc + 1]
                                         ).then_inc(p2.esem, 1)
                            smm += 1
            def ln_scalar(ph, tt, lnv_base, lns_base):
                sl = slice(tt * D, (tt + 1) * D)
                s.wait_ge(ph.lns, lns_base + 2 * tt)
                s.activation(scr[:, :], stream[:, sl], AF.Square,
                             accum_out=sq[:, tt:tt + 1]).then_inc(ph.lns, 1)
                s.wait_ge(ph.lnv, lnv_base + 7 * tt + 4)
                s.activation(rsd[:, tt:tt + 1], varv[:, tt:tt + 1], AF.Sqrt,
                             bias=epsln[:, :], scale=-1.0).then_inc(ph.lns, 1)

            s.wait_ge(p3.evs, 32)
            for tt in range(NTT):
                ln_scalar(p3, tt, 0, 0)
            ntr = 0
            for tt in range(NTT):
                for cg in range(4):
                    s.wait_ge(p3.trs, ntr + 1)
                    s.copy(x_f[:, :].rearrange("p (c t) -> p c t", c=NCH)
                           [:, cg * 4:(cg + 1) * 4, tt * P:(tt + 1) * P],
                           p3.pxf[ntr % 2][:, :].rearrange("p (c4 t) -> p c4 t", c4=4)
                           ).then_inc(p3.xfs, 1)
                    ntr += 1
            for tt in range(NTT):
                s.wait_ge(p3.mm, 32 + tt + 1)
                s.activation(p3.gexp[:, tt * 8:(tt + 1) * 8], p3.pg[0][:, 0:8],
                             AF.Exp).then_inc(p3.gsem, 1)
            for e in range(8):
                pi = 0
                for (foff, fw) in PIECES:
                    for tt in range(NTT):
                        gidx = e * 16 + pi
                        s.wait_ge(p3.mm, mm_gu3(gidx))
                        if gidx >= 2:
                            s.wait_ge(p3.a2s, gidx - 1)
                        s.activation(p3.gs[gidx % 2][:, 0:fw], p3.pg[gidx % 2][:, 0:fw],
                                     AF.Silu).then_inc(p3.sil, 1)
                        s.wait_ge(p3.trs, 32 + gidx + 1)
                        if e >= 1 and pi == 0:
                            s.wait_ge(p3.dmm, 192 * e)
                        nt = fw // P
                        s.copy(p3.a2f[:, tt * 6 * P + (foff // P) * P:
                                      tt * 6 * P + (foff // P) * P + nt * P],
                               p3.ptr[gidx % 2][:, 0:nt * P]).then_inc(p3.a2e, 1)
                        pi += 1
            s.wait_ge(p3.dev, 256)
            for tt in range(NTT):
                ln_scalar(p3, tt, 56, 16)
            ntr = 0
            for tt in range(NTT):
                for cg in range(4):
                    s.wait_ge(p4.trs, ntr + 1)
                    s.copy(x_f[:, :].rearrange("p (c t) -> p c t", c=NCH)
                           [:, cg * 4:(cg + 1) * 4, tt * P:(tt + 1) * P],
                           p4.ps[ntr % 2][:, :].rearrange("p (c4 t) -> p c4 t", c4=4)
                           ).then_inc(p4.xfs, 1)
                    ntr += 1
            gi = 0
            for mi in range(3):
                for ob in range(4):
                    for tt in range(NTT):
                        grp = gi * 8 + tt
                        s.wait_ge(p4.mm, grp + 1)
                        if mi > 0 and ob == 0 and tt == 0:
                            s.wait_ge(p4.sto, 16 * mi)
                        s.copy(p4.stage[:, tt * D + ob * 512: tt * D + (ob + 1) * 512],
                               p4.ps[grp % 2][:, :]).then_inc(p4.esem, 1)
                    gi += 1
            for tt in range(NTT):
                s.wait_ge(p4.mgk, tt + 1)
                s.activation(p4.gks[:, tt * 32:(tt + 1) * 32], p4.pgk[:, :],
                             AF.Sigmoid).then_inc(p4.gsig, 1)
            for tt in range(NTT):
                s.wait_ge(p4.gsig, NTT + tt)
                s.activation(p4.gks[:, tt * 32:(tt + 1) * 32],
                             p4.gks[:, tt * 32:(tt + 1) * 32],
                             AF.Ln).then_inc(p4.gsig, 1)
            for kt in range(16):
                s.wait_ge(p5.gtr, 2 * kt + 1)
                s.copy(p5.gf[0:32, kt * P:(kt + 1) * P],
                       p5.pbc[0:32, 0:P]).then_inc(p5.gtr, 1)
            for pr in range(16):
                s.wait_ge(p5.gqt, pr + 1)
                if pr >= 1:
                    s.wait_ge(p5.esem, p5.sched[(pr - 1) * 4 + 3]["exp1"])
                s.activation(p5.gq16[:, :], p5.pbc[:, 0:16], AF.Copy,
                             scale=1.0 / 16.0).then_inc(p5.gqe, 1)
                for j in range(4):
                    s.wait_ge(p5.bcs, 4 * pr + j + 1)
                    s.copy(p5.gkbc[:, j * 512:(j + 1) * 512], p5.pbc[:, :]
                           ).then_inc(p5.bce, 1)
                for gq in range(4):
                    rec = p5.sched[pr * 4 + gq]
                    smm = rec["smm0"]
                    if gq == 0:
                        s.wait_ge(p5.bce, 4 * (pr + 1))
                    for i in range(4):
                        qt = 4 * gq + i
                        for (kc, cw) in chunks_of(qt):
                            if smm >= 2:
                                s.wait_ge(p5.asem, smm - 1)
                            s.activation(p5.esb[smm % 2][:, 0:cw],
                                         p5.gkbc[:, kc * 512:kc * 512 + cw],
                                         AF.Exp, bias=p5.gq16[:, qt:qt + 1],
                                         scale=-1.0 / 16.0).then_inc(p5.esem, 1)
                            smm += 1
            s.wait_ge(p3.lns, 32)
            s.wait_ge(p6.ldm, 48)
            gi = 0
            for ob in range(8):
                for tt in range(NTT):
                    s.wait_ge(p6.mm, gi + 1)
                    s.activation(p6.sg_sb[:, tt * D + ob * 256: tt * D + (ob + 1) * 256],
                                 p6.ps[gi % 2][:, 0:256], AF.Silu).then_inc(p6.sil, 1)
                    gi += 1
            for tt in range(NTT):
                s.wait_ge(p6.ldo[0], 16 * (tt + 1))
                if tt >= 1:
                    s.wait_ge(p6.onr, 5 * (tt - 1) + 1)
                s.activation(scr[:, :], p6.otb[tt % 2][:, :], AF.Square).then_inc(p6.lns, 1)
                s.wait_ge(p6.onr, 5 * tt + 1)
                s.activation(p6.hvar[:, :], p6.hvar[:, :], AF.Sqrt,
                             bias=epsrm[:, :], scale=1.0 / 64.0).then_inc(p6.lns, 1)
            ntr = 0
            for tt in range(NTT):
                for cg in range(4):
                    s.wait_ge(p6.trs, ntr + 1)
                    s.copy(p6.ogf[:, :].rearrange("p (c t) -> p c t", c=NCH)
                           [:, cg * 4:(cg + 1) * 4, tt * P:(tt + 1) * P],
                           p6.ptrb[ntr % 2][:, 0:512].rearrange("p (c4 t) -> p c4 t", c4=4)
                           ).then_inc(p6.xfs, 1)
                    ntr += 1
            s.wait_ge(p6.evs, NWO6)
            for tt in range(NTT):
                ln_scalar(p6, tt, 0, 16)
            for tt in range(NTT):
                for cg in range(4):
                    s.wait_ge(p6.trs, ntr + 1)
                    s.copy(x_f[:, :].rearrange("p (c t) -> p c t", c=NCH)
                           [:, cg * 4:(cg + 1) * 4, tt * P:(tt + 1) * P],
                           p6.ps[ntr % 2][:, :].rearrange("p (c4 t) -> p c4 t", c4=4)
                           ).then_inc(p6.xfs, 1)
                    ntr += 1
            for th in range(2):
                tsl = slice(th * 512, (th + 1) * 512)
                s.wait_ge(p6.rsm, 2 * th + 1)
                s.copy(p6.rsrow[0:1, tsl], p6.prow[0:1, :]).then_inc(p6.rse, 1)
                for hc in range(NHC):
                    qi = th * NHC + hc
                    s.wait_ge(p6.mm, NGP6 + NWO6 + qi + 1)
                    if qi >= 2:
                        s.wait_ge(p6.a2s, qi - 1)
                    s.activation(p6.sgh[qi % 2][:, :], p6.ps[qi % 2][:, :], AF.Silu,
                                 scale=p6.sgc[:, hc:hc + 1]).then_inc(p6.gup, 1)
                    s.activation(p6.a2sb[:, hc * 512:(hc + 1) * 512], p6.pu[qi % 2][:, :],
                                 AF.Copy, scale=p6.suc[:, hc:hc + 1]).then_inc(p6.gup, 1)
                s.wait_ge(p6.rsm, 2 * th + 2)
                s.copy(p6.csrow[0:1, tsl], p6.prow[0:1, :]).then_inc(p6.rse, 1)
                for dc in range(NCH):
                    di = th * NCH + dc
                    s.wait_ge(p6.dmm, di + 1)
                    if di >= 2:
                        s.wait_ge(p6.dtr, di - 1)
                    s.activation(p6.yfb[di % 2][:, :], p6.pu[1][:, :], AF.Copy,
                                 scale=p6.sdc[:, dc:dc + 1]).then_inc(p6.dse, 1)
            s.wait_ge(p6.dev, 32)
            for tt in range(NTT):
                ln_scalar(p6, tt, 56, 32)

    ctx.close()
    return nc


from concourse.bass_utils import run_bass_kernel_spmd
import ml_dtypes

_BF = ml_dtypes.bfloat16
_cc = np.ascontiguousarray


def _planar_perm():
    perm = np.empty(2048, np.int64)
    for h in range(32):
        for i in range(64):
            perm[h * 64 + i] = h * 64 + (2 * i if i < 32 else 2 * (i - 32) + 1)
    return perm


def kernel(L0_attn_x, L0_attn_w_attn, L0_attn_b_attn, L0_attn_w_proj, L0_attn_b_proj,
           L0_ffn_gate_w, L0_ffn_w1, L0_ffn_w2, L0_ffn_w3,
           L1_attn_Wq, L1_attn_Wk, L1_attn_Wv, L1_attn_Wg, L1_attn_Wgk, L1_attn_Wo,
           L1_ffn_wq_gate, L1_ffn_s_gate, L1_ffn_wq_up, L1_ffn_s_up,
           L1_ffn_wq_down, L1_ffn_s_down):
    x = np.asarray(L0_attn_x, np.float32)
    perm = _planar_perm()
    weights = {
        "w_attn": np.asarray(L0_attn_w_attn, np.float32),
        "b_attn": np.asarray(L0_attn_b_attn, np.float32),
        "w_proj": np.asarray(L0_attn_w_proj, np.float32),
        "b_proj": np.asarray(L0_attn_b_proj, np.float32),
        "gate_w": np.asarray(L0_ffn_gate_w, np.float32),
        "w1": np.asarray(L0_ffn_w1, np.float32),
        "w2": np.asarray(L0_ffn_w2, np.float32),
        "w3": np.asarray(L0_ffn_w3, np.float32),
        "WqT": _cc(np.asarray(L1_attn_Wq, np.float32).T[:, perm]),
        "WkT": _cc(np.asarray(L1_attn_Wk, np.float32).T[:, perm]),
        "WvT": _cc(np.asarray(L1_attn_Wv, np.float32).T),
        "WgT": _cc(np.asarray(L1_attn_Wg, np.float32).T),
        "WgkT": _cc(np.asarray(L1_attn_Wgk, np.float32).T),
        "WoT": _cc(np.asarray(L1_attn_Wo, np.float32).T),
        "wqgT": _cc(np.asarray(L1_ffn_wq_gate, np.int32).T),
        "wquT": _cc(np.asarray(L1_ffn_wq_up, np.int32).T),
        "wqdT": _cc(np.asarray(L1_ffn_wq_down, np.int32).T),
        "sg2d": _cc(np.asarray(L1_ffn_s_gate, np.float32).reshape(44, 128).T),
        "su2d": _cc(np.asarray(L1_ffn_s_up, np.float32).reshape(44, 128).T),
        "sd2d": _cc(np.asarray(L1_ffn_s_down, np.float32).reshape(16, 128).T),
    }
    id32 = np.eye(128, dtype=np.float32)
    consts = {
        "id32": id32,
        "idbf": id32.astype(_BF),
        "maskneg": np.triu(np.full((128, 128), -1e30, np.float32), 1),
        "maskbf": np.tril(np.ones((128, 128), np.float32)).astype(_BF),
    }
    inv = 1.0 / 10000.0 ** (np.arange(0, 64, 2) / 64.0)
    in_maps = []
    for c in range(8):
        b, half = c // 2, c % 2
        sl = slice(half * 1024, half * 1024 + 1024)
        pos = np.arange(half * 1024, half * 1024 + 1024, dtype=np.float64)
        fr = pos[:, None] * inv[None, :]
        cosm = np.concatenate([np.cos(fr), np.cos(fr)], 1).astype(np.float32)
        sinm = np.concatenate([np.sin(fr), np.sin(fr)], 1).astype(np.float32)
        m = {"x_f": _cc(x[b, sl].T), "x_t": _cc(x[b, sl]),
             "coreinfo": np.array([[1024 * half, 1024 * half, 1024 * half,
                                    16 * half]], np.int32),
             "rope_cos": cosm, "rope_sin": sinm}
        m.update(weights)
        m.update(consts)
        in_maps.append(m)
    nc = build_fused()
    res = run_bass_kernel_spmd(nc, in_maps, core_ids=list(range(8)))
    out = np.empty((4, 2048, 2048), np.float32)
    for c in range(8):
        b, half = c // 2, c % 2
        out[b, half * 1024:half * 1024 + 1024] = res.results[c]["y"]
    return out


# version 05
# speedup vs baseline: 1.1308x; 1.0649x over previous
import contextlib
from types import SimpleNamespace as NS
import numpy as np
import concourse.bass as bass
import concourse.mybir as mybir

F32 = mybir.dt.float32
BF16 = mybir.dt.bfloat16
I32 = mybir.dt.int32
AF = mybir.ActivationFunctionType
ALU = mybir.AluOpType
AX = mybir.AxisListType

P = 128
D = 2048
NCH = 16
TOK = 1024
NTT = 8
T = 2048
EPS_LN = 1e-5
EPS_RMS = 1e-6
PIECES = [(0, 512), (512, 256)]


def chunks_of(qt):
    klen = (qt + 1) * P
    out = []
    kc = 0
    while kc * 512 < klen:
        out.append((kc, min(512, klen - kc * 512)))
        kc += 1
    return out


def attn_sched(npairs):
    sched = []
    c_smm = c_mask = c_exp = c_zsc = c_kt = c_oev = 0
    for p in range(npairs):
        for g in range(4):
            rec = {"smm0": c_smm, "mask0": c_mask, "exp0": c_exp,
                   "zsc0": c_zsc, "kt0": c_kt, "oev0": c_oev}
            for i in range(4):
                qt = 4 * g + i
                n = len(chunks_of(qt))
                c_smm += n
                c_exp += n
                c_mask += 1
                c_zsc += 1
            c_kt += 4 * g + 4
            c_oev += 1
            rec.update({"smm1": c_smm, "mask1": c_mask, "exp1": c_exp,
                        "zsc1": c_zsc, "kt1": c_kt, "oev1": c_oev})
            sched.append(rec)
    return sched


def build_fused():
    nc = bass.Bass("TRN2", target_bir_lowering=False)

    x_f_in = nc.dram_tensor("x_f", [D, TOK], F32, kind="ExternalInput")
    x_t_in = nc.dram_tensor("x_t", [TOK, D], F32, kind="ExternalInput")
    coreinfo = nc.dram_tensor("coreinfo", [1, 4], I32, kind="ExternalInput")
    w_attn = nc.dram_tensor("w_attn", [D, 3 * D], F32, kind="ExternalInput")
    b_attn = nc.dram_tensor("b_attn", [3 * D], F32, kind="ExternalInput")
    w_proj = nc.dram_tensor("w_proj", [D, D], F32, kind="ExternalInput")
    b_proj = nc.dram_tensor("b_proj", [D], F32, kind="ExternalInput")
    gate_w = nc.dram_tensor("gate_w", [D, 8], F32, kind="ExternalInput")
    w1 = nc.dram_tensor("w1", [8, D, 768], F32, kind="ExternalInput")
    w2 = nc.dram_tensor("w2", [8, 768, D], F32, kind="ExternalInput")
    w3 = nc.dram_tensor("w3", [8, D, 768], F32, kind="ExternalInput")
    WqT = nc.dram_tensor("WqT", [D, D], F32, kind="ExternalInput")
    WkT = nc.dram_tensor("WkT", [D, D], F32, kind="ExternalInput")
    WvT = nc.dram_tensor("WvT", [D, D], F32, kind="ExternalInput")
    WgkT = nc.dram_tensor("WgkT", [D, 32], F32, kind="ExternalInput")
    WgT = nc.dram_tensor("WgT", [D, D], F32, kind="ExternalInput")
    WoT = nc.dram_tensor("WoT", [D, D], F32, kind="ExternalInput")
    wqgT = nc.dram_tensor("wqgT", [D, 5632], I32, kind="ExternalInput")
    wquT = nc.dram_tensor("wquT", [D, 5632], I32, kind="ExternalInput")
    wqdT = nc.dram_tensor("wqdT", [5632, D], I32, kind="ExternalInput")
    sg2d = nc.dram_tensor("sg2d", [P, 44], F32, kind="ExternalInput")
    su2d = nc.dram_tensor("su2d", [P, 44], F32, kind="ExternalInput")
    sd2d = nc.dram_tensor("sd2d", [P, NCH], F32, kind="ExternalInput")
    rope_cos = nc.dram_tensor("rope_cos", [TOK, 64], F32, kind="ExternalInput")
    rope_sin = nc.dram_tensor("rope_sin", [TOK, 64], F32, kind="ExternalInput")
    id32_in = nc.dram_tensor("id32", [P, P], F32, kind="ExternalInput")
    idbf_in = nc.dram_tensor("idbf", [P, P], BF16, kind="ExternalInput")
    maskneg_in = nc.dram_tensor("maskneg", [P, P], F32, kind="ExternalInput")
    maskbf_in = nc.dram_tensor("maskbf", [P, P], BF16, kind="ExternalInput")
    y_out = nc.dram_tensor("y", [TOK, D], F32, kind="ExternalOutput")

    A0own = nc.dram_tensor("A0own", [TOK, 3 * D], BF16)
    A0o = nc.dram_tensor("A0o", [T, 3 * D], BF16)
    B0own = nc.dram_tensor("B0own", [TOK, D], BF16)
    B0o = nc.dram_tensor("B0o", [T, D], BF16)
    A1own = nc.dram_tensor("A1own", [TOK, 3 * D], BF16)
    A1o = nc.dram_tensor("A1o", [T, 3 * D], BF16)
    Agown = nc.dram_tensor("Agown", [TOK, 32], F32)
    Ago = nc.dram_tensor("Ago", [T, 32], F32)
    B1own = nc.dram_tensor("B1own", [TOK, D], BF16)
    B1o = nc.dram_tensor("B1o", [T, D], BF16)
    gdram = nc.dram_tensor("gdram", [32, T], F32)
    A0loc = nc.dram_tensor("A0loc", [T, 3 * TOK], BF16)
    A1loc = nc.dram_tensor("A1loc", [T, 3 * TOK], BF16)
    B1loc = nc.dram_tensor("B1loc", [T, TOK], BF16)
    gloc = nc.dram_tensor("gloc", [16, T], F32)
    RG = [[0, 1], [2, 3], [4, 5], [6, 7]]

    ctx = contextlib.ExitStack()
    E = ctx.enter_context

    stream = E(nc.sbuf_tensor("stream", [P, NTT * D], F32))
    x_f = E(nc.sbuf_tensor("x_fb", [P, NCH * TOK], BF16))
    isb = E(nc.sbuf_tensor("isb", [1, 4], I32))
    i32sb = E(nc.sbuf_tensor("i32sb", [P, P], F32))
    ibfsb = E(nc.sbuf_tensor("ibfsb", [P, P], BF16))
    mnegsb = E(nc.sbuf_tensor("mnegsb", [P, P], F32))
    mbfsb = E(nc.sbuf_tensor("mbfsb", [P, P], BF16))
    ones1 = E(nc.sbuf_tensor("ones1", [1, P], BF16))
    onec32 = E(nc.sbuf_tensor("onec32", [1, P], F32))
    nones = E(nc.sbuf_tensor("nones", [P, 1], BF16))
    epsln = E(nc.sbuf_tensor("epsln", [P, 1], F32))
    epsrm = E(nc.sbuf_tensor("epsrm", [P, 1], F32))
    shcol = E(nc.sbuf_tensor("shcol", [P, 1], F32))
    sq = E(nc.sbuf_tensor("sq", [P, NTT], F32))
    sums = E(nc.sbuf_tensor("sums", [P, NTT], F32))
    mu = E(nc.sbuf_tensor("mu", [P, NTT], F32))
    varv = E(nc.sbuf_tensor("varv", [P, NTT], F32))
    rsd = E(nc.sbuf_tensor("rsd", [P, NTT], F32))
    scr = E(nc.sbuf_tensor("scr", [P, D], F32))
    sgc_p = E(nc.sbuf_tensor("sgc_p", [P, 44], F32))
    suc_p = E(nc.sbuf_tensor("suc_p", [P, 44], F32))
    sdc_p = E(nc.sbuf_tensor("sdc_p", [P, NCH], F32))

    cc0 = E(nc.semaphore("cc0"))
    cc1 = E(nc.semaphore("cc1"))
    cc2 = E(nc.semaphore("cc2"))
    cc3 = E(nc.semaphore("cc3"))
    cst = E(nc.semaphore("cst"))

    sc = contextlib.ExitStack()
    p1 = NS()
    p1.wblk = [sc.enter_context(nc.sbuf_tensor(f"p1_wblk{i}", [P, NCH * 512], BF16)) for i in range(2)]
    p1.ev = [sc.enter_context(nc.sbuf_tensor(f"p1_ev{i}", [P, NTT * 512], BF16)) for i in range(2)]
    p1.brow = sc.enter_context(nc.sbuf_tensor("p1_brow", [1, 3 * D], BF16))
    p1.ps = [sc.enter_context(nc.psum_tensor(f"p1_ps{i}", [P, 512], F32)) for i in range(2)]
    _s = E(nc.semaphore("p1_ldw"))
    p1.ldw = [_s, _s]
    p1.ldx = E(nc.semaphore("p1_ldx"))
    p1.mm = E(nc.semaphore("p1_mm"))
    p1.evs = E(nc.semaphore("p1_evs"))
    _s = E(nc.semaphore("p1_so"))
    p1.so = [_s, _s]
    sc.close()
    sc = contextlib.ExitStack()
    p2 = NS()
    p2.qsb = [sc.enter_context(nc.sbuf_tensor(f"p2_qsb{i}", [P, T], BF16)) for i in range(2)]
    p2.ksb = [sc.enter_context(nc.sbuf_tensor(f"p2_ksb{i}", [P, T], BF16)) for i in range(2)]
    p2.vsb = [sc.enter_context(nc.sbuf_tensor(f"p2_vsb{i}", [P, 16 * P], BF16)) for i in range(2)]
    p2.psb = [sc.enter_context(nc.sbuf_tensor(f"p2_psb{i}", [P, 4 * T], BF16)) for i in range(2)]
    p2.ptsb = [sc.enter_context(nc.sbuf_tensor(f"p2_ptsb{i}", [P, 512], BF16)) for i in range(2)]
    p2.zp = sc.enter_context(nc.sbuf_tensor("p2_zp", [P, 16], F32))
    p2.zinv = sc.enter_context(nc.sbuf_tensor("p2_zinv", [P, 4], F32))
    p2.osb = [sc.enter_context(nc.sbuf_tensor(f"p2_osb{i}", [P, T], BF16)) for i in range(2)]
    p2.ps_s = [sc.enter_context(nc.psum_tensor(f"p2_ps_s{i}", [P, 512], F32)) for i in range(4)]
    p2.pt_ps = [sc.enter_context(nc.psum_tensor(f"p2_pt_ps{i}", [P, 1024], BF16)) for i in range(2)]
    p2.o_ps = sc.enter_context(nc.psum_tensor("p2_o_ps", [P, 512], F32))
    for nm in ["qld", "ssem", "msem", "esem", "zsem", "tsem", "ptsem", "avsem",
               "osem", "vself"]:
        setattr(p2, nm, E(nc.semaphore(f"p2_{nm}")))
    _s = E(nc.semaphore("p2_sto"))
    p2.sto = [_s, _s]
    p2.sched = attn_sched(8)
    sc.close()
    sc = contextlib.ExitStack()
    p3 = NS()
    p3.gwsb = sc.enter_context(nc.sbuf_tensor("p3_gwsb", [P, NCH * 8], BF16))
    p3.bprow = sc.enter_context(nc.sbuf_tensor("p3_bprow", [1, D], BF16))
    p3.gexp = sc.enter_context(nc.sbuf_tensor("p3_gexp", [P, NTT * 8], F32))
    p3.comb = sc.enter_context(nc.sbuf_tensor("p3_comb", [P, NTT * 8], F32))
    p3.e8a = sc.enter_context(nc.sbuf_tensor("p3_e8a", [P, 8], F32))
    p3.t2a = sc.enter_context(nc.sbuf_tensor("p3_t2a", [P, 1], F32))
    p3.t2b = sc.enter_context(nc.sbuf_tensor("p3_t2b", [P, 1], F32))
    scb = contextlib.ExitStack()
    p3.ofb = scb.enter_context(nc.sbuf_tensor("p3_ofb", [P, NCH * TOK], BF16))
    p3.wpb = [scb.enter_context(nc.sbuf_tensor(f"p3_wpb{i}", [P, NCH * 512], BF16)) for i in range(2)]
    scb.close()
    p3.w13b = [sc.enter_context(nc.sbuf_tensor(f"p3_w13b{i}", [P, NCH * 512], BF16)) for i in range(4)]
    p3.w2b = [sc.enter_context(nc.sbuf_tensor(f"p3_w2b{i}", [P, 6 * 512], BF16)) for i in range(2)]
    p3.gs = [sc.enter_context(nc.sbuf_tensor(f"p3_gs{i}", [P, 512], F32)) for i in range(2)]
    p3.a2t = [sc.enter_context(nc.sbuf_tensor(f"p3_a2t{i}", [P, 512], BF16)) for i in range(2)]
    p3.a2f = sc.enter_context(nc.sbuf_tensor("p3_a2f", [P, NTT * 6 * P], BF16))
    p3.ps = [sc.enter_context(nc.psum_tensor(f"p3_ps{i}", [P, 512], F32)) for i in range(2)]
    p3.pg = [sc.enter_context(nc.psum_tensor(f"p3_pg{i}", [P, 512], F32)) for i in range(2)]
    p3.pu = [sc.enter_context(nc.psum_tensor(f"p3_pu{i}", [P, 512], F32)) for i in range(2)]
    p3.ptr = [sc.enter_context(nc.psum_tensor(f"p3_ptr{i}", [P, 1024], BF16)) for i in range(2)]
    p3.pxf = p3.pu
    for nm in ["ldx", "ldc", "mm", "evs", "lnv", "lns", "trs", "xfs", "gsem",
               "vch", "sil", "a2s", "a2e", "dmm", "dev"]:
        setattr(p3, nm, E(nc.semaphore(f"p3_{nm}")))
    _s = E(nc.semaphore("p3_ldw")); p3.ldw = [_s, _s]
    _s = E(nc.semaphore("p3_lde")); p3.lde = [_s, _s]
    _s = E(nc.semaphore("p3_ld2")); p3.ld2 = [_s, _s]
    sc.close()
    sc = contextlib.ExitStack()
    p4 = NS()
    p4.wblk = [sc.enter_context(nc.sbuf_tensor(f"p4_wblk{i}", [P, NCH * 512], BF16)) for i in range(2)]
    p4.wgk = sc.enter_context(nc.sbuf_tensor("p4_wgk", [P, NCH * 32], BF16))
    p4.stage = sc.enter_context(nc.sbuf_tensor("p4_stage", [P, NTT * D], BF16))
    p4.cosb = sc.enter_context(nc.sbuf_tensor("p4_cosb", [P, NTT * 64], BF16))
    p4.sinb = sc.enter_context(nc.sbuf_tensor("p4_sinb", [P, NTT * 64], BF16))
    p4.tmp = [sc.enter_context(nc.sbuf_tensor(f"p4_tmp{i}", [P, 1024], BF16)) for i in range(4)]
    p4.gks = sc.enter_context(nc.sbuf_tensor("p4_gks", [P, NTT * 32], F32))
    p4.ps = [sc.enter_context(nc.psum_tensor(f"p4_ps{i}", [P, 512], F32)) for i in range(2)]
    p4.pgk = sc.enter_context(nc.psum_tensor("p4_pgk", [P, 32], F32))
    for nm in ["ldc", "mm", "mgk", "esem", "gsig", "vch", "sto", "trs", "xfs"]:
        setattr(p4, nm, E(nc.semaphore(f"p4_{nm}")))
    _s = E(nc.semaphore("p4_ldw")); p4.ldw = [_s, _s]
    sc.close()
    sc = contextlib.ExitStack()
    p5 = NS()
    HD5 = 64
    p5.qsb = [sc.enter_context(nc.sbuf_tensor(f"p5_qsb{i}", [P, T], BF16)) for i in range(2)]
    p5.ksb = [sc.enter_context(nc.sbuf_tensor(f"p5_ksb{i}", [P, T], BF16)) for i in range(2)]
    p5.vsb = [sc.enter_context(nc.sbuf_tensor(f"p5_vsb{i}", [P, 16 * P], BF16)) for i in range(2)]
    p5.gsb = sc.enter_context(nc.sbuf_tensor("p5_gsb", [P, 16 * 32], F32))
    p5.gz = sc.enter_context(nc.sbuf_tensor("p5_gz", [32, 1], F32))
    p5.grow = [sc.enter_context(nc.sbuf_tensor(f"p5_grow{i}", [1, T], F32)) for i in range(2)]
    p5.gq16 = sc.enter_context(nc.sbuf_tensor("p5_gq16", [P, 16], F32))
    p5.gkbc = sc.enter_context(nc.sbuf_tensor("p5_gkbc", [P, T], F32))
    p5.esb = [sc.enter_context(nc.sbuf_tensor(f"p5_esb{i}", [P, 512], BF16)) for i in range(2)]
    p5.ptsb = [sc.enter_context(nc.sbuf_tensor(f"p5_ptsb{i}", [P, 512], BF16)) for i in range(2)]
    p5.osb = [sc.enter_context(nc.sbuf_tensor(f"p5_osb{i}", [HD5, T], BF16)) for i in range(2)]
    _scg = contextlib.ExitStack()
    p5.gf = _scg.enter_context(nc.sbuf_tensor("p5_gf", [32, T], F32))
    p5.gcum = _scg.enter_context(nc.sbuf_tensor("p5_gcum", [32, T], F32))
    _scg.close()
    p5.asb = [sc.enter_context(nc.sbuf_tensor(f"p5_asb{i}", [P, 4 * T], BF16)) for i in range(2)]
    p5.ps_s = [sc.enter_context(nc.psum_tensor(f"p5_ps_s{i}", [P, 512], F32)) for i in range(4)]
    p5.pt_ps = [sc.enter_context(nc.psum_tensor(f"p5_pt_ps{i}", [P, 1024], BF16)) for i in range(2)]
    p5.o_ps = sc.enter_context(nc.psum_tensor("p5_o_ps", [P, 512], F32))
    p5.pbc = sc.enter_context(nc.psum_tensor("p5_pbc", [P, 512], F32))
    for nm in ["qld", "gsc", "gtr", "gqt", "gqe", "bcs", "bce", "ssem", "esem",
               "asem", "tsem", "ptsem", "avsem", "osem", "gst", "vsf", "gqs"]:
        setattr(p5, nm, E(nc.semaphore(f"p5_{nm}")))
    _s = E(nc.semaphore("p5_gld")); p5.gld = [_s, _s]
    _s = E(nc.semaphore("p5_sto")); p5.sto = [_s, _s]
    p5.sched = attn_sched(16)
    sc.close()
    sc = contextlib.ExitStack()
    p6 = NS()
    NHC = 44
    p6.sgc = sgc_p
    p6.suc = suc_p
    p6.sdc = sdc_p
    p6.hvar = sc.enter_context(nc.sbuf_tensor("p6_hvar", [P, 32], F32))
    p6.rsrow = sc.enter_context(nc.sbuf_tensor("p6_rsrow", [1, TOK], BF16))
    p6.csrow = sc.enter_context(nc.sbuf_tensor("p6_csrow", [1, TOK], BF16))
    scb = contextlib.ExitStack()
    p6.ogf = scb.enter_context(nc.sbuf_tensor("p6_ogf", [P, NCH * TOK], BF16))
    p6.sg_sb = scb.enter_context(nc.sbuf_tensor("p6_sg_sb", [P, NTT * D], BF16))
    p6.otb = [scb.enter_context(nc.sbuf_tensor(f"p6_otb{i}", [P, D], BF16)) for i in range(2)]
    p6.wb = [scb.enter_context(nc.sbuf_tensor(f"p6_wb{i}", [P, NCH * 256], BF16)) for i in range(2)]
    scb.close()
    p6.a2sb = sc.enter_context(nc.sbuf_tensor("p6_a2sb", [P, NHC * 512], BF16))
    p6.wqg_b = [sc.enter_context(nc.sbuf_tensor(f"p6_wqg_b{i}", [P, NCH * P], BF16)) for i in range(2)]
    p6.wqu_b = [sc.enter_context(nc.sbuf_tensor(f"p6_wqu_b{i}", [P, NCH * P], BF16)) for i in range(2)]
    p6.wqd_b = [sc.enter_context(nc.sbuf_tensor(f"p6_wqd_b{i}", [P, NHC * P], BF16)) for i in range(2)]
    p6.sgh = [sc.enter_context(nc.sbuf_tensor(f"p6_sgh{i}", [P, 512], BF16)) for i in range(2)]
    p6.yfb = [sc.enter_context(nc.sbuf_tensor(f"p6_yfb{i}", [P, 512], BF16)) for i in range(2)]
    p6.ps = [sc.enter_context(nc.psum_tensor(f"p6_ps{i}", [P, 512], F32)) for i in range(2)]
    p6.pu = [sc.enter_context(nc.psum_tensor(f"p6_pu{i}", [P, 512], F32)) for i in range(2)]
    p6.ptrb = [sc.enter_context(nc.psum_tensor(f"p6_ptr{i}", [P, 1024], BF16)) for i in range(2)]
    p6.prow = sc.enter_context(nc.psum_tensor("p6_prow", [P, 512], F32))
    for nm in ["ldm", "mm", "evs", "lnv", "lns", "onr", "sil", "trs", "xfs",
               "rsm", "rse", "gup", "a2s", "dmm", "dse", "dtr", "dev", "sto"]:
        setattr(p6, nm, E(nc.semaphore(f"p6_{nm}")))
    _s = E(nc.semaphore("p6_ldo")); p6.ldo = [_s, _s]
    _s = E(nc.semaphore("p6_ldw")); p6.ldw = [_s, _s]
    _s = E(nc.semaphore("p6_ldq")); p6.ldq = [_s, _s]
    _s = E(nc.semaphore("p6_ldd")); p6.ldd = [_s, _s]
    sc.close()

    NGP6 = 64
    NWO6 = 64

    def mm_gu3(gidx):
        return 40 + gidx + 1

    NHC = 44
    with nc.Block() as block:

        @block.sync
        def _(sy):
            sy.dma_start(out=isb[:, :], in_=coreinfo[:, :]).then_inc(cst, 16)
            sy.dma_start(out=i32sb[:, :], in_=id32_in[:, :]).then_inc(cst, 16)
            sy.dma_start(out=ibfsb[:, :], in_=idbf_in[:, :]).then_inc(cst, 16)
            sy.dma_start(out=mnegsb[:, :], in_=maskneg_in[:, :]).then_inc(cst, 16)
            sy.dma_start(out=mbfsb[:, :], in_=maskbf_in[:, :]).then_inc(cst, 16)
            sy.dma_start(out=stream[:, :].rearrange("p (tt d) -> p tt d", tt=NTT),
                         in_=x_t_in.ap().rearrange("(tt p) d -> p tt d", p=P)
                         ).then_inc(p3.ldx, 16)
            sy.dma_start(out=p6.sgc[:, :], in_=sg2d[:, :]).then_inc(p6.ldm, 16)
            sy.dma_start(out=p6.suc[:, :], in_=su2d[:, :]).then_inc(p6.ldm, 16)
            sy.dma_start(out=p6.sdc[:, :], in_=sd2d[:, :]).then_inc(p6.ldm, 16)
            sy.wait_ge(cst, 81)
            rhb0 = sy.alloc_register("rhb0")
            rhb1 = sy.alloc_register("rhb1")
            rtok = sy.alloc_register("rtok")
            sy.load(rhb0, isb[0:1, 0:1])
            sy.load(rhb1, isb[0:1, 1:2])
            sy.load(rtok, isb[0:1, 2:3])
            vhb0 = sy.snap(rhb0, min_val=0, max_val=1024)
            vhb1 = sy.snap(rhb1, min_val=0, max_val=1024)
            vtok = sy.snap(rtok, min_val=0, max_val=1024)

            A0r = A0own.ap().rearrange("(tt p) o -> p tt o", p=P)
            for ob in range(12):
                sy.wait_ge(p1.evs, 8 * (ob + 1))
                if ob >= 1:
                    sy.wait_ge(p1.so[0], 16 * ob)
                sy.dma_start(out=A0r[:, :, ob * 512:(ob + 1) * 512],
                             in_=p1.ev[ob % 2][:, :].rearrange("p (tt o) -> p tt o", tt=NTT)
                             ).then_inc(p1.so[0], 16)

            sy.wait_ge(cc0, 1)
            sy.dma_start(out=A0loc[:, 0:TOK],
                         in_=A0o[:, bass.ds(vhb0, TOK)]).then_inc(cc0, 16)
            sy.dma_start(out=A0loc[:, TOK:2 * TOK],
                         in_=A0o[:, bass.ds(vhb0 + 2048, TOK)]).then_inc(cc0, 16)
            sy.dma_start(out=A0loc[:, 2 * TOK:3 * TOK],
                         in_=A0o[:, bass.ds(vhb0 + 4096, TOK)]).then_inc(cc0, 16)
            for h in range(8):
                if h >= 1:
                    sy.wait_ge(p2.qld, 48 * h)
                if h >= 2:
                    sy.wait_ge(p2.avsem, p2.sched[(h - 2) * 4 + 3]["kt1"])
                else:
                    sy.wait_ge(cc0, 49)
                sy.dma_start(out=p2.qsb[h % 2][:, :],
                             in_=A0loc[:, h * P:(h + 1) * P],
                             transpose=True).then_inc(p2.qld, 16)
                sy.dma_start(out=p2.ksb[h % 2][:, :],
                             in_=A0loc[:, TOK + h * P:TOK + (h + 1) * P],
                             transpose=True).then_inc(p2.qld, 16)
                sy.dma_start(out=p2.vsb[h % 2][:, :].rearrange("p (kt d) -> p kt d", kt=16),
                             in_=A0loc.ap().rearrange("(kt p) o -> p kt o", p=P)
                             [:, :, 2 * TOK + h * P:2 * TOK + (h + 1) * P]
                             ).then_inc(p2.qld, 16)

            sy.wait_ge(cc1, 1)
            sy.dma_start(out=p3.ofb[:, :].rearrange("p (c t) -> p c t", c=NCH),
                         in_=B0o.ap().rearrange("(c p) t -> p c t", p=P)
                         [:, :, bass.ds(vtok, TOK)]).then_inc(p3.ldx, 16)

            A1r = A1own.ap().rearrange("(tt p) o -> p tt o", p=P)
            for mi in range(3):
                if mi < 2:
                    sy.wait_ge(p4.vch, 48 * (mi + 1))
                else:
                    sy.wait_ge(p4.esem, 96)
                if mi >= 1:
                    sy.wait_ge(p4.sto, 16 * mi)
                sy.dma_start(out=A1r[:, :, mi * D:(mi + 1) * D],
                             in_=p4.stage[:, :].rearrange("p (tt o) -> p tt o", tt=NTT)
                             ).then_inc(p4.sto, 16)
            sy.wait_ge(p4.gsig, 16)
            sy.dma_start(out=Agown.ap().rearrange("(tt p) e -> p tt e", p=P),
                         in_=p4.gks[:, :].rearrange("p (tt e) -> p tt e", tt=NTT)
                         ).then_inc(p4.sto, 16)

            sy.wait_ge(cc2, 2)
            sy.dma_start(out=A1loc[:, 0:TOK],
                         in_=A1o[:, bass.ds(vhb1, TOK)]).then_inc(cc2, 16)
            sy.dma_start(out=A1loc[:, TOK:2 * TOK],
                         in_=A1o[:, bass.ds(vhb1 + 2048, TOK)]).then_inc(cc2, 16)
            sy.dma_start(out=A1loc[:, 2 * TOK:3 * TOK],
                         in_=A1o[:, bass.ds(vhb1 + 4096, TOK)]).then_inc(cc2, 16)
            for prr in range(8):
                if prr >= 1:
                    sy.wait_ge(p5.qld, 48 * prr)
                if prr >= 2:
                    sy.wait_ge(p5.avsem, p5.sched[(2 * (prr - 2) + 1) * 4 + 3]["kt1"])
                else:
                    sy.wait_ge(cc2, 50)
                sy.dma_start(out=p5.qsb[prr % 2][:, :],
                             in_=A1loc[:, prr * P:(prr + 1) * P],
                             transpose=True).then_inc(p5.qld, 16)
                sy.dma_start(out=p5.ksb[prr % 2][:, :],
                             in_=A1loc[:, TOK + prr * P:TOK + (prr + 1) * P],
                             transpose=True).then_inc(p5.qld, 16)
                sy.dma_start(out=p5.vsb[prr % 2][:, :].rearrange("p (kt d) -> p kt d", kt=16),
                             in_=A1loc.ap().rearrange("(kt p) o -> p kt o", p=P)
                             [:, :, 2 * TOK + prr * P:2 * TOK + (prr + 1) * P]
                             ).then_inc(p5.qld, 16)

            sy.wait_ge(cc3, 1)
            sy.dma_start(out=B1loc[:, :],
                         in_=B1o[:, bass.ds(vtok, TOK)]).then_inc(cc3, 16)
            for tt in range(NTT):
                if tt >= 1:
                    sy.wait_ge(p6.ldo[0], 16 * tt)
                if tt >= 2:
                    sy.wait_ge(p6.onr, 5 * (tt - 1))
                else:
                    sy.wait_ge(cc3, 17)
                sy.dma_start(out=p6.otb[tt % 2][:, :],
                             in_=B1loc[:, tt * P:(tt + 1) * P],
                             transpose=True).then_inc(p6.ldo[0], 16)
            for tt in range(NTT):
                sy.wait_ge(p6.lnv, 56 + 7 * (tt + 1))
                sy.dma_start(out=y_out.ap().rearrange("(tt p) d -> p tt d", p=P)[:, tt, :],
                             in_=stream[:, tt * D:(tt + 1) * D]).then_inc(p6.sto, 16)
            sy.wait_ge(p6.sto, 16 * NTT)

        @block.gpsimd
        def _(g):
            g.memset(ones1[:, :], 1.0)
            g.memset(onec32[:, :], 1.0)
            g.memset(nones[:, :], -128.0)
            g.memset(epsln[:, :], EPS_LN)
            g.memset(epsrm[:, :], EPS_RMS)
            g.memset(p5.gz[:, :], 0.0)
            g.memset(shcol[:, :], -16.0).then_inc(cst, 1)
            g.dma_start(out=x_f[:, :].rearrange("p (c t) -> p c t", c=NCH),
                        in_=x_f_in.ap().rearrange("(c p) t -> p c t", p=P)
                        ).then_inc(p1.ldx, 16)
            g.dma_start(out=p1.brow[:, :], in_=b_attn.ap().unsqueeze(0)).then_inc(p1.ldx, 16)
            for ob in range(12):
                if ob >= 1:
                    g.wait_ge(p1.ldw[0], 16 * ob)
                if ob >= 2:
                    g.wait_ge(p1.mm, 8 * (ob - 1))
                g.dma_start(out=p1.wblk[ob % 2][:, :].rearrange("p (c o) -> p c o", c=NCH),
                            in_=w_attn.ap().rearrange("(c p) o -> p c o", p=P)
                            [:, :, ob * 512:(ob + 1) * 512]).then_inc(p1.ldw[0], 16)
            g.wait_ge(p1.so[0], 16 * 12)
            g.collective_compute("AllGather", ALU.bypass, replica_groups=RG,
                                 ins=[A0own.ap().opt()], outs=[A0o.ap().opt()]
                                 ).then_inc(cc0, 1)
            for h in range(8):
                g.wait_ge(p2.osem, 4 * (h + 1))
                if h >= 1:
                    g.wait_ge(p2.sto[0], 16 * h)
                g.dma_start(out=B0own[h * P:(h + 1) * P, :],
                            in_=p2.osb[h % 2][:, :]).then_inc(p2.sto[0], 16)
            g.wait_ge(p2.sto[0], 16 * 8)
            g.collective_compute("AllGather", ALU.bypass, replica_groups=RG,
                                 ins=[B0own.ap().opt()], outs=[B0o.ap().opt()]
                                 ).then_inc(cc1, 1)
            g.wait_ge(cc1, 1)
            g.dma_start(out=p3.bprow[:, :], in_=b_proj.ap().unsqueeze(0)).then_inc(p3.ldc, 16)
            g.dma_start(out=p3.gwsb[:, :].rearrange("p (c e) -> p c e", c=NCH),
                        in_=gate_w.ap().rearrange("(c p) e -> p c e", p=P)).then_inc(p3.ldc, 16)
            for ob in range(4):
                if ob >= 1:
                    g.wait_ge(p3.ldw[0], 16 * ob)
                if ob >= 2:
                    g.wait_ge(p3.mm, 8 * (ob - 1))
                g.dma_start(out=p3.wpb[ob % 2][:, :].rearrange("p (c o) -> p c o", c=NCH),
                            in_=w_proj.ap().rearrange("(c p) o -> p c o", p=P)
                            [:, :, ob * 512:(ob + 1) * 512]).then_inc(p3.ldw[0], 16)
            g.wait_ge(p3.mm, 32)
            for e in range(8):
                for piece, (foff, fw) in enumerate(PIECES):
                    pig = e * 2 + piece
                    par = pig % 2
                    if pig >= 1:
                        g.wait_ge(p3.lde[0], 32 * pig)
                    for (mat, slot) in [(w1, 2 * par), (w3, 2 * par + 1)]:
                        if pig >= 2:
                            pe_, pp_ = divmod(pig - 2, 2)
                            g.wait_ge(p3.mm, mm_gu3(pe_ * 16 + pp_ * 8 + 7))
                        g.dma_start(
                            out=p3.w13b[slot][:, 0:NCH * fw].rearrange(
                                "p (c f) -> p c f", c=NCH),
                            in_=mat[e, :, foff:foff + fw].rearrange("(c p) f -> p c f", p=P)
                        ).then_inc(p3.lde[0], 16)
                for db in range(4):
                    j = e * 4 + db
                    if j >= 1:
                        g.wait_ge(p3.ld2[0], 16 * j)
                    if j >= 2:
                        g.wait_ge(p3.dmm, 48 * (j - 1))
                    g.dma_start(
                        out=p3.w2b[j % 2][:, :].rearrange("p (c d) -> p c d", c=6),
                        in_=w2[e, :, db * 512:(db + 1) * 512].rearrange("(c p) d -> p c d", p=P)
                    ).then_inc(p3.ld2[0], 16)
            g.wait_ge(p3.lnv, 112)
            g.dma_start(out=p4.wgk[:, :].rearrange("p (c e) -> p c e", c=NCH),
                        in_=WgkT.ap().rearrange("(c p) e -> p c e", p=P)).then_inc(p4.ldc, 16)
            g.dma_start(out=p4.cosb[:, :].rearrange("p (tt f) -> p tt f", tt=NTT),
                        in_=rope_cos.ap().rearrange("(tt p) f -> p tt f", p=P)
                        ).then_inc(p4.ldc, 16)
            g.dma_start(out=p4.sinb[:, :].rearrange("p (tt f) -> p tt f", tt=NTT),
                        in_=rope_sin.ap().rearrange("(tt p) f -> p tt f", p=P)
                        ).then_inc(p4.ldc, 16)
            bi = 0
            for mi, mat in enumerate([WqT, WkT, WvT]):
                for ob in range(4):
                    if bi >= 1:
                        g.wait_ge(p4.ldw[0], 16 * bi)
                    if bi >= 2:
                        g.wait_ge(p4.mm, 8 * (bi - 1))
                    g.dma_start(out=p4.wblk[bi % 2][:, :].rearrange("p (c o) -> p c o", c=NCH),
                                in_=mat.ap().rearrange("(c p) o -> p c o", p=P)
                                [:, :, ob * 512:(ob + 1) * 512]).then_inc(p4.ldw[0], 16)
                    bi += 1
            g.wait_ge(p4.sto, 64)
            g.collective_compute("AllGather", ALU.bypass, replica_groups=RG,
                                 ins=[A1own.ap().opt()], outs=[A1o.ap().opt()]
                                 ).then_inc(cc2, 1)
            g.collective_compute("AllGather", ALU.bypass, replica_groups=RG,
                                 ins=[Agown.ap().opt()], outs=[Ago.ap().opt()]
                                 ).then_inc(cc2, 1)
            g.wait_ge(cc2, 2)
            g.dma_start(out=p5.gsb[:, :].rearrange("p (tt h) -> p tt h", tt=16),
                        in_=Ago.ap().rearrange("(tt p) h -> p tt h", p=P)
                        ).then_inc(p5.gqs, 16)
            g.wait_ge(p5.gsc, 1)
            g.dma_start(out=gdram[:, :], in_=p5.gcum[:, :]).then_inc(p5.gst, 16)
            g.wait_ge(p5.gst, 16)
            rp5 = g.alloc_register("rp5")
            g.load(rp5, isb[0:1, 3:4])
            offv = g.snap(rp5, min_val=0, max_val=16)
            g.dma_start(out=gloc[:, :],
                        in_=gdram[bass.ds(offv, 16), :]).then_inc(p5.gst, 16)
            g.wait_ge(p5.gst, 32)

            def p5_store(pr):
                g.wait_ge(p5.osem, 4 * (pr + 1))
                if pr >= 1:
                    g.wait_ge(p5.sto[0], 16 * pr)
                g.dma_start(out=B1own[pr * 64:(pr + 1) * 64, :],
                            in_=p5.osb[pr % 2][:, :]).then_inc(p5.sto[0], 16)

            for pr in range(16):
                if pr >= 1:
                    g.wait_ge(p5.gld[0], 16 * pr)
                if pr >= 2:
                    g.wait_ge(p5.bcs, 4 * (pr - 1))
                g.dma_start(out=p5.grow[pr % 2][:, :],
                            in_=gloc[pr:pr + 1, :]).then_inc(p5.gld[0], 16)
                if pr >= 2:
                    p5_store(pr - 2)
            p5_store(14)
            p5_store(15)
            g.wait_ge(p5.sto[0], 16 * 16)
            g.collective_compute("AllGather", ALU.bypass, replica_groups=RG,
                                 ins=[B1own.ap().opt()], outs=[B1o.ap().opt()]
                                 ).then_inc(cc3, 1)
            g.wait_ge(cc3, 1)
            bi = 0
            for mat in (WgT, WoT):
                for ob in range(8):
                    if bi >= 1:
                        g.wait_ge(p6.ldw[0], 16 * bi)
                    if bi >= 2:
                        g.wait_ge(p6.mm, 8 * (bi - 1))
                    g.dma_start(out=p6.wb[bi % 2][:, :].rearrange("p (c o) -> p c o", c=NCH),
                                in_=mat.ap().rearrange("(c p) o -> p c o", p=P)
                                [:, :, ob * 256:(ob + 1) * 256]).then_inc(p6.ldw[0], 16)
                    bi += 1
            qi = 0
            for th in range(2):
                for hc in range(NHC):
                    if qi >= 1:
                        g.wait_ge(p6.ldq[0], 32 * qi)
                    if qi >= 2:
                        g.wait_ge(p6.gup, 2 * (qi - 1))
                    else:
                        g.wait_ge(p6.mm, NGP6 + NWO6)
                    g.dma_start(out=p6.wqg_b[qi % 2][:, :].rearrange("p (c h) -> p c h", c=NCH),
                                in_=wqgT.ap().rearrange("(c p) h -> p c h", p=P)
                                [:, :, hc * P:(hc + 1) * P]).then_inc(p6.ldq[0], 16)
                    g.dma_start(out=p6.wqu_b[qi % 2][:, :].rearrange("p (c h) -> p c h", c=NCH),
                                in_=wquT.ap().rearrange("(c p) h -> p c h", p=P)
                                [:, :, hc * P:(hc + 1) * P]).then_inc(p6.ldq[0], 16)
                    qi += 1
                for dc in range(NCH):
                    di = th * NCH + dc
                    if di >= 1:
                        g.wait_ge(p6.ldd[0], 16 * di)
                    if di >= 2:
                        g.wait_ge(p6.dse, di - 1)
                    else:
                        g.wait_ge(p6.mm, NGP6 + NWO6)
                    g.dma_start(out=p6.wqd_b[di % 2][:, :].rearrange("p (c d) -> p c d", c=NHC),
                                in_=wqdT.ap().rearrange("(c p) d -> p c d", p=P)
                                [:, :, dc * P:(dc + 1) * P]).then_inc(p6.ldd[0], 16)

        @block.tensor
        def _(t):
            t.wait_ge(cst, 81)
            t.wait_ge(p1.ldx, 32)
            gi = 0
            for ob in range(12):
                t.wait_ge(p1.ldw[0], 16 * (ob + 1))
                for tt in range(NTT):
                    pb = p1.ps[gi % 2]
                    if gi >= 2:
                        t.wait_ge(p1.evs, gi - 1)
                    for c in range(NCH):
                        t.matmul(pb[:, :],
                                 x_f[:, c * TOK + tt * P: c * TOK + (tt + 1) * P],
                                 p1.wblk[ob % 2][:, c * 512:(c + 1) * 512],
                                 start=(c == 0), stop=False)
                    t.matmul(pb[:, :], ones1[:, :],
                             p1.brow[0:1, ob * 512:(ob + 1) * 512],
                             start=False, stop=True).then_inc(p1.mm, 1)
                    gi += 1
            def p2_smms(idx):
                h, gq = divmod(idx, 4)
                rec = p2.sched[idx]
                smm = rec["smm0"]
                for i in range(4):
                    qt = 4 * gq + i
                    for (kc, cw) in chunks_of(qt):
                        if smm >= 4:
                            t.wait_ge(p2.esem, smm - 3)
                        t.matmul(p2.ps_s[smm % 4][:, 0:cw],
                                 p2.qsb[h % 2][:, qt * P:(qt + 1) * P],
                                 p2.ksb[h % 2][:, kc * 512:kc * 512 + cw],
                                 start=True, stop=True).then_inc(p2.ssem, 1)
                        smm += 1

            def p2_tav(idx):
                h, gq = divmod(idx, 4)
                rec = p2.sched[idx]
                t.wait_ge(p2.zsem, rec["zsc1"])
                nkt = 4 * gq + 4

                def p2_tr(kt):
                    ktc = rec["kt0"] + kt
                    i0 = max(0, kt - 4 * gq)
                    if ktc >= 2:
                        t.wait_ge(p2.ptsem, ktc - 1)
                    last = None
                    for i in range(i0, 4):
                        last = t.transpose(p2.pt_ps[ktc % 2][:, i * P:(i + 1) * P],
                                           p2.psb[idx % 2][:, i * T + kt * P: i * T + (kt + 1) * P],
                                           ibfsb[:, :])
                    last.then_inc(p2.tsem, 1)

                def p2_av(kt):
                    ktc = rec["kt0"] + kt
                    i0 = max(0, kt - 4 * gq)
                    t.wait_ge(p2.ptsem, ktc + 1)
                    if kt == 0 and rec["oev0"] > 0:
                        t.wait_ge(p2.osem, rec["oev0"])
                    t.matmul(p2.o_ps[:, i0 * P:512],
                             p2.vsb[h % 2][:, kt * P:(kt + 1) * P],
                             p2.ptsb[ktc % 2][:, i0 * P:512],
                             start=(kt == 0), stop=(kt == nkt - 1)
                             ).then_inc(p2.avsem, 1)

                p2_tr(0)
                for kt in range(1, nkt):
                    p2_tr(kt)
                    p2_av(kt - 1)
                p2_av(nkt - 1)

            for idx in range(32):
                if idx % 4 == 0:
                    t.wait_ge(p2.qld, 48 * (idx // 4 + 1))
                p2_smms(idx)
                if idx >= 1:
                    p2_tav(idx - 1)
            p2_tav(31)
            t.wait_ge(p3.ldx, 32)
            t.wait_ge(p3.ldc, 32)
            gi = 0
            for ob in range(4):
                t.wait_ge(p3.ldw[0], 16 * (ob + 1))
                for tt in range(NTT):
                    pb = p3.ps[gi % 2]
                    if gi >= 2:
                        t.wait_ge(p3.evs, gi - 1)
                    for c in range(NCH):
                        t.matmul(pb[:, :],
                                 p3.ofb[:, c * TOK + tt * P: c * TOK + (tt + 1) * P],
                                 p3.wpb[ob % 2][:, c * 512:(c + 1) * 512],
                                 start=(c == 0), stop=False)
                    t.matmul(pb[:, :], ones1[:, :],
                             p3.bprow[0:1, ob * 512:(ob + 1) * 512],
                             start=False, stop=True).then_inc(p3.mm, 1)
                    gi += 1
            ntr = 0
            for tt in range(NTT):
                t.wait_ge(p3.lnv, 7 * (tt + 1))
                for cg in range(4):
                    if ntr >= 2:
                        t.wait_ge(p3.xfs, ntr - 1)
                    last = None
                    for c4 in range(4):
                        c = cg * 4 + c4
                        last = t.transpose(p3.pxf[ntr % 2][:, c4 * P:(c4 + 1) * P],
                                           stream[:, tt * D + c * P: tt * D + (c + 1) * P],
                                           i32sb[:, :])
                    last.then_inc(p3.trs, 1)
                    ntr += 1
            for tt in range(NTT):
                t.wait_ge(p3.xfs, 4 * (tt + 1))
                if tt > 0:
                    t.wait_ge(p3.gsem, tt)
                for c in range(NCH):
                    t.matmul(p3.pg[0][:, 0:8],
                             x_f[:, c * TOK + tt * P: c * TOK + (tt + 1) * P],
                             p3.gwsb[:, c * 8:(c + 1) * 8],
                             start=(c == 0), stop=(c == NCH - 1)
                             ).then_inc(p3.mm, 1 if c == NCH - 1 else 0)
            t.wait_ge(p3.gsem, 8)
            t.wait_ge(p3.evs, 32)
            t.wait_ge(p3.xfs, 32)
            for e in range(8):
                pi = 0
                for piece, (foff, fw) in enumerate(PIECES):
                    pig = e * 2 + piece
                    par = pig % 2
                    t.wait_ge(p3.lde[0], 32 * (pig + 1))
                    for tt in range(NTT):
                        gidx = e * 16 + pi
                        if gidx >= 2:
                            t.wait_ge(p3.sil, gidx - 1)
                            t.wait_ge(p3.a2s, gidx - 1)
                        for c in range(NCH):
                            t.matmul(p3.pg[gidx % 2][:, 0:fw],
                                     x_f[:, c * TOK + tt * P: c * TOK + (tt + 1) * P],
                                     p3.w13b[2 * par][:, c * fw:(c + 1) * fw],
                                     start=(c == 0), stop=(c == NCH - 1))
                        inst = None
                        for c in range(NCH):
                            inst = t.matmul(p3.pu[gidx % 2][:, 0:fw],
                                            x_f[:, c * TOK + tt * P: c * TOK + (tt + 1) * P],
                                            p3.w13b[2 * par + 1][:, c * fw:(c + 1) * fw],
                                            start=(c == 0), stop=(c == NCH - 1))
                        inst.then_inc(p3.mm, 1)
                        t.wait_ge(p3.a2s, gidx + 1)
                        if gidx >= 2:
                            t.wait_ge(p3.a2e, gidx - 1)
                        nt = fw // P
                        last = None
                        for c4 in range(nt):
                            last = t.transpose(p3.ptr[gidx % 2][:, c4 * P:(c4 + 1) * P],
                                               p3.a2t[gidx % 2][:, c4 * P:(c4 + 1) * P],
                                               ibfsb[:, :])
                        last.then_inc(p3.trs, 1)
                        pi += 1
                for db in range(4):
                    j = e * 4 + db
                    t.wait_ge(p3.ld2[0], 16 * (j + 1))
                    for tt in range(NTT):
                        di = e * 32 + db * 8 + tt
                        if di >= 2:
                            t.wait_ge(p3.dev, di - 1)
                        t.wait_ge(p3.a2e, e * 16 + 8 + tt + 1)
                        inst = None
                        for fc in range(6):
                            inst = t.matmul(p3.ps[di % 2][:, :],
                                            p3.a2f[:, tt * 6 * P + fc * P: tt * 6 * P + (fc + 1) * P],
                                            p3.w2b[j % 2][:, fc * 512:(fc + 1) * 512],
                                            start=(fc == 0), stop=(fc == 5))
                        inst.then_inc(p3.dmm, 6)
            t.wait_ge(p3.lnv, 112)
            t.wait_ge(p3.mm, 168)
            ntr = 0
            for tt in range(NTT):
                for cg in range(4):
                    if ntr >= 2:
                        t.wait_ge(p4.xfs, ntr - 1)
                    last = None
                    for c4 in range(4):
                        c = cg * 4 + c4
                        last = t.transpose(p4.ps[ntr % 2][:, c4 * P:(c4 + 1) * P],
                                           stream[:, tt * D + c * P: tt * D + (c + 1) * P],
                                           i32sb[:, :])
                    last.then_inc(p4.trs, 1)
                    ntr += 1
            t.wait_ge(p4.ldc, 48)
            t.wait_ge(p4.xfs, 32)
            gi = 0
            for mi in range(3):
                for ob in range(4):
                    t.wait_ge(p4.ldw[0], 16 * (gi + 1))
                    for tt in range(NTT):
                        grp = gi * 8 + tt
                        pb = p4.ps[grp % 2]
                        if grp >= 2:
                            t.wait_ge(p4.esem, grp - 1)
                        inst = None
                        for c in range(NCH):
                            inst = t.matmul(pb[:, :],
                                            x_f[:, c * TOK + tt * P: c * TOK + (tt + 1) * P],
                                            p4.wblk[gi % 2][:, c * 512:(c + 1) * 512],
                                            start=(c == 0), stop=(c == NCH - 1))
                        inst.then_inc(p4.mm, 1)
                    gi += 1
            for tt in range(NTT):
                if tt > 0:
                    t.wait_ge(p4.gsig, tt)
                inst = None
                for c in range(NCH):
                    inst = t.matmul(p4.pgk[:, :],
                                    x_f[:, c * TOK + tt * P: c * TOK + (tt + 1) * P],
                                    p4.wgk[:, c * 32:(c + 1) * 32],
                                    start=(c == 0), stop=(c == NCH - 1))
                inst.then_inc(p4.mgk, 1)
            t.wait_ge(p4.esem, 96)
            t.wait_ge(p4.gsig, 16)
            t.wait_ge(p5.gqs, 16)
            for kt in range(16):
                if kt >= 1:
                    t.wait_ge(p5.gtr, 2 * kt)
                t.transpose(p5.pbc[0:32, 0:128], p5.gsb[:, kt * 32:(kt + 1) * 32],
                            i32sb[:, :]).then_inc(p5.gtr, 1)
            t.wait_ge(p5.gtr, 32)
            for pr in range(16):
                buf5 = (pr // 2) % 2
                sub5 = (pr % 2) * 64
                t.wait_ge(p5.qld, 48 * (pr // 2 + 1))
                t.wait_ge(p5.gld[0], 16 * (pr + 1))
                if pr >= 1:
                    t.wait_ge(p5.esem, p5.sched[(pr - 1) * 4 + 3]["exp1"])
                    t.wait_ge(p5.bce, 4 * pr)
                last = None
                for j in range(16):
                    last = t.transpose(p5.pbc[:, j:j + 1],
                                       p5.grow[pr % 2][0:1, j * P:(j + 1) * P],
                                       i32sb[0:1, 0:1])
                last.then_inc(p5.gqt, 1)
                t.wait_ge(p5.gqe, pr + 1)
                for j in range(4):
                    if pr > 0 or j > 0:
                        t.wait_ge(p5.bce, 4 * pr + j)
                    t.matmul(p5.pbc[:, :], onec32[:, :],
                             p5.grow[pr % 2][0:1, j * 512:(j + 1) * 512],
                             start=True, stop=True).then_inc(p5.bcs, 1)
                t.wait_ge(p5.bce, 4 * (pr + 1))

                def p5_smms(idx):
                    gq = idx % 4
                    rec = p5.sched[idx]
                    smm = rec["smm0"]
                    for i in range(4):
                        qt = 4 * gq + i
                        for (kc, cw) in chunks_of(qt):
                            if smm >= 4:
                                t.wait_ge(p5.asem, smm - 3)
                            t.matmul(p5.ps_s[smm % 4][:, 0:cw],
                                     p5.qsb[buf5][sub5:sub5 + 64, qt * P:(qt + 1) * P],
                                     p5.ksb[buf5][sub5:sub5 + 64, kc * 512:kc * 512 + cw],
                                     start=True, stop=True).then_inc(p5.ssem, 1)
                            smm += 1

                def p5_tav(idx):
                    gq = idx % 4
                    rec = p5.sched[idx]
                    t.wait_ge(p5.asem, rec["smm1"])
                    nkt = 4 * gq + 4

                    def p5_tr(kt):
                        ktc = rec["kt0"] + kt
                        i0 = max(0, kt - 4 * gq)
                        if ktc >= 2:
                            t.wait_ge(p5.ptsem, ktc - 1)
                        last = None
                        for i in range(i0, 4):
                            last = t.transpose(p5.pt_ps[ktc % 2][:, i * P:(i + 1) * P],
                                               p5.asb[idx % 2][:, i * T + kt * P: i * T + (kt + 1) * P],
                                               ibfsb[:, :])
                        last.then_inc(p5.tsem, 1)

                    def p5_av(kt):
                        ktc = rec["kt0"] + kt
                        i0 = max(0, kt - 4 * gq)
                        t.wait_ge(p5.ptsem, ktc + 1)
                        if kt == 0 and rec["oev0"] > 0:
                            t.wait_ge(p5.osem, rec["oev0"])
                        t.matmul(p5.o_ps[0:64, i0 * P:512],
                                 p5.vsb[buf5][:, kt * P + sub5: kt * P + sub5 + 64],
                                 p5.ptsb[ktc % 2][:, i0 * P:512],
                                 start=(kt == 0), stop=(kt == nkt - 1)
                                 ).then_inc(p5.avsem, 1)

                    p5_tr(0)
                    for kt in range(1, nkt):
                        p5_tr(kt)
                        p5_av(kt - 1)
                    p5_av(nkt - 1)

                for gq in range(4):
                    p5_smms(pr * 4 + gq)
                    if gq >= 1:
                        p5_tav(pr * 4 + gq - 1)
                p5_tav(pr * 4 + 3)
            t.wait_ge(p5.osem, 64)
            gi = 0
            for ob in range(8):
                t.wait_ge(p6.ldw[0], 16 * (ob + 1))
                for tt in range(NTT):
                    pb = p6.ps[gi % 2]
                    if gi >= 2:
                        t.wait_ge(p6.sil, gi - 1)
                    inst = None
                    for c in range(NCH):
                        inst = t.matmul(pb[:, 0:256],
                                        x_f[:, c * TOK + tt * P: c * TOK + (tt + 1) * P],
                                        p6.wb[ob % 2][:, c * 256:(c + 1) * 256],
                                        start=(c == 0), stop=(c == NCH - 1))
                    inst.then_inc(p6.mm, 1)
                    gi += 1
            ntr = 0
            for tt in range(NTT):
                t.wait_ge(p6.onr, 5 * tt + 5)
                for cg in range(4):
                    if ntr >= 2:
                        t.wait_ge(p6.xfs, ntr - 1)
                    last = None
                    for c4 in range(4):
                        c = cg * 4 + c4
                        last = t.transpose(p6.ptrb[ntr % 2][:, c4 * P:(c4 + 1) * P],
                                           p6.sg_sb[:, tt * D + c * P: tt * D + (c + 1) * P],
                                           ibfsb[:, :])
                    last.then_inc(p6.trs, 1)
                    ntr += 1
            gi = 0
            for ob in range(8):
                bi = 8 + ob
                t.wait_ge(p6.ldw[0], 16 * (bi + 1))
                for tt in range(NTT):
                    grp = NGP6 + gi
                    pb = p6.ps[grp % 2]
                    if gi >= 2:
                        t.wait_ge(p6.evs, gi - 1)
                    else:
                        t.wait_ge(p6.sil, NGP6)
                    t.wait_ge(p6.xfs, 4 * (tt + 1))
                    inst = None
                    for c in range(NCH):
                        inst = t.matmul(pb[:, 0:256],
                                        p6.ogf[:, c * TOK + tt * P: c * TOK + (tt + 1) * P],
                                        p6.wb[bi % 2][:, c * 256:(c + 1) * 256],
                                        start=(c == 0), stop=(c == NCH - 1))
                    inst.then_inc(p6.mm, 1)
                    gi += 1
            ntr = 32
            for tt in range(NTT):
                t.wait_ge(p6.lnv, 7 * (tt + 1))
                for cg in range(4):
                    if ntr >= 34:
                        t.wait_ge(p6.xfs, ntr - 1)
                    else:
                        t.wait_ge(p6.evs, NWO6)
                    last = None
                    for c4 in range(4):
                        c = cg * 4 + c4
                        last = t.transpose(p6.ps[ntr % 2][:, c4 * P:(c4 + 1) * P],
                                           stream[:, tt * D + c * P: tt * D + (c + 1) * P],
                                           i32sb[:, :])
                    last.then_inc(p6.trs, 1)
                    ntr += 1
            for th in range(2):
                tsl = slice(th * 512, (th + 1) * 512)
                t.wait_ge(p6.xfs, 64)
                if th == 1:
                    t.wait_ge(p6.rse, 2)
                inst = None
                for c in range(NCH):
                    inst = t.matmul(p6.prow[0:1, :], nones[:, :],
                                    x_f[:, c * TOK + th * 512: c * TOK + (th + 1) * 512],
                                    start=(c == 0), stop=(c == NCH - 1))
                inst.then_inc(p6.rsm, 1)
                t.wait_ge(p6.rse, 2 * th + 1)
                for hc in range(NHC):
                    qi = th * NHC + hc
                    t.wait_ge(p6.ldq[0], 32 * (qi + 1))
                    if qi >= 2:
                        t.wait_ge(p6.gup, 2 * (qi - 1))
                    for c in range(NCH):
                        t.matmul(p6.ps[qi % 2][:, :],
                                 p6.wqg_b[qi % 2][:, c * P:(c + 1) * P],
                                 x_f[:, c * TOK + th * 512: c * TOK + (th + 1) * 512],
                                 start=(c == 0), stop=False)
                    t.matmul(p6.ps[qi % 2][:, :], ones1[:, :], p6.rsrow[0:1, tsl],
                             start=False, stop=True)
                    inst = None
                    for c in range(NCH):
                        t.matmul(p6.pu[qi % 2][:, :],
                                 p6.wqu_b[qi % 2][:, c * P:(c + 1) * P],
                                 x_f[:, c * TOK + th * 512: c * TOK + (th + 1) * 512],
                                 start=(c == 0), stop=False)
                    inst = t.matmul(p6.pu[qi % 2][:, :], ones1[:, :], p6.rsrow[0:1, tsl],
                                    start=False, stop=True)
                    inst.then_inc(p6.mm, 1)
                t.wait_ge(p6.a2s, NHC * (th + 1))
                inst = None
                for hc in range(NHC):
                    inst = t.matmul(p6.prow[0:1, :], nones[:, :],
                                    p6.a2sb[:, hc * 512:(hc + 1) * 512],
                                    start=(hc == 0), stop=(hc == NHC - 1))
                inst.then_inc(p6.rsm, 1)
                t.wait_ge(p6.rse, 2 * th + 2)
                for dc in range(NCH):
                    di = th * NCH + dc
                    t.wait_ge(p6.ldd[0], 16 * (di + 1))
                    if di >= 1:
                        t.wait_ge(p6.dse, di)
                    for hc in range(NHC):
                        t.matmul(p6.pu[1][:, :],
                                 p6.wqd_b[di % 2][:, hc * P:(hc + 1) * P],
                                 p6.a2sb[:, hc * 512:(hc + 1) * 512],
                                 start=(hc == 0), stop=False)
                    inst = t.matmul(p6.pu[1][:, :], ones1[:, :], p6.csrow[0:1, tsl],
                                    start=False, stop=True)
                    inst.then_inc(p6.dmm, 1)
                    t.wait_ge(p6.dse, di + 1)
                    if di >= 2:
                        t.wait_ge(p6.dev, di - 1)
                    last = None
                    for j in range(4):
                        last = t.transpose(p6.ptrb[di % 2][:, j * P:(j + 1) * P],
                                           p6.yfb[di % 2][:, j * P:(j + 1) * P],
                                           ibfsb[:, :])
                    last.then_inc(p6.dtr, 1)

        @block.vector
        def _(v):
            v.wait_ge(cst, 81)
            gi = 0
            for ob in range(12):
                for tt in range(NTT):
                    v.wait_ge(p1.mm, gi + 1)
                    if ob >= 2:
                        v.wait_ge(p1.so[0], 16 * (ob - 1))
                    v.tensor_copy(p1.ev[ob % 2][:, tt * 512:(tt + 1) * 512],
                                  p1.ps[gi % 2][:, :]).then_inc(p1.evs, 1)
                    gi += 1
            nvself = [0]

            def p2_v_softmax(idx):
                h, gq = divmod(idx, 4)
                rec = p2.sched[idx]
                smm = rec["smm0"]
                for i in range(4):
                    qt = 4 * gq + i
                    chs = chunks_of(qt)
                    for (kc, cw) in chs:
                        if kc == len(chs) - 1:
                            v.wait_ge(p2.ssem, smm + 1)
                            v.tensor_tensor(p2.ps_s[smm % 4][:, cw - P:cw],
                                            p2.ps_s[smm % 4][:, cw - P:cw],
                                            mnegsb[:, :], ALU.add).then_inc(p2.msem, 1)
                        smm += 1
                    v.wait_ge(p2.esem, smm)
                    nkc = len(chs)
                    zq = rec["zsc0"] + i
                    if zq > 0:
                        v.wait_ge(p2.zsem, zq)
                    v.reduce_sum(p2.zinv[:, i:i + 1], p2.zp[:, i * 4:i * 4 + nkc],
                                 AX.X).then_inc(p2.vself, 1)
                    nvself[0] += 1
                    v.wait_ge(p2.vself, nvself[0])
                    v.reciprocal(p2.zinv[:, i:i + 1], p2.zinv[:, i:i + 1]
                                 ).then_inc(p2.vself, 1)
                    nvself[0] += 1
                    v.wait_ge(p2.vself, nvself[0])
                    v.tensor_scalar_mul(p2.psb[idx % 2][:, i * T: i * T + (qt + 1) * P],
                                        p2.psb[idx % 2][:, i * T: i * T + (qt + 1) * P],
                                        p2.zinv[:, i:i + 1]).then_inc(p2.zsem, 1)

            def p2_v_evacs(idx):
                h, gq = divmod(idx, 4)
                rec = p2.sched[idx]
                ktc = rec["kt0"]
                for kt in range(4 * gq + 4):
                    i0 = max(0, kt - 4 * gq)
                    v.wait_ge(p2.tsem, ktc + 1)
                    v.tensor_copy(p2.ptsb[ktc % 2][:, i0 * P:512],
                                  p2.pt_ps[ktc % 2][:, i0 * P:512]).then_inc(p2.ptsem, 1)
                    ktc += 1
                v.wait_ge(p2.avsem, rec["kt1"])
                if h >= 2 and gq == 0:
                    v.wait_ge(p2.sto[0], 16 * (h - 1))
                v.tensor_copy(p2.osb[h % 2][:, gq * 512:(gq + 1) * 512],
                              p2.o_ps[:, :]).then_inc(p2.osem, 1)

            for idx in range(32):
                p2_v_softmax(idx)
                if idx >= 1:
                    p2_v_evacs(idx - 1)
            p2_v_evacs(31)
            gi = 0
            for ob in range(4):
                for tt in range(NTT):
                    v.wait_ge(p3.mm, gi + 1)
                    v.tensor_tensor(stream[:, tt * D + ob * 512: tt * D + (ob + 1) * 512],
                                    p3.ps[gi % 2][:, :],
                                    stream[:, tt * D + ob * 512: tt * D + (ob + 1) * 512],
                                    ALU.add).then_inc(p3.evs, 1)
                    gi += 1

            def ln_vector(ph, tt, lnv_base, lns_base):
                sl = slice(tt * D, (tt + 1) * D)
                v.reduce_sum(sums[:, tt:tt + 1], stream[:, sl], AX.X).then_inc(ph.lnv, 1)
                v.wait_ge(ph.lns, lns_base + 2 * tt + 1)
                v.wait_ge(ph.lnv, lnv_base + 7 * tt + 1)
                v.tensor_scalar_mul(mu[:, tt:tt + 1], sums[:, tt:tt + 1], 1.0 / D
                                    ).then_inc(ph.lnv, 1)
                v.wait_ge(ph.lnv, lnv_base + 7 * tt + 2)
                v.tensor_scalar_mul(varv[:, tt:tt + 1], sq[:, tt:tt + 1], 1.0 / D
                                    ).then_inc(ph.lnv, 1)
                v.wait_ge(ph.lnv, lnv_base + 7 * tt + 3)
                v.scalar_tensor_tensor(varv[:, tt:tt + 1], mu[:, tt:tt + 1], mu[:, tt:tt + 1],
                                       varv[:, tt:tt + 1], ALU.mult, ALU.subtract
                                       ).then_inc(ph.lnv, 1)
                v.wait_ge(ph.lns, lns_base + 2 * tt + 2)
                v.reciprocal(rsd[:, tt:tt + 1], rsd[:, tt:tt + 1]).then_inc(ph.lnv, 1)
                v.wait_ge(ph.lnv, lnv_base + 7 * tt + 5)
                v.tensor_scalar(stream[:, sl], stream[:, sl],
                                mu[:, tt:tt + 1], rsd[:, tt:tt + 1],
                                ALU.subtract, ALU.mult).then_inc(ph.lnv, 2)

            v.wait_ge(p3.evs, 32)
            for tt in range(NTT):
                ln_vector(p3, tt, 0, 0)
            NGOP = 9
            for tt in range(NTT):
                g8 = slice(tt * 8, (tt + 1) * 8)
                v.wait_ge(p3.gsem, tt + 1)
                if tt > 0:
                    v.wait_ge(p3.vch, NGOP * tt)
                v.reduce_max(p3.t2a[:, :], p3.gexp[:, g8], AX.X).then_inc(p3.vch, 1)
                v.wait_ge(p3.vch, NGOP * tt + 1)
                v.tensor_scalar(p3.e8a[:, :], p3.gexp[:, g8], p3.t2a[:, 0:1], None,
                                ALU.is_equal).then_inc(p3.vch, 1)
                v.wait_ge(p3.vch, NGOP * tt + 2)
                v.scalar_tensor_tensor(p3.e8a[:, :], p3.e8a[:, :], -1e30, p3.gexp[:, g8],
                                       ALU.mult, ALU.add).then_inc(p3.vch, 1)
                v.wait_ge(p3.vch, NGOP * tt + 3)
                v.reduce_max(p3.t2b[:, :], p3.e8a[:, :], AX.X).then_inc(p3.vch, 1)
                v.wait_ge(p3.vch, NGOP * tt + 4)
                v.tensor_scalar(p3.e8a[:, :], p3.gexp[:, g8], p3.t2b[:, 0:1], None,
                                ALU.is_ge).then_inc(p3.vch, 1)
                v.wait_ge(p3.vch, NGOP * tt + 5)
                v.tensor_tensor(p3.e8a[:, :], p3.e8a[:, :], p3.gexp[:, g8],
                                ALU.mult).then_inc(p3.vch, 1)
                v.tensor_tensor(p3.t2a[:, :], p3.t2a[:, :], p3.t2b[:, :],
                                ALU.add).then_inc(p3.vch, 1)
                v.wait_ge(p3.vch, NGOP * tt + 7)
                v.reciprocal(p3.t2a[:, :], p3.t2a[:, :]).then_inc(p3.vch, 1)
                v.wait_ge(p3.vch, NGOP * tt + 8)
                v.tensor_scalar_mul(p3.comb[:, g8], p3.e8a[:, :], p3.t2a[:, 0:1]
                                    ).then_inc(p3.vch, 1)
            v.wait_ge(p3.vch, NGOP * NTT)
            for e in range(8):
                pi = 0
                for (foff, fw) in PIECES:
                    for tt in range(NTT):
                        gidx = e * 16 + pi
                        v.wait_ge(p3.sil, gidx + 1)
                        if gidx >= 2:
                            v.wait_ge(p3.trs, 32 + gidx - 1)
                        v.scalar_tensor_tensor(p3.a2t[gidx % 2][:, 0:fw],
                                               p3.pu[gidx % 2][:, 0:fw],
                                               p3.comb[:, tt * 8 + e: tt * 8 + e + 1],
                                               p3.gs[gidx % 2][:, 0:fw],
                                               ALU.mult, ALU.mult).then_inc(p3.a2s, 1)
                        pi += 1
                for db in range(4):
                    for tt in range(NTT):
                        di = e * 32 + db * 8 + tt
                        v.wait_ge(p3.dmm, 6 * (di + 1))
                        v.tensor_tensor(stream[:, tt * D + db * 512: tt * D + (db + 1) * 512],
                                        p3.ps[di % 2][:, :],
                                        stream[:, tt * D + db * 512: tt * D + (db + 1) * 512],
                                        ALU.add).then_inc(p3.dev, 1)
            v.wait_ge(p3.dev, 256)
            for tt in range(NTT):
                ln_vector(p3, tt, 56, 16)
            nv4 = [0]
            v.wait_ge(p4.ldc, 48)
            for mi in range(2):
                for tt in range(NTT):
                    v.wait_ge(p4.esem, mi * 32 + 24 + tt + 1)
                    v.wait_ge(p4.vch, nv4[0])
                    st3 = p4.stage[:, tt * D:(tt + 1) * D].rearrange("p (h f) -> p h f", h=32)
                    x1 = st3[:, :, 0:32]
                    x2v = st3[:, :, 32:64]
                    ccb = p4.cosb[:, tt * 64: tt * 64 + 32].unsqueeze(1).broadcast_to([P, 32, 32])
                    ssb = p4.sinb[:, tt * 64: tt * 64 + 32].unsqueeze(1).broadcast_to([P, 32, 32])
                    t3 = [p4.tmp[i][:, :].rearrange("p (h f) -> p h f", h=32) for i in range(4)]
                    v.tensor_tensor(t3[0], x1, ccb, ALU.mult).then_inc(p4.vch, 1); nv4[0] += 1
                    v.tensor_tensor(t3[1], x2v, ssb, ALU.mult).then_inc(p4.vch, 1); nv4[0] += 1
                    v.tensor_tensor(t3[2], x1, ssb, ALU.mult).then_inc(p4.vch, 1); nv4[0] += 1
                    v.tensor_tensor(t3[3], x2v, ccb, ALU.mult).then_inc(p4.vch, 1); nv4[0] += 1
                    v.wait_ge(p4.vch, nv4[0])
                    v.tensor_tensor(x1, t3[0], t3[1], ALU.subtract).then_inc(p4.vch, 1); nv4[0] += 1
                    v.tensor_tensor(x2v, t3[2], t3[3], ALU.add).then_inc(p4.vch, 1); nv4[0] += 1
            v.wait_ge(p5.gtr, 32)
            v.tensor_tensor_scan(p5.gcum[:, :], p5.gf[:, :], p5.gf[:, :], 0.0,
                                 ALU.add, ALU.bypass).then_inc(p5.gsc, 1)
            nv5 = [0]
            v.wait_ge(p5.gst, 16)

            def p5_v_amul(idx):
                gq = idx % 4
                rec = p5.sched[idx]
                smm = rec["smm0"]
                if idx >= 2:
                    v.wait_ge(p5.tsem, p5.sched[idx - 2]["kt1"])
                for i in range(4):
                    qt = 4 * gq + i
                    chs = chunks_of(qt)
                    for (kc, cw) in chs:
                        isdiag = (kc == len(chs) - 1)
                        v.wait_ge(p5.esem, smm + 1)
                        if isdiag:
                            v.tensor_tensor(p5.esb[smm % 2][:, cw - P:cw],
                                            p5.esb[smm % 2][:, cw - P:cw],
                                            mbfsb[:, :], ALU.mult).then_inc(p5.vsf, 1)
                            nv5[0] += 1
                            v.wait_ge(p5.vsf, nv5[0])
                        v.wait_ge(p5.ssem, smm + 1)
                        v.scalar_tensor_tensor(
                            p5.asb[idx % 2][:, i * T + kc * 512: i * T + kc * 512 + cw],
                            p5.ps_s[smm % 4][:, 0:cw], 0.125,
                            p5.esb[smm % 2][:, 0:cw],
                            ALU.mult, ALU.mult).then_inc(p5.asem, 1)
                        smm += 1

            def p5_v_evacs(idx):
                pr, gq = divmod(idx, 4)
                rec = p5.sched[idx]
                ktc = rec["kt0"]
                for kt in range(4 * gq + 4):
                    i0 = max(0, kt - 4 * gq)
                    v.wait_ge(p5.tsem, ktc + 1)
                    v.tensor_copy(p5.ptsb[ktc % 2][:, i0 * P:512],
                                  p5.pt_ps[ktc % 2][:, i0 * P:512]).then_inc(p5.ptsem, 1)
                    ktc += 1
                v.wait_ge(p5.avsem, rec["kt1"])
                if pr >= 2 and gq == 0:
                    v.wait_ge(p5.sto[0], 16 * (pr - 1))
                v.tensor_copy(p5.osb[pr % 2][:, gq * 512:(gq + 1) * 512],
                              p5.o_ps[0:64, :]).then_inc(p5.osem, 1)

            for idx5 in range(64):
                if idx5 >= 1 and idx5 % 4 == 0:
                    p5_v_evacs(idx5 - 1)
                p5_v_amul(idx5)
                if idx5 >= 1 and idx5 % 4 != 0:
                    p5_v_evacs(idx5 - 1)
            p5_v_evacs(63)

            v.wait_ge(p3.lnv, 112)
            for tt in range(NTT):
                v.wait_ge(p6.ldo[0], 16 * (tt + 1))
                v.wait_ge(p6.onr, 5 * tt)
                v.wait_ge(p6.lns, 2 * tt + 1)
                o3 = p6.otb[tt % 2][:, :].rearrange("p (h f) -> p h f", h=32)
                v.reduce_sum(p6.hvar[:, :].unsqueeze(2),
                             scr[:, :].rearrange("p (h f) -> p h f", h=32),
                             AX.X).then_inc(p6.onr, 1)
                v.wait_ge(p6.onr, 5 * tt + 1)
                v.wait_ge(p6.lns, 2 * tt + 2)
                v.reciprocal(p6.hvar[:, :], p6.hvar[:, :]).then_inc(p6.onr, 1)
                v.wait_ge(p6.onr, 5 * tt + 2)
                v.tensor_tensor(p6.otb[tt % 2][:, :].rearrange("p (h f) -> p h f", h=32),
                                o3, p6.hvar[:, :].unsqueeze(2).broadcast_to([P, 32, 64]),
                                ALU.mult).then_inc(p6.onr, 1)
                v.wait_ge(p6.onr, 5 * tt + 3)
                v.wait_ge(p6.sil, NGP6)
                v.tensor_tensor(p6.sg_sb[:, tt * D:(tt + 1) * D],
                                p6.sg_sb[:, tt * D:(tt + 1) * D],
                                p6.otb[tt % 2][:, :], ALU.mult).then_inc(p6.onr, 2)
            gi = 0
            for ob in range(8):
                for tt in range(NTT):
                    grp = NGP6 + gi
                    v.wait_ge(p6.mm, grp + 1)
                    v.tensor_tensor(stream[:, tt * D + ob * 256: tt * D + (ob + 1) * 256],
                                    p6.ps[grp % 2][:, 0:256],
                                    stream[:, tt * D + ob * 256: tt * D + (ob + 1) * 256],
                                    ALU.add).then_inc(p6.evs, 1)
                    gi += 1
            v.wait_ge(p6.evs, NWO6)
            for tt in range(NTT):
                ln_vector(p6, tt, 0, 16)
            for th in range(2):
                for hc in range(NHC):
                    qi = th * NHC + hc
                    v.wait_ge(p6.gup, 2 * (qi + 1))
                    v.tensor_tensor(p6.a2sb[:, hc * 512:(hc + 1) * 512],
                                    p6.sgh[qi % 2][:, :],
                                    p6.a2sb[:, hc * 512:(hc + 1) * 512],
                                    ALU.mult).then_inc(p6.a2s, 1)
                for dc in range(NCH):
                    di = th * NCH + dc
                    v.wait_ge(p6.dtr, di + 1)
                    for j in range(4):
                        tt = th * 4 + j
                        inst = v.tensor_tensor(
                            stream[:, tt * D + dc * P: tt * D + (dc + 1) * P],
                            p6.ptrb[di % 2][:, j * P:(j + 1) * P],
                            stream[:, tt * D + dc * P: tt * D + (dc + 1) * P],
                            ALU.add)
                        if j == 3:
                            inst.then_inc(p6.dev, 1)
            v.wait_ge(p6.dev, 32)
            for tt in range(NTT):
                ln_vector(p6, tt, 56, 32)

        @block.scalar
        def _(s):
            s.wait_ge(cst, 81)
            SCALE2 = 128.0 ** -0.5
            for h in range(8):
                for gq in range(4):
                    idx = h * 4 + gq
                    rec = p2.sched[idx]
                    if idx > 1:
                        prev = p2.sched[idx - 2]
                        s.wait_ge(p2.tsem, prev["kt1"])
                    if idx > 0:
                        s.wait_ge(p2.zsem, p2.sched[idx - 1]["zsc1"])
                    smm = rec["smm0"]
                    mk = rec["mask0"]
                    for i in range(4):
                        qt = 4 * gq + i
                        chs = chunks_of(qt)
                        for (kc, cw) in chs:
                            if kc == len(chs) - 1:
                                mk += 1
                                s.wait_ge(p2.msem, mk)
                            else:
                                s.wait_ge(p2.ssem, smm + 1)
                            s.activation(p2.psb[idx % 2][:, i * T + kc * 512: i * T + kc * 512 + cw],
                                         p2.ps_s[smm % 4][:, 0:cw], AF.Exp,
                                         bias=shcol[:, :], scale=SCALE2,
                                         accum_out=p2.zp[:, i * 4 + kc:i * 4 + kc + 1]
                                         ).then_inc(p2.esem, 1)
                            smm += 1
            def ln_scalar(ph, tt, lnv_base, lns_base):
                sl = slice(tt * D, (tt + 1) * D)
                s.wait_ge(ph.lns, lns_base + 2 * tt)
                s.activation(scr[:, :], stream[:, sl], AF.Square,
                             accum_out=sq[:, tt:tt + 1]).then_inc(ph.lns, 1)
                s.wait_ge(ph.lnv, lnv_base + 7 * tt + 4)
                s.activation(rsd[:, tt:tt + 1], varv[:, tt:tt + 1], AF.Sqrt,
                             bias=epsln[:, :], scale=-1.0).then_inc(ph.lns, 1)

            s.wait_ge(p3.evs, 32)
            for tt in range(NTT):
                ln_scalar(p3, tt, 0, 0)
            ntr = 0
            for tt in range(NTT):
                for cg in range(4):
                    s.wait_ge(p3.trs, ntr + 1)
                    s.copy(x_f[:, :].rearrange("p (c t) -> p c t", c=NCH)
                           [:, cg * 4:(cg + 1) * 4, tt * P:(tt + 1) * P],
                           p3.pxf[ntr % 2][:, :].rearrange("p (c4 t) -> p c4 t", c4=4)
                           ).then_inc(p3.xfs, 1)
                    ntr += 1
            for tt in range(NTT):
                s.wait_ge(p3.mm, 32 + tt + 1)
                s.activation(p3.gexp[:, tt * 8:(tt + 1) * 8], p3.pg[0][:, 0:8],
                             AF.Exp).then_inc(p3.gsem, 1)
            for e in range(8):
                pi = 0
                for (foff, fw) in PIECES:
                    for tt in range(NTT):
                        gidx = e * 16 + pi
                        s.wait_ge(p3.mm, mm_gu3(gidx))
                        if gidx >= 2:
                            s.wait_ge(p3.a2s, gidx - 1)
                        s.activation(p3.gs[gidx % 2][:, 0:fw], p3.pg[gidx % 2][:, 0:fw],
                                     AF.Silu).then_inc(p3.sil, 1)
                        s.wait_ge(p3.trs, 32 + gidx + 1)
                        if e >= 1 and pi == 0:
                            s.wait_ge(p3.dmm, 192 * e)
                        nt = fw // P
                        s.copy(p3.a2f[:, tt * 6 * P + (foff // P) * P:
                                      tt * 6 * P + (foff // P) * P + nt * P],
                               p3.ptr[gidx % 2][:, 0:nt * P]).then_inc(p3.a2e, 1)
                        pi += 1
            s.wait_ge(p3.dev, 256)
            for tt in range(NTT):
                ln_scalar(p3, tt, 56, 16)
            ntr = 0
            for tt in range(NTT):
                for cg in range(4):
                    s.wait_ge(p4.trs, ntr + 1)
                    s.copy(x_f[:, :].rearrange("p (c t) -> p c t", c=NCH)
                           [:, cg * 4:(cg + 1) * 4, tt * P:(tt + 1) * P],
                           p4.ps[ntr % 2][:, :].rearrange("p (c4 t) -> p c4 t", c4=4)
                           ).then_inc(p4.xfs, 1)
                    ntr += 1
            gi = 0
            for mi in range(3):
                for ob in range(4):
                    for tt in range(NTT):
                        grp = gi * 8 + tt
                        s.wait_ge(p4.mm, grp + 1)
                        if mi > 0 and ob == 0 and tt == 0:
                            s.wait_ge(p4.sto, 16 * mi)
                        s.copy(p4.stage[:, tt * D + ob * 512: tt * D + (ob + 1) * 512],
                               p4.ps[grp % 2][:, :]).then_inc(p4.esem, 1)
                    gi += 1
            for tt in range(NTT):
                s.wait_ge(p4.mgk, tt + 1)
                s.activation(p4.gks[:, tt * 32:(tt + 1) * 32], p4.pgk[:, :],
                             AF.Sigmoid).then_inc(p4.gsig, 1)
            for tt in range(NTT):
                s.wait_ge(p4.gsig, NTT + tt)
                s.activation(p4.gks[:, tt * 32:(tt + 1) * 32],
                             p4.gks[:, tt * 32:(tt + 1) * 32],
                             AF.Ln).then_inc(p4.gsig, 1)
            for kt in range(16):
                s.wait_ge(p5.gtr, 2 * kt + 1)
                s.copy(p5.gf[0:32, kt * P:(kt + 1) * P],
                       p5.pbc[0:32, 0:P]).then_inc(p5.gtr, 1)
            for pr in range(16):
                s.wait_ge(p5.gqt, pr + 1)
                if pr >= 1:
                    s.wait_ge(p5.esem, p5.sched[(pr - 1) * 4 + 3]["exp1"])
                s.activation(p5.gq16[:, :], p5.pbc[:, 0:16], AF.Copy,
                             scale=1.0 / 16.0).then_inc(p5.gqe, 1)
                for j in range(4):
                    s.wait_ge(p5.bcs, 4 * pr + j + 1)
                    s.copy(p5.gkbc[:, j * 512:(j + 1) * 512], p5.pbc[:, :]
                           ).then_inc(p5.bce, 1)
                for gq in range(4):
                    rec = p5.sched[pr * 4 + gq]
                    smm = rec["smm0"]
                    if gq == 0:
                        s.wait_ge(p5.bce, 4 * (pr + 1))
                    for i in range(4):
                        qt = 4 * gq + i
                        for (kc, cw) in chunks_of(qt):
                            if smm >= 2:
                                s.wait_ge(p5.asem, smm - 1)
                            s.activation(p5.esb[smm % 2][:, 0:cw],
                                         p5.gkbc[:, kc * 512:kc * 512 + cw],
                                         AF.Exp, bias=p5.gq16[:, qt:qt + 1],
                                         scale=-1.0 / 16.0).then_inc(p5.esem, 1)
                            smm += 1
            s.wait_ge(p3.lns, 32)
            s.wait_ge(p6.ldm, 48)
            gi = 0
            for ob in range(8):
                for tt in range(NTT):
                    s.wait_ge(p6.mm, gi + 1)
                    s.activation(p6.sg_sb[:, tt * D + ob * 256: tt * D + (ob + 1) * 256],
                                 p6.ps[gi % 2][:, 0:256], AF.Silu).then_inc(p6.sil, 1)
                    gi += 1
            for tt in range(NTT):
                s.wait_ge(p6.ldo[0], 16 * (tt + 1))
                if tt >= 1:
                    s.wait_ge(p6.onr, 5 * (tt - 1) + 1)
                s.activation(scr[:, :], p6.otb[tt % 2][:, :], AF.Square).then_inc(p6.lns, 1)
                s.wait_ge(p6.onr, 5 * tt + 1)
                s.activation(p6.hvar[:, :], p6.hvar[:, :], AF.Sqrt,
                             bias=epsrm[:, :], scale=1.0 / 64.0).then_inc(p6.lns, 1)
            ntr = 0
            for tt in range(NTT):
                for cg in range(4):
                    s.wait_ge(p6.trs, ntr + 1)
                    s.copy(p6.ogf[:, :].rearrange("p (c t) -> p c t", c=NCH)
                           [:, cg * 4:(cg + 1) * 4, tt * P:(tt + 1) * P],
                           p6.ptrb[ntr % 2][:, 0:512].rearrange("p (c4 t) -> p c4 t", c4=4)
                           ).then_inc(p6.xfs, 1)
                    ntr += 1
            s.wait_ge(p6.evs, NWO6)
            for tt in range(NTT):
                ln_scalar(p6, tt, 0, 16)
            for tt in range(NTT):
                for cg in range(4):
                    s.wait_ge(p6.trs, ntr + 1)
                    s.copy(x_f[:, :].rearrange("p (c t) -> p c t", c=NCH)
                           [:, cg * 4:(cg + 1) * 4, tt * P:(tt + 1) * P],
                           p6.ps[ntr % 2][:, :].rearrange("p (c4 t) -> p c4 t", c4=4)
                           ).then_inc(p6.xfs, 1)
                    ntr += 1
            for th in range(2):
                tsl = slice(th * 512, (th + 1) * 512)
                s.wait_ge(p6.rsm, 2 * th + 1)
                s.copy(p6.rsrow[0:1, tsl], p6.prow[0:1, :]).then_inc(p6.rse, 1)
                for hc in range(NHC):
                    qi = th * NHC + hc
                    s.wait_ge(p6.mm, NGP6 + NWO6 + qi + 1)
                    if qi >= 2:
                        s.wait_ge(p6.a2s, qi - 1)
                    s.activation(p6.sgh[qi % 2][:, :], p6.ps[qi % 2][:, :], AF.Silu,
                                 scale=p6.sgc[:, hc:hc + 1]).then_inc(p6.gup, 1)
                    s.activation(p6.a2sb[:, hc * 512:(hc + 1) * 512], p6.pu[qi % 2][:, :],
                                 AF.Copy, scale=p6.suc[:, hc:hc + 1]).then_inc(p6.gup, 1)
                s.wait_ge(p6.rsm, 2 * th + 2)
                s.copy(p6.csrow[0:1, tsl], p6.prow[0:1, :]).then_inc(p6.rse, 1)
                for dc in range(NCH):
                    di = th * NCH + dc
                    s.wait_ge(p6.dmm, di + 1)
                    if di >= 2:
                        s.wait_ge(p6.dtr, di - 1)
                    s.activation(p6.yfb[di % 2][:, :], p6.pu[1][:, :], AF.Copy,
                                 scale=p6.sdc[:, dc:dc + 1]).then_inc(p6.dse, 1)
            s.wait_ge(p6.dev, 32)
            for tt in range(NTT):
                ln_scalar(p6, tt, 56, 32)

    ctx.close()
    return nc


from concourse.bass_utils import run_bass_kernel_spmd
import ml_dtypes

_BF = ml_dtypes.bfloat16
_cc = np.ascontiguousarray


def _planar_perm():
    perm = np.empty(2048, np.int64)
    for h in range(32):
        for i in range(64):
            perm[h * 64 + i] = h * 64 + (2 * i if i < 32 else 2 * (i - 32) + 1)
    return perm


def kernel(L0_attn_x, L0_attn_w_attn, L0_attn_b_attn, L0_attn_w_proj, L0_attn_b_proj,
           L0_ffn_gate_w, L0_ffn_w1, L0_ffn_w2, L0_ffn_w3,
           L1_attn_Wq, L1_attn_Wk, L1_attn_Wv, L1_attn_Wg, L1_attn_Wgk, L1_attn_Wo,
           L1_ffn_wq_gate, L1_ffn_s_gate, L1_ffn_wq_up, L1_ffn_s_up,
           L1_ffn_wq_down, L1_ffn_s_down):
    x = np.asarray(L0_attn_x, np.float32)
    perm = _planar_perm()
    weights = {
        "w_attn": np.asarray(L0_attn_w_attn, np.float32),
        "b_attn": np.asarray(L0_attn_b_attn, np.float32),
        "w_proj": np.asarray(L0_attn_w_proj, np.float32),
        "b_proj": np.asarray(L0_attn_b_proj, np.float32),
        "gate_w": np.asarray(L0_ffn_gate_w, np.float32),
        "w1": np.asarray(L0_ffn_w1, np.float32),
        "w2": np.asarray(L0_ffn_w2, np.float32),
        "w3": np.asarray(L0_ffn_w3, np.float32),
        "WqT": _cc(np.asarray(L1_attn_Wq, np.float32).T[:, perm]),
        "WkT": _cc(np.asarray(L1_attn_Wk, np.float32).T[:, perm]),
        "WvT": _cc(np.asarray(L1_attn_Wv, np.float32).T),
        "WgT": _cc(np.asarray(L1_attn_Wg, np.float32).T),
        "WgkT": _cc(np.asarray(L1_attn_Wgk, np.float32).T),
        "WoT": _cc(np.asarray(L1_attn_Wo, np.float32).T),
        "wqgT": _cc(np.asarray(L1_ffn_wq_gate, np.int32).T),
        "wquT": _cc(np.asarray(L1_ffn_wq_up, np.int32).T),
        "wqdT": _cc(np.asarray(L1_ffn_wq_down, np.int32).T),
        "sg2d": _cc(np.asarray(L1_ffn_s_gate, np.float32).reshape(44, 128).T),
        "su2d": _cc(np.asarray(L1_ffn_s_up, np.float32).reshape(44, 128).T),
        "sd2d": _cc(np.asarray(L1_ffn_s_down, np.float32).reshape(16, 128).T),
    }
    id32 = np.eye(128, dtype=np.float32)
    consts = {
        "id32": id32,
        "idbf": id32.astype(_BF),
        "maskneg": np.triu(np.full((128, 128), -1e30, np.float32), 1),
        "maskbf": np.tril(np.ones((128, 128), np.float32)).astype(_BF),
    }
    inv = 1.0 / 10000.0 ** (np.arange(0, 64, 2) / 64.0)
    in_maps = []
    for c in range(8):
        b, half = c // 2, c % 2
        sl = slice(half * 1024, half * 1024 + 1024)
        pos = np.arange(half * 1024, half * 1024 + 1024, dtype=np.float64)
        fr = pos[:, None] * inv[None, :]
        cosm = np.concatenate([np.cos(fr), np.cos(fr)], 1).astype(np.float32)
        sinm = np.concatenate([np.sin(fr), np.sin(fr)], 1).astype(np.float32)
        m = {"x_f": _cc(x[b, sl].T), "x_t": _cc(x[b, sl]),
             "coreinfo": np.array([[1024 * half, 1024 * half, 1024 * half,
                                    16 * half]], np.int32),
             "rope_cos": cosm, "rope_sin": sinm}
        m.update(weights)
        m.update(consts)
        in_maps.append(m)
    nc = build_fused()
    res = run_bass_kernel_spmd(nc, in_maps, core_ids=list(range(8)))
    out = np.empty((4, 2048, 2048), np.float32)
    for c in range(8):
        b, half = c // 2, c % 2
        out[b, half * 1024:half * 1024 + 1024] = res.results[c]["y"]
    return out


# version 06
# speedup vs baseline: 1.2249x; 1.0832x over previous
import contextlib
from types import SimpleNamespace as NS
import numpy as np
import concourse.bass as bass
import concourse.mybir as mybir

F32 = mybir.dt.float32
BF16 = mybir.dt.bfloat16
I32 = mybir.dt.int32
AF = mybir.ActivationFunctionType
ALU = mybir.AluOpType
AX = mybir.AxisListType

P = 128
D = 2048
NCH = 16
TOK = 1024
NTT = 8
T = 2048
EPS_LN = 1e-5
EPS_RMS = 1e-6
PIECES = [(0, 512), (512, 256)]


def chunks_of(qt):
    klen = (qt + 1) * P
    out = []
    kc = 0
    while kc * 512 < klen:
        out.append((kc, min(512, klen - kc * 512)))
        kc += 1
    return out


def attn_sched(npairs):
    sched = []
    c_smm = c_mask = c_exp = c_zsc = c_kt = c_oev = 0
    for p in range(npairs):
        for g in range(4):
            rec = {"smm0": c_smm, "mask0": c_mask, "exp0": c_exp,
                   "zsc0": c_zsc, "kt0": c_kt, "oev0": c_oev}
            for i in range(4):
                qt = 4 * g + i
                n = len(chunks_of(qt))
                c_smm += n
                c_exp += n
                c_mask += 1
                c_zsc += 1
            c_kt += 4 * g + 4
            c_oev += 1
            rec.update({"smm1": c_smm, "mask1": c_mask, "exp1": c_exp,
                        "zsc1": c_zsc, "kt1": c_kt, "oev1": c_oev})
            sched.append(rec)
    return sched


def build_fused():
    nc = bass.Bass("TRN2", target_bir_lowering=False)

    x_f_in = nc.dram_tensor("x_f", [D, TOK], F32, kind="ExternalInput")
    x_t_in = nc.dram_tensor("x_t", [TOK, D], F32, kind="ExternalInput")
    coreinfo = nc.dram_tensor("coreinfo", [1, 4], I32, kind="ExternalInput")
    w_attn = nc.dram_tensor("w_attn", [D, 3 * D], F32, kind="ExternalInput")
    b_attn = nc.dram_tensor("b_attn", [3 * D], F32, kind="ExternalInput")
    w_proj = nc.dram_tensor("w_proj", [D, D], F32, kind="ExternalInput")
    b_proj = nc.dram_tensor("b_proj", [D], F32, kind="ExternalInput")
    gate_w = nc.dram_tensor("gate_w", [D, 8], F32, kind="ExternalInput")
    w1 = nc.dram_tensor("w1", [8, D, 768], F32, kind="ExternalInput")
    w2 = nc.dram_tensor("w2", [8, 768, D], F32, kind="ExternalInput")
    w3 = nc.dram_tensor("w3", [8, D, 768], F32, kind="ExternalInput")
    WqT = nc.dram_tensor("WqT", [D, D], F32, kind="ExternalInput")
    WkT = nc.dram_tensor("WkT", [D, D], F32, kind="ExternalInput")
    WvT = nc.dram_tensor("WvT", [D, D], F32, kind="ExternalInput")
    WgkT = nc.dram_tensor("WgkT", [D, 32], F32, kind="ExternalInput")
    WgT = nc.dram_tensor("WgT", [D, D], F32, kind="ExternalInput")
    WoT = nc.dram_tensor("WoT", [D, D], F32, kind="ExternalInput")
    wqgT = nc.dram_tensor("wqgT", [D, 5632], I32, kind="ExternalInput")
    wquT = nc.dram_tensor("wquT", [D, 5632], I32, kind="ExternalInput")
    wqdT = nc.dram_tensor("wqdT", [5632, D], I32, kind="ExternalInput")
    sg2d = nc.dram_tensor("sg2d", [P, 44], F32, kind="ExternalInput")
    su2d = nc.dram_tensor("su2d", [P, 44], F32, kind="ExternalInput")
    sd2d = nc.dram_tensor("sd2d", [P, NCH], F32, kind="ExternalInput")
    rope_cos = nc.dram_tensor("rope_cos", [TOK, 64], F32, kind="ExternalInput")
    rope_sin = nc.dram_tensor("rope_sin", [TOK, 64], F32, kind="ExternalInput")
    id32_in = nc.dram_tensor("id32", [P, P], F32, kind="ExternalInput")
    idbf_in = nc.dram_tensor("idbf", [P, P], BF16, kind="ExternalInput")
    maskneg_in = nc.dram_tensor("maskneg", [P, P], F32, kind="ExternalInput")
    maskbf_in = nc.dram_tensor("maskbf", [P, P], BF16, kind="ExternalInput")
    y_out = nc.dram_tensor("y", [TOK, D], F32, kind="ExternalOutput")

    A0own = nc.dram_tensor("A0own", [TOK, 3 * D], BF16)
    A0o = nc.dram_tensor("A0o", [T, 3 * D], BF16)
    B0own = nc.dram_tensor("B0own", [TOK, D], BF16)
    B0o = nc.dram_tensor("B0o", [T, D], BF16)
    A1own = nc.dram_tensor("A1own", [TOK, 3 * D], BF16)
    A1o = nc.dram_tensor("A1o", [T, 3 * D], BF16)
    Agown = nc.dram_tensor("Agown", [TOK, 32], F32)
    Ago = nc.dram_tensor("Ago", [T, 32], F32)
    B1own = nc.dram_tensor("B1own", [TOK, D], BF16)
    B1o = nc.dram_tensor("B1o", [T, D], BF16)
    gdram = nc.dram_tensor("gdram", [32, T], F32)
    A0loc = nc.dram_tensor("A0loc", [T, 3 * TOK], BF16)
    A1loc = nc.dram_tensor("A1loc", [T, 3 * TOK], BF16)
    B1loc = nc.dram_tensor("B1loc", [T, TOK], BF16)
    gloc = nc.dram_tensor("gloc", [16, T], F32)
    RG = [[0, 1], [2, 3], [4, 5], [6, 7]]

    ctx = contextlib.ExitStack()
    E = ctx.enter_context

    stream = E(nc.sbuf_tensor("stream", [P, NTT * D], F32))
    x_f = E(nc.sbuf_tensor("x_fb", [P, NCH * TOK], BF16))
    isb = E(nc.sbuf_tensor("isb", [1, 4], I32))
    i32sb = E(nc.sbuf_tensor("i32sb", [P, P], F32))
    ibfsb = E(nc.sbuf_tensor("ibfsb", [P, P], BF16))
    mnegsb = E(nc.sbuf_tensor("mnegsb", [P, P], F32))
    mbfsb = E(nc.sbuf_tensor("mbfsb", [P, P], BF16))
    ones1 = E(nc.sbuf_tensor("ones1", [1, P], BF16))
    onec32 = E(nc.sbuf_tensor("onec32", [1, P], F32))
    nones = E(nc.sbuf_tensor("nones", [P, 1], BF16))
    epsln = E(nc.sbuf_tensor("epsln", [P, 1], F32))
    epsrm = E(nc.sbuf_tensor("epsrm", [P, 1], F32))
    shcol = E(nc.sbuf_tensor("shcol", [P, 1], F32))
    sq = E(nc.sbuf_tensor("sq", [P, NTT], F32))
    sums = E(nc.sbuf_tensor("sums", [P, NTT], F32))
    mu = E(nc.sbuf_tensor("mu", [P, NTT], F32))
    varv = E(nc.sbuf_tensor("varv", [P, NTT], F32))
    rsd = E(nc.sbuf_tensor("rsd", [P, NTT], F32))
    scr = E(nc.sbuf_tensor("scr", [P, D], F32))
    sgc_p = E(nc.sbuf_tensor("sgc_p", [P, 44], F32))
    suc_p = E(nc.sbuf_tensor("suc_p", [P, 44], F32))
    sdc_p = E(nc.sbuf_tensor("sdc_p", [P, NCH], F32))

    cc0 = E(nc.semaphore("cc0"))
    cc1 = E(nc.semaphore("cc1"))
    cc2 = E(nc.semaphore("cc2"))
    cc3 = E(nc.semaphore("cc3"))
    cst = E(nc.semaphore("cst"))

    sc = contextlib.ExitStack()
    p1 = NS()
    p1.wblk = [sc.enter_context(nc.sbuf_tensor(f"p1_wblk{i}", [P, NCH * 512], BF16)) for i in range(2)]
    p1.ev = [sc.enter_context(nc.sbuf_tensor(f"p1_ev{i}", [P, NTT * 512], BF16)) for i in range(2)]
    p1.brow = sc.enter_context(nc.sbuf_tensor("p1_brow", [1, 3 * D], BF16))
    p1.ps = [sc.enter_context(nc.psum_tensor(f"p1_ps{i}", [P, 512], F32)) for i in range(2)]
    _s = E(nc.semaphore("p1_ldw"))
    p1.ldw = [_s, _s]
    p1.ldx = E(nc.semaphore("p1_ldx"))
    p1.mm = E(nc.semaphore("p1_mm"))
    p1.evs = E(nc.semaphore("p1_evs"))
    _s = E(nc.semaphore("p1_so"))
    p1.so = [_s, _s]
    sc.close()
    sc = contextlib.ExitStack()
    p2 = NS()
    p2.qsb = [sc.enter_context(nc.sbuf_tensor(f"p2_qsb{i}", [P, T], BF16)) for i in range(2)]
    p2.ksb = [sc.enter_context(nc.sbuf_tensor(f"p2_ksb{i}", [P, T], BF16)) for i in range(2)]
    p2.vsb = [sc.enter_context(nc.sbuf_tensor(f"p2_vsb{i}", [P, 16 * P], BF16)) for i in range(2)]
    p2.psb = [sc.enter_context(nc.sbuf_tensor(f"p2_psb{i}", [P, 4 * T], BF16)) for i in range(2)]
    p2.ptsb = [sc.enter_context(nc.sbuf_tensor(f"p2_ptsb{i}", [P, 512], BF16)) for i in range(2)]
    p2.zp = sc.enter_context(nc.sbuf_tensor("p2_zp", [P, 16], F32))
    p2.zinv = sc.enter_context(nc.sbuf_tensor("p2_zinv", [P, 4], F32))
    p2.osb = [sc.enter_context(nc.sbuf_tensor(f"p2_osb{i}", [P, T], BF16)) for i in range(2)]
    p2.ps_s = [sc.enter_context(nc.psum_tensor(f"p2_ps_s{i}", [P, 512], F32)) for i in range(4)]
    p2.pt_ps = [sc.enter_context(nc.psum_tensor(f"p2_pt_ps{i}", [P, 1024], BF16)) for i in range(2)]
    p2.o_ps = sc.enter_context(nc.psum_tensor("p2_o_ps", [P, 512], F32))
    for nm in ["qld", "ssem", "msem", "esem", "zsem", "tsem", "ptsem", "avsem",
               "osem", "vself"]:
        setattr(p2, nm, E(nc.semaphore(f"p2_{nm}")))
    _s = E(nc.semaphore("p2_sto"))
    p2.sto = [_s, _s]
    p2.sched = attn_sched(8)
    sc.close()
    sc = contextlib.ExitStack()
    p3 = NS()
    p3.gwsb = sc.enter_context(nc.sbuf_tensor("p3_gwsb", [P, NCH * 8], BF16))
    p3.bprow = sc.enter_context(nc.sbuf_tensor("p3_bprow", [1, D], BF16))
    p3.gexp = sc.enter_context(nc.sbuf_tensor("p3_gexp", [P, NTT * 8], F32))
    p3.comb = sc.enter_context(nc.sbuf_tensor("p3_comb", [P, NTT * 8], F32))
    p3.e8a = sc.enter_context(nc.sbuf_tensor("p3_e8a", [P, 8], F32))
    p3.t2a = sc.enter_context(nc.sbuf_tensor("p3_t2a", [P, 1], F32))
    p3.t2b = sc.enter_context(nc.sbuf_tensor("p3_t2b", [P, 1], F32))
    scb = contextlib.ExitStack()
    p3.ofb = scb.enter_context(nc.sbuf_tensor("p3_ofb", [P, NCH * TOK], BF16))
    p3.wpb = [scb.enter_context(nc.sbuf_tensor(f"p3_wpb{i}", [P, NCH * 512], BF16)) for i in range(2)]
    scb.close()
    p3.w13b = [sc.enter_context(nc.sbuf_tensor(f"p3_w13b{i}", [P, NCH * 512], BF16)) for i in range(4)]
    p3.w2b = [sc.enter_context(nc.sbuf_tensor(f"p3_w2b{i}", [P, 6 * 512], BF16)) for i in range(2)]
    p3.gs = [sc.enter_context(nc.sbuf_tensor(f"p3_gs{i}", [P, 512], F32)) for i in range(2)]
    p3.a2t = [sc.enter_context(nc.sbuf_tensor(f"p3_a2t{i}", [P, 512], BF16)) for i in range(2)]
    p3.a2f = sc.enter_context(nc.sbuf_tensor("p3_a2f", [P, NTT * 6 * P], BF16))
    p3.ps = [sc.enter_context(nc.psum_tensor(f"p3_ps{i}", [P, 512], F32)) for i in range(2)]
    p3.pg = [sc.enter_context(nc.psum_tensor(f"p3_pg{i}", [P, 512], F32)) for i in range(2)]
    p3.pu = [sc.enter_context(nc.psum_tensor(f"p3_pu{i}", [P, 512], F32)) for i in range(2)]
    p3.ptr = [sc.enter_context(nc.psum_tensor(f"p3_ptr{i}", [P, 1024], BF16)) for i in range(2)]
    p3.pxf = p3.pu
    for nm in ["ldx", "ldc", "mm", "evs", "lnv", "lns", "trs", "xfs", "gsem",
               "vch", "sil", "a2s", "a2e", "dmm", "dev"]:
        setattr(p3, nm, E(nc.semaphore(f"p3_{nm}")))
    _s = E(nc.semaphore("p3_ldw")); p3.ldw = [_s, _s]
    _s = E(nc.semaphore("p3_lde")); p3.lde = [_s, _s]
    _s = E(nc.semaphore("p3_ld2")); p3.ld2 = [_s, _s]
    sc.close()
    sc = contextlib.ExitStack()
    p4 = NS()
    p4.wblk = [sc.enter_context(nc.sbuf_tensor(f"p4_wblk{i}", [P, NCH * 512], BF16)) for i in range(2)]
    p4.wgk = sc.enter_context(nc.sbuf_tensor("p4_wgk", [P, NCH * 32], BF16))
    p4.stage = sc.enter_context(nc.sbuf_tensor("p4_stage", [P, NTT * D], BF16))
    p4.cosb = sc.enter_context(nc.sbuf_tensor("p4_cosb", [P, NTT * 64], BF16))
    p4.sinb = sc.enter_context(nc.sbuf_tensor("p4_sinb", [P, NTT * 64], BF16))
    p4.tmp = [sc.enter_context(nc.sbuf_tensor(f"p4_tmp{i}", [P, 1024], BF16)) for i in range(4)]
    p4.gks = sc.enter_context(nc.sbuf_tensor("p4_gks", [P, NTT * 32], F32))
    p4.ps = [sc.enter_context(nc.psum_tensor(f"p4_ps{i}", [P, 512], F32)) for i in range(2)]
    p4.pgk = sc.enter_context(nc.psum_tensor("p4_pgk", [P, 32], F32))
    for nm in ["ldc", "mm", "mgk", "esem", "gsig", "vch", "sto", "trs", "xfs"]:
        setattr(p4, nm, E(nc.semaphore(f"p4_{nm}")))
    _s = E(nc.semaphore("p4_ldw")); p4.ldw = [_s, _s]
    sc.close()
    sc = contextlib.ExitStack()
    p5 = NS()
    HD5 = 64
    p5.qsb = [sc.enter_context(nc.sbuf_tensor(f"p5_qsb{i}", [P, T], BF16)) for i in range(2)]
    p5.ksb = [sc.enter_context(nc.sbuf_tensor(f"p5_ksb{i}", [P, T], BF16)) for i in range(2)]
    p5.vsb = [sc.enter_context(nc.sbuf_tensor(f"p5_vsb{i}", [P, 16 * P], BF16)) for i in range(2)]
    p5.gsb = sc.enter_context(nc.sbuf_tensor("p5_gsb", [P, 16 * 32], F32))
    p5.gz = sc.enter_context(nc.sbuf_tensor("p5_gz", [32, 1], F32))
    p5.grow = [sc.enter_context(nc.sbuf_tensor(f"p5_grow{i}", [1, T], F32)) for i in range(2)]
    p5.gq16 = sc.enter_context(nc.sbuf_tensor("p5_gq16", [P, 16], F32))
    p5.gkbc = sc.enter_context(nc.sbuf_tensor("p5_gkbc", [P, T], F32))
    p5.esb = [sc.enter_context(nc.sbuf_tensor(f"p5_esb{i}", [P, 512], BF16)) for i in range(2)]
    p5.ptsb = [sc.enter_context(nc.sbuf_tensor(f"p5_ptsb{i}", [P, 512], BF16)) for i in range(2)]
    p5.osb = [sc.enter_context(nc.sbuf_tensor(f"p5_osb{i}", [HD5, T], BF16)) for i in range(2)]
    _scg = contextlib.ExitStack()
    p5.gf = _scg.enter_context(nc.sbuf_tensor("p5_gf", [32, T], F32))
    p5.gcum = _scg.enter_context(nc.sbuf_tensor("p5_gcum", [32, T], F32))
    _scg.close()
    p5.asb = [sc.enter_context(nc.sbuf_tensor(f"p5_asb{i}", [P, 4 * T], BF16)) for i in range(2)]
    p5.ps_s = [sc.enter_context(nc.psum_tensor(f"p5_ps_s{i}", [P, 512], F32)) for i in range(4)]
    p5.pt_ps = [sc.enter_context(nc.psum_tensor(f"p5_pt_ps{i}", [P, 1024], BF16)) for i in range(2)]
    p5.o_ps = sc.enter_context(nc.psum_tensor("p5_o_ps", [P, 512], F32))
    p5.pbc = sc.enter_context(nc.psum_tensor("p5_pbc", [P, 512], F32))
    for nm in ["qld", "gsc", "gtr", "gqt", "gqe", "bcs", "bce", "ssem", "esem",
               "asem", "tsem", "ptsem", "ptsem_s", "avsem", "osem", "gst", "vsf", "gqs"]:
        setattr(p5, nm, E(nc.semaphore(f"p5_{nm}")))
    _s = E(nc.semaphore("p5_gld")); p5.gld = [_s, _s]
    _s = E(nc.semaphore("p5_sto")); p5.sto = [_s, _s]
    p5.sched = attn_sched(16)
    sc.close()
    sc = contextlib.ExitStack()
    p6 = NS()
    NHC = 44
    p6.sgc = sgc_p
    p6.suc = suc_p
    p6.sdc = sdc_p
    p6.hvar = sc.enter_context(nc.sbuf_tensor("p6_hvar", [P, 32], F32))
    p6.rsrow = sc.enter_context(nc.sbuf_tensor("p6_rsrow", [1, TOK], BF16))
    p6.csrow = sc.enter_context(nc.sbuf_tensor("p6_csrow", [1, TOK], BF16))
    scb = contextlib.ExitStack()
    p6.ogf = scb.enter_context(nc.sbuf_tensor("p6_ogf", [P, NCH * TOK], BF16))
    p6.sg_sb = scb.enter_context(nc.sbuf_tensor("p6_sg_sb", [P, NTT * D], BF16))
    p6.otb = [scb.enter_context(nc.sbuf_tensor(f"p6_otb{i}", [P, D], BF16)) for i in range(2)]
    p6.wb = [scb.enter_context(nc.sbuf_tensor(f"p6_wb{i}", [P, NCH * 256], BF16)) for i in range(2)]
    scb.close()
    p6.a2sb = sc.enter_context(nc.sbuf_tensor("p6_a2sb", [P, NHC * 512], BF16))
    p6.wqg_b = [sc.enter_context(nc.sbuf_tensor(f"p6_wqg_b{i}", [P, NCH * P], BF16)) for i in range(2)]
    p6.wqu_b = [sc.enter_context(nc.sbuf_tensor(f"p6_wqu_b{i}", [P, NCH * P], BF16)) for i in range(2)]
    p6.wqd_b = [sc.enter_context(nc.sbuf_tensor(f"p6_wqd_b{i}", [P, NHC * P], BF16)) for i in range(2)]
    p6.sgh = [sc.enter_context(nc.sbuf_tensor(f"p6_sgh{i}", [P, 512], BF16)) for i in range(2)]
    p6.yfb = [sc.enter_context(nc.sbuf_tensor(f"p6_yfb{i}", [P, 512], BF16)) for i in range(2)]
    p6.ps = [sc.enter_context(nc.psum_tensor(f"p6_ps{i}", [P, 512], F32)) for i in range(2)]
    p6.pu = [sc.enter_context(nc.psum_tensor(f"p6_pu{i}", [P, 512], F32)) for i in range(2)]
    p6.ptrb = [sc.enter_context(nc.psum_tensor(f"p6_ptr{i}", [P, 1024], BF16)) for i in range(2)]
    p6.prow = sc.enter_context(nc.psum_tensor("p6_prow", [P, 512], F32))
    for nm in ["ldm", "mm", "evs", "lnv", "lns", "onr", "sil", "trs", "xfs",
               "rsm", "rse", "gup", "a2s", "dmm", "dse", "dtr", "dev", "sto"]:
        setattr(p6, nm, E(nc.semaphore(f"p6_{nm}")))
    _s = E(nc.semaphore("p6_ldo")); p6.ldo = [_s, _s]
    _s = E(nc.semaphore("p6_ldw")); p6.ldw = [_s, _s]
    _s = E(nc.semaphore("p6_ldq")); p6.ldq = [_s, _s]
    _s = E(nc.semaphore("p6_ldd")); p6.ldd = [_s, _s]
    sc.close()

    NGP6 = 64
    NWO6 = 64

    def mm_gu3(gidx):
        return 40 + gidx + 1

    NHC = 44
    with nc.Block() as block:

        @block.sync
        def _(sy):
            sy.dma_start(out=isb[:, :], in_=coreinfo[:, :]).then_inc(cst, 16)
            sy.dma_start(out=i32sb[:, :], in_=id32_in[:, :]).then_inc(cst, 16)
            sy.dma_start(out=ibfsb[:, :], in_=idbf_in[:, :]).then_inc(cst, 16)
            sy.dma_start(out=mnegsb[:, :], in_=maskneg_in[:, :]).then_inc(cst, 16)
            sy.dma_start(out=mbfsb[:, :], in_=maskbf_in[:, :]).then_inc(cst, 16)
            sy.dma_start(out=stream[:, :].rearrange("p (tt d) -> p tt d", tt=NTT),
                         in_=x_t_in.ap().rearrange("(tt p) d -> p tt d", p=P)
                         ).then_inc(p3.ldx, 16)
            sy.dma_start(out=p6.sgc[:, :], in_=sg2d[:, :]).then_inc(p6.ldm, 16)
            sy.dma_start(out=p6.suc[:, :], in_=su2d[:, :]).then_inc(p6.ldm, 16)
            sy.dma_start(out=p6.sdc[:, :], in_=sd2d[:, :]).then_inc(p6.ldm, 16)
            sy.wait_ge(cst, 81)
            rhb0 = sy.alloc_register("rhb0")
            rhb1 = sy.alloc_register("rhb1")
            rtok = sy.alloc_register("rtok")
            sy.load(rhb0, isb[0:1, 0:1])
            sy.load(rhb1, isb[0:1, 1:2])
            sy.load(rtok, isb[0:1, 2:3])
            vhb0 = sy.snap(rhb0, min_val=0, max_val=1024)
            vhb1 = sy.snap(rhb1, min_val=0, max_val=1024)
            vtok = sy.snap(rtok, min_val=0, max_val=1024)

            A0r = A0own.ap().rearrange("(tt p) o -> p tt o", p=P)
            for ob in range(12):
                sy.wait_ge(p1.evs, 8 * (ob + 1))
                if ob >= 1:
                    sy.wait_ge(p1.so[0], 16 * ob)
                sy.dma_start(out=A0r[:, :, ob * 512:(ob + 1) * 512],
                             in_=p1.ev[ob % 2][:, :].rearrange("p (tt o) -> p tt o", tt=NTT)
                             ).then_inc(p1.so[0], 16)

            sy.wait_ge(cc0, 1)
            sy.dma_start(out=A0loc[:, 0:TOK],
                         in_=A0o[:, bass.ds(vhb0, TOK)]).then_inc(cc0, 16)
            sy.dma_start(out=A0loc[:, TOK:2 * TOK],
                         in_=A0o[:, bass.ds(vhb0 + 2048, TOK)]).then_inc(cc0, 16)
            sy.dma_start(out=A0loc[:, 2 * TOK:3 * TOK],
                         in_=A0o[:, bass.ds(vhb0 + 4096, TOK)]).then_inc(cc0, 16)
            for h in range(8):
                if h >= 1:
                    sy.wait_ge(p2.qld, 48 * h)
                if h >= 2:
                    sy.wait_ge(p2.avsem, p2.sched[(h - 2) * 4 + 3]["kt1"])
                else:
                    sy.wait_ge(cc0, 49)
                sy.dma_start(out=p2.qsb[h % 2][:, :],
                             in_=A0loc[:, h * P:(h + 1) * P],
                             transpose=True).then_inc(p2.qld, 16)
                sy.dma_start(out=p2.ksb[h % 2][:, :],
                             in_=A0loc[:, TOK + h * P:TOK + (h + 1) * P],
                             transpose=True).then_inc(p2.qld, 16)
                sy.dma_start(out=p2.vsb[h % 2][:, :].rearrange("p (kt d) -> p kt d", kt=16),
                             in_=A0loc.ap().rearrange("(kt p) o -> p kt o", p=P)
                             [:, :, 2 * TOK + h * P:2 * TOK + (h + 1) * P]
                             ).then_inc(p2.qld, 16)

            sy.wait_ge(cc1, 1)
            sy.dma_start(out=p3.ofb[:, :].rearrange("p (c t) -> p c t", c=NCH),
                         in_=B0o.ap().rearrange("(c p) t -> p c t", p=P)
                         [:, :, bass.ds(vtok, TOK)]).then_inc(p3.ldx, 16)

            A1r = A1own.ap().rearrange("(tt p) o -> p tt o", p=P)
            for mi in range(3):
                if mi < 2:
                    sy.wait_ge(p4.vch, 48 * (mi + 1))
                else:
                    sy.wait_ge(p4.esem, 96)
                if mi >= 1:
                    sy.wait_ge(p4.sto, 16 * mi)
                sy.dma_start(out=A1r[:, :, mi * D:(mi + 1) * D],
                             in_=p4.stage[:, :].rearrange("p (tt o) -> p tt o", tt=NTT)
                             ).then_inc(p4.sto, 16)
            sy.wait_ge(p4.gsig, 16)
            sy.dma_start(out=Agown.ap().rearrange("(tt p) e -> p tt e", p=P),
                         in_=p4.gks[:, :].rearrange("p (tt e) -> p tt e", tt=NTT)
                         ).then_inc(p4.sto, 16)

            sy.wait_ge(cc2, 2)
            sy.dma_start(out=A1loc[:, 0:TOK],
                         in_=A1o[:, bass.ds(vhb1, TOK)]).then_inc(cc2, 16)
            sy.dma_start(out=A1loc[:, TOK:2 * TOK],
                         in_=A1o[:, bass.ds(vhb1 + 2048, TOK)]).then_inc(cc2, 16)
            sy.dma_start(out=A1loc[:, 2 * TOK:3 * TOK],
                         in_=A1o[:, bass.ds(vhb1 + 4096, TOK)]).then_inc(cc2, 16)
            for prr in range(8):
                if prr >= 1:
                    sy.wait_ge(p5.qld, 48 * prr)
                if prr >= 2:
                    sy.wait_ge(p5.avsem, p5.sched[(2 * (prr - 2) + 1) * 4 + 3]["kt1"])
                else:
                    sy.wait_ge(cc2, 50)
                sy.dma_start(out=p5.qsb[prr % 2][:, :],
                             in_=A1loc[:, prr * P:(prr + 1) * P],
                             transpose=True).then_inc(p5.qld, 16)
                sy.dma_start(out=p5.ksb[prr % 2][:, :],
                             in_=A1loc[:, TOK + prr * P:TOK + (prr + 1) * P],
                             transpose=True).then_inc(p5.qld, 16)
                sy.dma_start(out=p5.vsb[prr % 2][:, :].rearrange("p (kt d) -> p kt d", kt=16),
                             in_=A1loc.ap().rearrange("(kt p) o -> p kt o", p=P)
                             [:, :, 2 * TOK + prr * P:2 * TOK + (prr + 1) * P]
                             ).then_inc(p5.qld, 16)

            sy.wait_ge(cc3, 1)
            sy.dma_start(out=B1loc[:, :],
                         in_=B1o[:, bass.ds(vtok, TOK)]).then_inc(cc3, 16)
            for tt in range(NTT):
                if tt >= 1:
                    sy.wait_ge(p6.ldo[0], 16 * tt)
                if tt >= 2:
                    sy.wait_ge(p6.onr, 5 * (tt - 1))
                else:
                    sy.wait_ge(cc3, 17)
                sy.dma_start(out=p6.otb[tt % 2][:, :],
                             in_=B1loc[:, tt * P:(tt + 1) * P],
                             transpose=True).then_inc(p6.ldo[0], 16)
            for tt in range(NTT):
                sy.wait_ge(p6.lnv, 56 + 7 * (tt + 1))
                sy.dma_start(out=y_out.ap().rearrange("(tt p) d -> p tt d", p=P)[:, tt, :],
                             in_=stream[:, tt * D:(tt + 1) * D]).then_inc(p6.sto, 16)
            sy.wait_ge(p6.sto, 16 * NTT)

        @block.gpsimd
        def _(g):
            g.memset(ones1[:, :], 1.0)
            g.memset(onec32[:, :], 1.0)
            g.memset(nones[:, :], -128.0)
            g.memset(epsln[:, :], EPS_LN)
            g.memset(epsrm[:, :], EPS_RMS)
            g.memset(p5.gz[:, :], 0.0)
            g.memset(shcol[:, :], -16.0).then_inc(cst, 1)
            g.dma_start(out=x_f[:, :].rearrange("p (c t) -> p c t", c=NCH),
                        in_=x_f_in.ap().rearrange("(c p) t -> p c t", p=P)
                        ).then_inc(p1.ldx, 16)
            g.dma_start(out=p1.brow[:, :], in_=b_attn.ap().unsqueeze(0)).then_inc(p1.ldx, 16)
            for ob in range(12):
                if ob >= 1:
                    g.wait_ge(p1.ldw[0], 16 * ob)
                if ob >= 2:
                    g.wait_ge(p1.mm, 8 * (ob - 1))
                g.dma_start(out=p1.wblk[ob % 2][:, :].rearrange("p (c o) -> p c o", c=NCH),
                            in_=w_attn.ap().rearrange("(c p) o -> p c o", p=P)
                            [:, :, ob * 512:(ob + 1) * 512]).then_inc(p1.ldw[0], 16)
            g.wait_ge(p1.so[0], 16 * 12)
            g.collective_compute("AllGather", ALU.bypass, replica_groups=RG,
                                 ins=[A0own.ap().opt()], outs=[A0o.ap().opt()]
                                 ).then_inc(cc0, 1)
            for h in range(8):
                g.wait_ge(p2.osem, 4 * (h + 1))
                if h >= 1:
                    g.wait_ge(p2.sto[0], 16 * h)
                g.dma_start(out=B0own[h * P:(h + 1) * P, :],
                            in_=p2.osb[h % 2][:, :]).then_inc(p2.sto[0], 16)
            g.wait_ge(p2.sto[0], 16 * 8)
            g.collective_compute("AllGather", ALU.bypass, replica_groups=RG,
                                 ins=[B0own.ap().opt()], outs=[B0o.ap().opt()]
                                 ).then_inc(cc1, 1)
            g.wait_ge(cc1, 1)
            g.dma_start(out=p3.bprow[:, :], in_=b_proj.ap().unsqueeze(0)).then_inc(p3.ldc, 16)
            g.dma_start(out=p3.gwsb[:, :].rearrange("p (c e) -> p c e", c=NCH),
                        in_=gate_w.ap().rearrange("(c p) e -> p c e", p=P)).then_inc(p3.ldc, 16)
            for ob in range(4):
                if ob >= 1:
                    g.wait_ge(p3.ldw[0], 16 * ob)
                if ob >= 2:
                    g.wait_ge(p3.mm, 8 * (ob - 1))
                g.dma_start(out=p3.wpb[ob % 2][:, :].rearrange("p (c o) -> p c o", c=NCH),
                            in_=w_proj.ap().rearrange("(c p) o -> p c o", p=P)
                            [:, :, ob * 512:(ob + 1) * 512]).then_inc(p3.ldw[0], 16)
            g.wait_ge(p3.mm, 32)
            for e in range(8):
                for piece, (foff, fw) in enumerate(PIECES):
                    pig = e * 2 + piece
                    par = pig % 2
                    if pig >= 1:
                        g.wait_ge(p3.lde[0], 32 * pig)
                    for (mat, slot) in [(w1, 2 * par), (w3, 2 * par + 1)]:
                        if pig >= 2:
                            pe_, pp_ = divmod(pig - 2, 2)
                            g.wait_ge(p3.mm, mm_gu3(pe_ * 16 + pp_ * 8 + 7))
                        g.dma_start(
                            out=p3.w13b[slot][:, 0:NCH * fw].rearrange(
                                "p (c f) -> p c f", c=NCH),
                            in_=mat[e, :, foff:foff + fw].rearrange("(c p) f -> p c f", p=P)
                        ).then_inc(p3.lde[0], 16)
                for db in range(4):
                    j = e * 4 + db
                    if j >= 1:
                        g.wait_ge(p3.ld2[0], 16 * j)
                    if j >= 2:
                        g.wait_ge(p3.dmm, 48 * (j - 1))
                    g.dma_start(
                        out=p3.w2b[j % 2][:, :].rearrange("p (c d) -> p c d", c=6),
                        in_=w2[e, :, db * 512:(db + 1) * 512].rearrange("(c p) d -> p c d", p=P)
                    ).then_inc(p3.ld2[0], 16)
            g.wait_ge(p3.lnv, 112)
            g.dma_start(out=p4.wgk[:, :].rearrange("p (c e) -> p c e", c=NCH),
                        in_=WgkT.ap().rearrange("(c p) e -> p c e", p=P)).then_inc(p4.ldc, 16)
            g.dma_start(out=p4.cosb[:, :].rearrange("p (tt f) -> p tt f", tt=NTT),
                        in_=rope_cos.ap().rearrange("(tt p) f -> p tt f", p=P)
                        ).then_inc(p4.ldc, 16)
            g.dma_start(out=p4.sinb[:, :].rearrange("p (tt f) -> p tt f", tt=NTT),
                        in_=rope_sin.ap().rearrange("(tt p) f -> p tt f", p=P)
                        ).then_inc(p4.ldc, 16)
            bi = 0
            for mi, mat in enumerate([WqT, WkT, WvT]):
                for ob in range(4):
                    if bi >= 1:
                        g.wait_ge(p4.ldw[0], 16 * bi)
                    if bi >= 2:
                        g.wait_ge(p4.mm, 8 * (bi - 1))
                    g.dma_start(out=p4.wblk[bi % 2][:, :].rearrange("p (c o) -> p c o", c=NCH),
                                in_=mat.ap().rearrange("(c p) o -> p c o", p=P)
                                [:, :, ob * 512:(ob + 1) * 512]).then_inc(p4.ldw[0], 16)
                    bi += 1
            g.wait_ge(p4.sto, 64)
            g.collective_compute("AllGather", ALU.bypass, replica_groups=RG,
                                 ins=[A1own.ap().opt()], outs=[A1o.ap().opt()]
                                 ).then_inc(cc2, 1)
            g.collective_compute("AllGather", ALU.bypass, replica_groups=RG,
                                 ins=[Agown.ap().opt()], outs=[Ago.ap().opt()]
                                 ).then_inc(cc2, 1)
            g.wait_ge(cc2, 2)
            g.dma_start(out=p5.gsb[:, :].rearrange("p (tt h) -> p tt h", tt=16),
                        in_=Ago.ap().rearrange("(tt p) h -> p tt h", p=P)
                        ).then_inc(p5.gqs, 16)
            g.wait_ge(p5.gsc, 1)
            g.dma_start(out=gdram[:, :], in_=p5.gcum[:, :]).then_inc(p5.gst, 16)
            g.wait_ge(p5.gst, 16)
            rp5 = g.alloc_register("rp5")
            g.load(rp5, isb[0:1, 3:4])
            offv = g.snap(rp5, min_val=0, max_val=16)
            g.dma_start(out=gloc[:, :],
                        in_=gdram[bass.ds(offv, 16), :]).then_inc(p5.gst, 16)
            g.wait_ge(p5.gst, 32)

            def p5_store(pr):
                g.wait_ge(p5.osem, 4 * (pr + 1))
                if pr >= 1:
                    g.wait_ge(p5.sto[0], 16 * pr)
                g.dma_start(out=B1own[pr * 64:(pr + 1) * 64, :],
                            in_=p5.osb[pr % 2][:, :]).then_inc(p5.sto[0], 16)

            for pr in range(16):
                if pr >= 1:
                    g.wait_ge(p5.gld[0], 16 * pr)
                if pr >= 2:
                    g.wait_ge(p5.bcs, 4 * (pr - 1))
                g.dma_start(out=p5.grow[pr % 2][:, :],
                            in_=gloc[pr:pr + 1, :]).then_inc(p5.gld[0], 16)
                if pr >= 2:
                    p5_store(pr - 2)
            p5_store(14)
            p5_store(15)
            g.wait_ge(p5.sto[0], 16 * 16)
            g.collective_compute("AllGather", ALU.bypass, replica_groups=RG,
                                 ins=[B1own.ap().opt()], outs=[B1o.ap().opt()]
                                 ).then_inc(cc3, 1)
            g.wait_ge(cc3, 1)
            bi = 0
            for mat in (WgT, WoT):
                for ob in range(8):
                    if bi >= 1:
                        g.wait_ge(p6.ldw[0], 16 * bi)
                    if bi >= 2:
                        g.wait_ge(p6.mm, 8 * (bi - 1))
                    g.dma_start(out=p6.wb[bi % 2][:, :].rearrange("p (c o) -> p c o", c=NCH),
                                in_=mat.ap().rearrange("(c p) o -> p c o", p=P)
                                [:, :, ob * 256:(ob + 1) * 256]).then_inc(p6.ldw[0], 16)
                    bi += 1
            qi = 0
            for th in range(2):
                for hc in range(NHC):
                    if qi >= 1:
                        g.wait_ge(p6.ldq[0], 32 * qi)
                    if qi >= 2:
                        g.wait_ge(p6.gup, 2 * (qi - 1))
                    else:
                        g.wait_ge(p6.mm, NGP6 + NWO6)
                    g.dma_start(out=p6.wqg_b[qi % 2][:, :].rearrange("p (c h) -> p c h", c=NCH),
                                in_=wqgT.ap().rearrange("(c p) h -> p c h", p=P)
                                [:, :, hc * P:(hc + 1) * P]).then_inc(p6.ldq[0], 16)
                    g.dma_start(out=p6.wqu_b[qi % 2][:, :].rearrange("p (c h) -> p c h", c=NCH),
                                in_=wquT.ap().rearrange("(c p) h -> p c h", p=P)
                                [:, :, hc * P:(hc + 1) * P]).then_inc(p6.ldq[0], 16)
                    qi += 1
                for dc in range(NCH):
                    di = th * NCH + dc
                    if di >= 1:
                        g.wait_ge(p6.ldd[0], 16 * di)
                    if di >= 2:
                        g.wait_ge(p6.dse, di - 1)
                    else:
                        g.wait_ge(p6.mm, NGP6 + NWO6)
                    g.dma_start(out=p6.wqd_b[di % 2][:, :].rearrange("p (c d) -> p c d", c=NHC),
                                in_=wqdT.ap().rearrange("(c p) d -> p c d", p=P)
                                [:, :, dc * P:(dc + 1) * P]).then_inc(p6.ldd[0], 16)

        @block.tensor
        def _(t):
            t.wait_ge(cst, 81)
            t.wait_ge(p1.ldx, 32)
            gi = 0
            for ob in range(12):
                t.wait_ge(p1.ldw[0], 16 * (ob + 1))
                for tt in range(NTT):
                    pb = p1.ps[gi % 2]
                    if gi >= 2:
                        t.wait_ge(p1.evs, gi - 1)
                    for c in range(NCH):
                        t.matmul(pb[:, :],
                                 x_f[:, c * TOK + tt * P: c * TOK + (tt + 1) * P],
                                 p1.wblk[ob % 2][:, c * 512:(c + 1) * 512],
                                 start=(c == 0), stop=False)
                    t.matmul(pb[:, :], ones1[:, :],
                             p1.brow[0:1, ob * 512:(ob + 1) * 512],
                             start=False, stop=True).then_inc(p1.mm, 1)
                    gi += 1
            def p2_smms(idx):
                h, gq = divmod(idx, 4)
                rec = p2.sched[idx]
                smm = rec["smm0"]
                for i in range(4):
                    qt = 4 * gq + i
                    for (kc, cw) in chunks_of(qt):
                        if smm >= 4:
                            t.wait_ge(p2.esem, smm - 3)
                        t.matmul(p2.ps_s[smm % 4][:, 0:cw],
                                 p2.qsb[h % 2][:, qt * P:(qt + 1) * P],
                                 p2.ksb[h % 2][:, kc * 512:kc * 512 + cw],
                                 start=True, stop=True).then_inc(p2.ssem, 1)
                        smm += 1

            def p2_tav(idx):
                h, gq = divmod(idx, 4)
                rec = p2.sched[idx]
                t.wait_ge(p2.zsem, rec["zsc1"])
                nkt = 4 * gq + 4

                def p2_tr(kt):
                    ktc = rec["kt0"] + kt
                    i0 = max(0, kt - 4 * gq)
                    if ktc >= 2:
                        t.wait_ge(p2.ptsem, ktc - 1)
                    last = None
                    for i in range(i0, 4):
                        last = t.transpose(p2.pt_ps[ktc % 2][:, i * P:(i + 1) * P],
                                           p2.psb[idx % 2][:, i * T + kt * P: i * T + (kt + 1) * P],
                                           ibfsb[:, :])
                    last.then_inc(p2.tsem, 1)

                def p2_av(kt):
                    ktc = rec["kt0"] + kt
                    i0 = max(0, kt - 4 * gq)
                    t.wait_ge(p2.ptsem, ktc + 1)
                    if kt == 0 and rec["oev0"] > 0:
                        t.wait_ge(p2.osem, rec["oev0"])
                    t.matmul(p2.o_ps[:, i0 * P:512],
                             p2.vsb[h % 2][:, kt * P:(kt + 1) * P],
                             p2.ptsb[ktc % 2][:, i0 * P:512],
                             start=(kt == 0), stop=(kt == nkt - 1)
                             ).then_inc(p2.avsem, 1)

                p2_tr(0)
                for kt in range(1, nkt):
                    p2_tr(kt)
                    p2_av(kt - 1)
                p2_av(nkt - 1)

            for idx in range(32):
                if idx % 4 == 0:
                    t.wait_ge(p2.qld, 48 * (idx // 4 + 1))
                p2_smms(idx)
                if idx >= 1:
                    p2_tav(idx - 1)
            p2_tav(31)
            t.wait_ge(p3.ldx, 32)
            t.wait_ge(p3.ldc, 32)
            gi = 0
            for ob in range(4):
                t.wait_ge(p3.ldw[0], 16 * (ob + 1))
                for tt in range(NTT):
                    pb = p3.ps[gi % 2]
                    if gi >= 2:
                        t.wait_ge(p3.evs, gi - 1)
                    for c in range(NCH):
                        t.matmul(pb[:, :],
                                 p3.ofb[:, c * TOK + tt * P: c * TOK + (tt + 1) * P],
                                 p3.wpb[ob % 2][:, c * 512:(c + 1) * 512],
                                 start=(c == 0), stop=False)
                    t.matmul(pb[:, :], ones1[:, :],
                             p3.bprow[0:1, ob * 512:(ob + 1) * 512],
                             start=False, stop=True).then_inc(p3.mm, 1)
                    gi += 1
            ntr = 0
            for tt in range(NTT):
                t.wait_ge(p3.lnv, 7 * (tt + 1))
                for cg in range(4):
                    if ntr >= 2:
                        t.wait_ge(p3.xfs, ntr - 1)
                    last = None
                    for c4 in range(4):
                        c = cg * 4 + c4
                        last = t.transpose(p3.pxf[ntr % 2][:, c4 * P:(c4 + 1) * P],
                                           stream[:, tt * D + c * P: tt * D + (c + 1) * P],
                                           i32sb[:, :])
                    last.then_inc(p3.trs, 1)
                    ntr += 1
            for tt in range(NTT):
                t.wait_ge(p3.xfs, 4 * (tt + 1))
                if tt > 0:
                    t.wait_ge(p3.gsem, tt)
                for c in range(NCH):
                    t.matmul(p3.pg[0][:, 0:8],
                             x_f[:, c * TOK + tt * P: c * TOK + (tt + 1) * P],
                             p3.gwsb[:, c * 8:(c + 1) * 8],
                             start=(c == 0), stop=(c == NCH - 1)
                             ).then_inc(p3.mm, 1 if c == NCH - 1 else 0)
            t.wait_ge(p3.gsem, 8)
            t.wait_ge(p3.evs, 32)
            t.wait_ge(p3.xfs, 32)
            for e in range(8):
                pi = 0
                def p3_a2tr(gidx):
                    piece_ = (gidx % 16) // 8
                    fw_ = PIECES[piece_][1]
                    t.wait_ge(p3.a2s, gidx + 1)
                    if gidx >= 2:
                        t.wait_ge(p3.a2e, gidx - 1)
                    nt = fw_ // P
                    last = None
                    for c4 in range(nt):
                        last = t.transpose(p3.ptr[gidx % 2][:, c4 * P:(c4 + 1) * P],
                                           p3.a2t[gidx % 2][:, c4 * P:(c4 + 1) * P],
                                           ibfsb[:, :])
                    last.then_inc(p3.trs, 1)

                for piece, (foff, fw) in enumerate(PIECES):
                    pig = e * 2 + piece
                    par = pig % 2
                    t.wait_ge(p3.lde[0], 32 * (pig + 1))
                    for tt in range(NTT):
                        gidx = e * 16 + pi
                        if gidx >= 2:
                            t.wait_ge(p3.sil, gidx - 1)
                            t.wait_ge(p3.a2s, gidx - 1)
                        for c in range(NCH):
                            t.matmul(p3.pg[gidx % 2][:, 0:fw],
                                     x_f[:, c * TOK + tt * P: c * TOK + (tt + 1) * P],
                                     p3.w13b[2 * par][:, c * fw:(c + 1) * fw],
                                     start=(c == 0), stop=(c == NCH - 1))
                        inst = None
                        for c in range(NCH):
                            inst = t.matmul(p3.pu[gidx % 2][:, 0:fw],
                                            x_f[:, c * TOK + tt * P: c * TOK + (tt + 1) * P],
                                            p3.w13b[2 * par + 1][:, c * fw:(c + 1) * fw],
                                            start=(c == 0), stop=(c == NCH - 1))
                        inst.then_inc(p3.mm, 1)
                        if pi >= 1:
                            p3_a2tr(e * 16 + pi - 1)
                        pi += 1
                p3_a2tr(e * 16 + 15)
                for db in range(4):
                    j = e * 4 + db
                    t.wait_ge(p3.ld2[0], 16 * (j + 1))
                    for tt in range(NTT):
                        di = e * 32 + db * 8 + tt
                        if di >= 2:
                            t.wait_ge(p3.dev, di - 1)
                        t.wait_ge(p3.a2e, e * 16 + 8 + tt + 1)
                        inst = None
                        for fc in range(6):
                            inst = t.matmul(p3.ps[di % 2][:, :],
                                            p3.a2f[:, tt * 6 * P + fc * P: tt * 6 * P + (fc + 1) * P],
                                            p3.w2b[j % 2][:, fc * 512:(fc + 1) * 512],
                                            start=(fc == 0), stop=(fc == 5))
                        inst.then_inc(p3.dmm, 6)
            t.wait_ge(p3.lnv, 112)
            t.wait_ge(p3.mm, 168)
            ntr = 0
            for tt in range(NTT):
                for cg in range(4):
                    if ntr >= 2:
                        t.wait_ge(p4.xfs, ntr - 1)
                    last = None
                    for c4 in range(4):
                        c = cg * 4 + c4
                        last = t.transpose(p4.ps[ntr % 2][:, c4 * P:(c4 + 1) * P],
                                           stream[:, tt * D + c * P: tt * D + (c + 1) * P],
                                           i32sb[:, :])
                    last.then_inc(p4.trs, 1)
                    ntr += 1
            t.wait_ge(p4.ldc, 48)
            t.wait_ge(p4.xfs, 32)
            gi = 0
            for mi in range(3):
                for ob in range(4):
                    t.wait_ge(p4.ldw[0], 16 * (gi + 1))
                    for tt in range(NTT):
                        grp = gi * 8 + tt
                        pb = p4.ps[grp % 2]
                        if grp >= 2:
                            t.wait_ge(p4.esem, grp - 1)
                        inst = None
                        for c in range(NCH):
                            inst = t.matmul(pb[:, :],
                                            x_f[:, c * TOK + tt * P: c * TOK + (tt + 1) * P],
                                            p4.wblk[gi % 2][:, c * 512:(c + 1) * 512],
                                            start=(c == 0), stop=(c == NCH - 1))
                        inst.then_inc(p4.mm, 1)
                    gi += 1
            for tt in range(NTT):
                if tt > 0:
                    t.wait_ge(p4.gsig, tt)
                inst = None
                for c in range(NCH):
                    inst = t.matmul(p4.pgk[:, :],
                                    x_f[:, c * TOK + tt * P: c * TOK + (tt + 1) * P],
                                    p4.wgk[:, c * 32:(c + 1) * 32],
                                    start=(c == 0), stop=(c == NCH - 1))
                inst.then_inc(p4.mgk, 1)
            t.wait_ge(p4.esem, 96)
            t.wait_ge(p4.gsig, 16)
            t.wait_ge(p5.gqs, 16)
            for kt in range(16):
                if kt >= 1:
                    t.wait_ge(p5.gtr, 2 * kt)
                t.transpose(p5.pbc[0:32, 0:128], p5.gsb[:, kt * 32:(kt + 1) * 32],
                            i32sb[:, :]).then_inc(p5.gtr, 1)
            t.wait_ge(p5.gtr, 32)
            for pr in range(16):
                buf5 = (pr // 2) % 2
                sub5 = (pr % 2) * 64
                t.wait_ge(p5.qld, 48 * (pr // 2 + 1))
                t.wait_ge(p5.gld[0], 16 * (pr + 1))
                if pr >= 1:
                    t.wait_ge(p5.esem, p5.sched[(pr - 1) * 4 + 3]["exp1"])
                    t.wait_ge(p5.bce, 4 * pr)
                last = None
                for j in range(16):
                    last = t.transpose(p5.pbc[:, j:j + 1],
                                       p5.grow[pr % 2][0:1, j * P:(j + 1) * P],
                                       i32sb[0:1, 0:1])
                last.then_inc(p5.gqt, 1)
                t.wait_ge(p5.gqe, pr + 1)
                for j in range(4):
                    if pr > 0 or j > 0:
                        t.wait_ge(p5.bce, 4 * pr + j)
                    t.matmul(p5.pbc[:, :], onec32[:, :],
                             p5.grow[pr % 2][0:1, j * 512:(j + 1) * 512],
                             start=True, stop=True).then_inc(p5.bcs, 1)
                t.wait_ge(p5.bce, 4 * (pr + 1))

                def p5_smms(idx):
                    gq = idx % 4
                    rec = p5.sched[idx]
                    smm = rec["smm0"]
                    for i in range(4):
                        qt = 4 * gq + i
                        for (kc, cw) in chunks_of(qt):
                            if smm >= 4:
                                t.wait_ge(p5.asem, smm - 3)
                            t.matmul(p5.ps_s[smm % 4][:, 0:cw],
                                     p5.qsb[buf5][sub5:sub5 + 64, qt * P:(qt + 1) * P],
                                     p5.ksb[buf5][sub5:sub5 + 64, kc * 512:kc * 512 + cw],
                                     start=True, stop=True).then_inc(p5.ssem, 1)
                            smm += 1

                def p5_tav(idx):
                    gq = idx % 4
                    rec = p5.sched[idx]
                    t.wait_ge(p5.asem, rec["smm1"])
                    nkt = 4 * gq + 4

                    def p5_tr(kt):
                        ktc = rec["kt0"] + kt
                        i0 = max(0, kt - 4 * gq)
                        if ktc >= 2:
                            sem_ = p5.ptsem if ktc % 2 == 0 else p5.ptsem_s
                            t.wait_ge(sem_, (ktc - 2) // 2 + 1)
                        last = None
                        for i in range(i0, 4):
                            last = t.transpose(p5.pt_ps[ktc % 2][:, i * P:(i + 1) * P],
                                               p5.asb[idx % 2][:, i * T + kt * P: i * T + (kt + 1) * P],
                                               ibfsb[:, :])
                        last.then_inc(p5.tsem, 1)

                    def p5_av(kt):
                        ktc = rec["kt0"] + kt
                        i0 = max(0, kt - 4 * gq)
                        sem_ = p5.ptsem if ktc % 2 == 0 else p5.ptsem_s
                        t.wait_ge(sem_, ktc // 2 + 1)
                        if kt == 0 and rec["oev0"] > 0:
                            t.wait_ge(p5.osem, rec["oev0"])
                        t.matmul(p5.o_ps[0:64, i0 * P:512],
                                 p5.vsb[buf5][:, kt * P + sub5: kt * P + sub5 + 64],
                                 p5.ptsb[ktc % 2][:, i0 * P:512],
                                 start=(kt == 0), stop=(kt == nkt - 1)
                                 ).then_inc(p5.avsem, 1)

                    p5_tr(0)
                    for kt in range(1, nkt):
                        p5_tr(kt)
                        p5_av(kt - 1)
                    p5_av(nkt - 1)

                for gq in range(4):
                    p5_smms(pr * 4 + gq)
                    if gq >= 1:
                        p5_tav(pr * 4 + gq - 1)
                p5_tav(pr * 4 + 3)
            t.wait_ge(p5.osem, 64)
            gi = 0
            for ob in range(8):
                t.wait_ge(p6.ldw[0], 16 * (ob + 1))
                for tt in range(NTT):
                    pb = p6.ps[gi % 2]
                    if gi >= 2:
                        t.wait_ge(p6.sil, gi - 1)
                    inst = None
                    for c in range(NCH):
                        inst = t.matmul(pb[:, 0:256],
                                        x_f[:, c * TOK + tt * P: c * TOK + (tt + 1) * P],
                                        p6.wb[ob % 2][:, c * 256:(c + 1) * 256],
                                        start=(c == 0), stop=(c == NCH - 1))
                    inst.then_inc(p6.mm, 1)
                    gi += 1
            ntr = 0
            for tt in range(NTT):
                t.wait_ge(p6.onr, 5 * tt + 5)
                for cg in range(4):
                    if ntr >= 2:
                        t.wait_ge(p6.xfs, ntr - 1)
                    last = None
                    for c4 in range(4):
                        c = cg * 4 + c4
                        last = t.transpose(p6.ptrb[ntr % 2][:, c4 * P:(c4 + 1) * P],
                                           p6.sg_sb[:, tt * D + c * P: tt * D + (c + 1) * P],
                                           ibfsb[:, :])
                    last.then_inc(p6.trs, 1)
                    ntr += 1
            gi = 0
            for ob in range(8):
                bi = 8 + ob
                t.wait_ge(p6.ldw[0], 16 * (bi + 1))
                for tt in range(NTT):
                    grp = NGP6 + gi
                    pb = p6.ps[grp % 2]
                    if gi >= 2:
                        t.wait_ge(p6.evs, gi - 1)
                    else:
                        t.wait_ge(p6.sil, NGP6)
                    t.wait_ge(p6.xfs, 4 * (tt + 1))
                    inst = None
                    for c in range(NCH):
                        inst = t.matmul(pb[:, 0:256],
                                        p6.ogf[:, c * TOK + tt * P: c * TOK + (tt + 1) * P],
                                        p6.wb[bi % 2][:, c * 256:(c + 1) * 256],
                                        start=(c == 0), stop=(c == NCH - 1))
                    inst.then_inc(p6.mm, 1)
                    gi += 1
            ntr = 32
            for tt in range(NTT):
                t.wait_ge(p6.lnv, 7 * (tt + 1))
                for cg in range(4):
                    if ntr >= 34:
                        t.wait_ge(p6.xfs, ntr - 1)
                    else:
                        t.wait_ge(p6.evs, NWO6)
                    last = None
                    for c4 in range(4):
                        c = cg * 4 + c4
                        last = t.transpose(p6.ps[ntr % 2][:, c4 * P:(c4 + 1) * P],
                                           stream[:, tt * D + c * P: tt * D + (c + 1) * P],
                                           i32sb[:, :])
                    last.then_inc(p6.trs, 1)
                    ntr += 1
            for th in range(2):
                tsl = slice(th * 512, (th + 1) * 512)
                t.wait_ge(p6.xfs, 64)
                if th == 1:
                    t.wait_ge(p6.rse, 2)
                inst = None
                for c in range(NCH):
                    inst = t.matmul(p6.prow[0:1, :], nones[:, :],
                                    x_f[:, c * TOK + th * 512: c * TOK + (th + 1) * 512],
                                    start=(c == 0), stop=(c == NCH - 1))
                inst.then_inc(p6.rsm, 1)
                t.wait_ge(p6.rse, 2 * th + 1)
                if th == 1:
                    t.wait_ge(p6.dse, NCH)
                for hc in range(NHC):
                    qi = th * NHC + hc
                    t.wait_ge(p6.ldq[0], 32 * (qi + 1))
                    if qi >= 2:
                        t.wait_ge(p6.gup, 2 * (qi - 1))
                    for c in range(NCH):
                        t.matmul(p6.ps[qi % 2][:, :],
                                 p6.wqg_b[qi % 2][:, c * P:(c + 1) * P],
                                 x_f[:, c * TOK + th * 512: c * TOK + (th + 1) * 512],
                                 start=(c == 0), stop=False)
                    t.matmul(p6.ps[qi % 2][:, :], ones1[:, :], p6.rsrow[0:1, tsl],
                             start=False, stop=True)
                    inst = None
                    for c in range(NCH):
                        t.matmul(p6.pu[qi % 2][:, :],
                                 p6.wqu_b[qi % 2][:, c * P:(c + 1) * P],
                                 x_f[:, c * TOK + th * 512: c * TOK + (th + 1) * 512],
                                 start=(c == 0), stop=False)
                    inst = t.matmul(p6.pu[qi % 2][:, :], ones1[:, :], p6.rsrow[0:1, tsl],
                                    start=False, stop=True)
                    inst.then_inc(p6.mm, 1)
                t.wait_ge(p6.a2s, NHC * (th + 1))
                inst = None
                for hc in range(NHC):
                    inst = t.matmul(p6.prow[0:1, :], nones[:, :],
                                    p6.a2sb[:, hc * 512:(hc + 1) * 512],
                                    start=(hc == 0), stop=(hc == NHC - 1))
                inst.then_inc(p6.rsm, 1)
                t.wait_ge(p6.rse, 2 * th + 2)
                def p6_dtr(pdi):
                    t.wait_ge(p6.dse, pdi + 1)
                    if pdi >= 2:
                        t.wait_ge(p6.dev, pdi - 1)
                    last = None
                    for j in range(4):
                        last = t.transpose(p6.ptrb[pdi % 2][:, j * P:(j + 1) * P],
                                           p6.yfb[pdi % 2][:, j * P:(j + 1) * P],
                                           ibfsb[:, :])
                    last.then_inc(p6.dtr, 1)

                for dc in range(NCH):
                    di = th * NCH + dc
                    t.wait_ge(p6.ldd[0], 16 * (di + 1))
                    if di >= 2:
                        t.wait_ge(p6.dse, di - 1)
                    for hc in range(NHC):
                        t.matmul(p6.pu[di % 2][:, :],
                                 p6.wqd_b[di % 2][:, hc * P:(hc + 1) * P],
                                 p6.a2sb[:, hc * 512:(hc + 1) * 512],
                                 start=(hc == 0), stop=False)
                    inst = t.matmul(p6.pu[di % 2][:, :], ones1[:, :], p6.csrow[0:1, tsl],
                                    start=False, stop=True)
                    inst.then_inc(p6.dmm, 1)
                    if dc >= 1:
                        p6_dtr(di - 1)
                    if dc == NCH - 1:
                        p6_dtr(di)

        @block.vector
        def _(v):
            v.wait_ge(cst, 81)
            gi = 0
            for ob in range(12):
                for tt in range(NTT):
                    v.wait_ge(p1.mm, gi + 1)
                    if ob >= 2:
                        v.wait_ge(p1.so[0], 16 * (ob - 1))
                    v.tensor_copy(p1.ev[ob % 2][:, tt * 512:(tt + 1) * 512],
                                  p1.ps[gi % 2][:, :]).then_inc(p1.evs, 1)
                    gi += 1
            nvself = [0]

            def p2_v_softmax(idx):
                h, gq = divmod(idx, 4)
                rec = p2.sched[idx]
                smm = rec["smm0"]
                for i in range(4):
                    qt = 4 * gq + i
                    chs = chunks_of(qt)
                    for (kc, cw) in chs:
                        if kc == len(chs) - 1:
                            v.wait_ge(p2.ssem, smm + 1)
                            v.tensor_tensor(p2.ps_s[smm % 4][:, cw - P:cw],
                                            p2.ps_s[smm % 4][:, cw - P:cw],
                                            mnegsb[:, :], ALU.add).then_inc(p2.msem, 1)
                        smm += 1
                    v.wait_ge(p2.esem, smm)
                    nkc = len(chs)
                    zq = rec["zsc0"] + i
                    if zq > 0:
                        v.wait_ge(p2.zsem, zq)
                    v.reduce_sum(p2.zinv[:, i:i + 1], p2.zp[:, i * 4:i * 4 + nkc],
                                 AX.X).then_inc(p2.vself, 1)
                    nvself[0] += 1
                    v.wait_ge(p2.vself, nvself[0])
                    v.reciprocal(p2.zinv[:, i:i + 1], p2.zinv[:, i:i + 1]
                                 ).then_inc(p2.vself, 1)
                    nvself[0] += 1
                    v.wait_ge(p2.vself, nvself[0])
                    v.tensor_scalar_mul(p2.psb[idx % 2][:, i * T: i * T + (qt + 1) * P],
                                        p2.psb[idx % 2][:, i * T: i * T + (qt + 1) * P],
                                        p2.zinv[:, i:i + 1]).then_inc(p2.zsem, 1)

            def p2_v_evacs(idx):
                h, gq = divmod(idx, 4)
                rec = p2.sched[idx]
                ktc = rec["kt0"]
                for kt in range(4 * gq + 4):
                    i0 = max(0, kt - 4 * gq)
                    v.wait_ge(p2.tsem, ktc + 1)
                    v.tensor_copy(p2.ptsb[ktc % 2][:, i0 * P:512],
                                  p2.pt_ps[ktc % 2][:, i0 * P:512]).then_inc(p2.ptsem, 1)
                    ktc += 1
                v.wait_ge(p2.avsem, rec["kt1"])
                if h >= 2 and gq == 0:
                    v.wait_ge(p2.sto[0], 16 * (h - 1))
                v.tensor_copy(p2.osb[h % 2][:, gq * 512:(gq + 1) * 512],
                              p2.o_ps[:, :]).then_inc(p2.osem, 1)

            for idx in range(32):
                p2_v_softmax(idx)
                if idx >= 1:
                    p2_v_evacs(idx - 1)
            p2_v_evacs(31)
            gi = 0
            for ob in range(4):
                for tt in range(NTT):
                    v.wait_ge(p3.mm, gi + 1)
                    v.tensor_tensor(stream[:, tt * D + ob * 512: tt * D + (ob + 1) * 512],
                                    p3.ps[gi % 2][:, :],
                                    stream[:, tt * D + ob * 512: tt * D + (ob + 1) * 512],
                                    ALU.add).then_inc(p3.evs, 1)
                    gi += 1

            def ln_vector(ph, tt, lnv_base, lns_base):
                sl = slice(tt * D, (tt + 1) * D)
                v.reduce_sum(sums[:, tt:tt + 1], stream[:, sl], AX.X).then_inc(ph.lnv, 1)
                v.wait_ge(ph.lns, lns_base + 2 * tt + 1)
                v.wait_ge(ph.lnv, lnv_base + 7 * tt + 1)
                v.tensor_scalar_mul(mu[:, tt:tt + 1], sums[:, tt:tt + 1], 1.0 / D
                                    ).then_inc(ph.lnv, 1)
                v.wait_ge(ph.lnv, lnv_base + 7 * tt + 2)
                v.tensor_scalar_mul(varv[:, tt:tt + 1], sq[:, tt:tt + 1], 1.0 / D
                                    ).then_inc(ph.lnv, 1)
                v.wait_ge(ph.lnv, lnv_base + 7 * tt + 3)
                v.scalar_tensor_tensor(varv[:, tt:tt + 1], mu[:, tt:tt + 1], mu[:, tt:tt + 1],
                                       varv[:, tt:tt + 1], ALU.mult, ALU.subtract
                                       ).then_inc(ph.lnv, 1)
                v.wait_ge(ph.lns, lns_base + 2 * tt + 2)
                v.reciprocal(rsd[:, tt:tt + 1], rsd[:, tt:tt + 1]).then_inc(ph.lnv, 1)
                v.wait_ge(ph.lnv, lnv_base + 7 * tt + 5)
                v.tensor_scalar(stream[:, sl], stream[:, sl],
                                mu[:, tt:tt + 1], rsd[:, tt:tt + 1],
                                ALU.subtract, ALU.mult).then_inc(ph.lnv, 2)

            v.wait_ge(p3.evs, 32)
            for tt in range(NTT):
                ln_vector(p3, tt, 0, 0)
            NGOP = 9
            for tt in range(NTT):
                g8 = slice(tt * 8, (tt + 1) * 8)
                v.wait_ge(p3.gsem, tt + 1)
                if tt > 0:
                    v.wait_ge(p3.vch, NGOP * tt)
                v.reduce_max(p3.t2a[:, :], p3.gexp[:, g8], AX.X).then_inc(p3.vch, 1)
                v.wait_ge(p3.vch, NGOP * tt + 1)
                v.tensor_scalar(p3.e8a[:, :], p3.gexp[:, g8], p3.t2a[:, 0:1], None,
                                ALU.is_equal).then_inc(p3.vch, 1)
                v.wait_ge(p3.vch, NGOP * tt + 2)
                v.scalar_tensor_tensor(p3.e8a[:, :], p3.e8a[:, :], -1e30, p3.gexp[:, g8],
                                       ALU.mult, ALU.add).then_inc(p3.vch, 1)
                v.wait_ge(p3.vch, NGOP * tt + 3)
                v.reduce_max(p3.t2b[:, :], p3.e8a[:, :], AX.X).then_inc(p3.vch, 1)
                v.wait_ge(p3.vch, NGOP * tt + 4)
                v.tensor_scalar(p3.e8a[:, :], p3.gexp[:, g8], p3.t2b[:, 0:1], None,
                                ALU.is_ge).then_inc(p3.vch, 1)
                v.wait_ge(p3.vch, NGOP * tt + 5)
                v.tensor_tensor(p3.e8a[:, :], p3.e8a[:, :], p3.gexp[:, g8],
                                ALU.mult).then_inc(p3.vch, 1)
                v.tensor_tensor(p3.t2a[:, :], p3.t2a[:, :], p3.t2b[:, :],
                                ALU.add).then_inc(p3.vch, 1)
                v.wait_ge(p3.vch, NGOP * tt + 7)
                v.reciprocal(p3.t2a[:, :], p3.t2a[:, :]).then_inc(p3.vch, 1)
                v.wait_ge(p3.vch, NGOP * tt + 8)
                v.tensor_scalar_mul(p3.comb[:, g8], p3.e8a[:, :], p3.t2a[:, 0:1]
                                    ).then_inc(p3.vch, 1)
            v.wait_ge(p3.vch, NGOP * NTT)
            for e in range(8):
                pi = 0
                for (foff, fw) in PIECES:
                    for tt in range(NTT):
                        gidx = e * 16 + pi
                        v.wait_ge(p3.sil, gidx + 1)
                        if gidx >= 2:
                            v.wait_ge(p3.trs, 32 + gidx - 1)
                        v.scalar_tensor_tensor(p3.a2t[gidx % 2][:, 0:fw],
                                               p3.pu[gidx % 2][:, 0:fw],
                                               p3.comb[:, tt * 8 + e: tt * 8 + e + 1],
                                               p3.gs[gidx % 2][:, 0:fw],
                                               ALU.mult, ALU.mult).then_inc(p3.a2s, 1)
                        pi += 1
                for db in range(4):
                    for tt in range(NTT):
                        di = e * 32 + db * 8 + tt
                        v.wait_ge(p3.dmm, 6 * (di + 1))
                        v.tensor_tensor(stream[:, tt * D + db * 512: tt * D + (db + 1) * 512],
                                        p3.ps[di % 2][:, :],
                                        stream[:, tt * D + db * 512: tt * D + (db + 1) * 512],
                                        ALU.add).then_inc(p3.dev, 1)
            v.wait_ge(p3.dev, 256)
            for tt in range(NTT):
                ln_vector(p3, tt, 56, 16)
            nv4 = [0]
            v.wait_ge(p4.ldc, 48)
            for mi in range(2):
                for tt in range(NTT):
                    v.wait_ge(p4.esem, mi * 32 + 24 + tt + 1)
                    v.wait_ge(p4.vch, nv4[0])
                    st3 = p4.stage[:, tt * D:(tt + 1) * D].rearrange("p (h f) -> p h f", h=32)
                    x1 = st3[:, :, 0:32]
                    x2v = st3[:, :, 32:64]
                    ccb = p4.cosb[:, tt * 64: tt * 64 + 32].unsqueeze(1).broadcast_to([P, 32, 32])
                    ssb = p4.sinb[:, tt * 64: tt * 64 + 32].unsqueeze(1).broadcast_to([P, 32, 32])
                    t3 = [p4.tmp[i][:, :].rearrange("p (h f) -> p h f", h=32) for i in range(4)]
                    v.tensor_tensor(t3[0], x1, ccb, ALU.mult).then_inc(p4.vch, 1); nv4[0] += 1
                    v.tensor_tensor(t3[1], x2v, ssb, ALU.mult).then_inc(p4.vch, 1); nv4[0] += 1
                    v.tensor_tensor(t3[2], x1, ssb, ALU.mult).then_inc(p4.vch, 1); nv4[0] += 1
                    v.tensor_tensor(t3[3], x2v, ccb, ALU.mult).then_inc(p4.vch, 1); nv4[0] += 1
                    v.wait_ge(p4.vch, nv4[0])
                    v.tensor_tensor(x1, t3[0], t3[1], ALU.subtract).then_inc(p4.vch, 1); nv4[0] += 1
                    v.tensor_tensor(x2v, t3[2], t3[3], ALU.add).then_inc(p4.vch, 1); nv4[0] += 1
            v.wait_ge(p5.gtr, 32)
            v.tensor_tensor_scan(p5.gcum[:, :], p5.gf[:, :], p5.gf[:, :], 0.0,
                                 ALU.add, ALU.bypass).then_inc(p5.gsc, 1)
            nv5 = [0]
            v.wait_ge(p5.gst, 16)

            def p5_v_amul(idx):
                gq = idx % 4
                rec = p5.sched[idx]
                smm = rec["smm0"]
                if idx >= 2:
                    v.wait_ge(p5.tsem, p5.sched[idx - 2]["kt1"])
                for i in range(4):
                    qt = 4 * gq + i
                    chs = chunks_of(qt)
                    for (kc, cw) in chs:
                        isdiag = (kc == len(chs) - 1)
                        v.wait_ge(p5.esem, smm + 1)
                        if isdiag:
                            v.tensor_tensor(p5.esb[smm % 2][:, cw - P:cw],
                                            p5.esb[smm % 2][:, cw - P:cw],
                                            mbfsb[:, :], ALU.mult).then_inc(p5.vsf, 1)
                            nv5[0] += 1
                            v.wait_ge(p5.vsf, nv5[0])
                        v.wait_ge(p5.ssem, smm + 1)
                        v.scalar_tensor_tensor(
                            p5.asb[idx % 2][:, i * T + kc * 512: i * T + kc * 512 + cw],
                            p5.ps_s[smm % 4][:, 0:cw], 0.125,
                            p5.esb[smm % 2][:, 0:cw],
                            ALU.mult, ALU.mult).then_inc(p5.asem, 1)
                        smm += 1

            def p5_v_evacs(idx):
                pr, gq = divmod(idx, 4)
                rec = p5.sched[idx]
                ktc = rec["kt0"]
                for kt in range(4 * gq + 4):
                    i0 = max(0, kt - 4 * gq)
                    if ktc % 2 == 0:
                        v.wait_ge(p5.tsem, ktc + 1)
                        v.tensor_copy(p5.ptsb[ktc % 2][:, i0 * P:512],
                                      p5.pt_ps[ktc % 2][:, i0 * P:512]).then_inc(p5.ptsem, 1)
                    ktc += 1
                v.wait_ge(p5.avsem, rec["kt1"])
                if pr >= 2 and gq == 0:
                    v.wait_ge(p5.sto[0], 16 * (pr - 1))
                v.tensor_copy(p5.osb[pr % 2][:, gq * 512:(gq + 1) * 512],
                              p5.o_ps[0:64, :]).then_inc(p5.osem, 1)

            for idx5 in range(64):
                if idx5 >= 1 and idx5 % 4 == 0:
                    p5_v_evacs(idx5 - 1)
                p5_v_amul(idx5)
                if idx5 >= 1 and idx5 % 4 != 0:
                    p5_v_evacs(idx5 - 1)
            p5_v_evacs(63)

            v.wait_ge(p3.lnv, 112)
            for tt in range(NTT):
                v.wait_ge(p6.ldo[0], 16 * (tt + 1))
                v.wait_ge(p6.onr, 5 * tt)
                v.wait_ge(p6.lns, 2 * tt + 1)
                o3 = p6.otb[tt % 2][:, :].rearrange("p (h f) -> p h f", h=32)
                v.reduce_sum(p6.hvar[:, :].unsqueeze(2),
                             scr[:, :].rearrange("p (h f) -> p h f", h=32),
                             AX.X).then_inc(p6.onr, 1)
                v.wait_ge(p6.onr, 5 * tt + 1)
                v.wait_ge(p6.lns, 2 * tt + 2)
                v.reciprocal(p6.hvar[:, :], p6.hvar[:, :]).then_inc(p6.onr, 1)
                v.wait_ge(p6.onr, 5 * tt + 2)
                v.tensor_tensor(p6.otb[tt % 2][:, :].rearrange("p (h f) -> p h f", h=32),
                                o3, p6.hvar[:, :].unsqueeze(2).broadcast_to([P, 32, 64]),
                                ALU.mult).then_inc(p6.onr, 1)
                v.wait_ge(p6.onr, 5 * tt + 3)
                v.wait_ge(p6.sil, NGP6)
                v.tensor_tensor(p6.sg_sb[:, tt * D:(tt + 1) * D],
                                p6.sg_sb[:, tt * D:(tt + 1) * D],
                                p6.otb[tt % 2][:, :], ALU.mult).then_inc(p6.onr, 2)
            gi = 0
            for ob in range(8):
                for tt in range(NTT):
                    grp = NGP6 + gi
                    v.wait_ge(p6.mm, grp + 1)
                    v.tensor_tensor(stream[:, tt * D + ob * 256: tt * D + (ob + 1) * 256],
                                    p6.ps[grp % 2][:, 0:256],
                                    stream[:, tt * D + ob * 256: tt * D + (ob + 1) * 256],
                                    ALU.add).then_inc(p6.evs, 1)
                    gi += 1
            v.wait_ge(p6.evs, NWO6)
            for tt in range(NTT):
                ln_vector(p6, tt, 0, 16)
            for th in range(2):
                for hc in range(NHC):
                    qi = th * NHC + hc
                    v.wait_ge(p6.gup, 2 * (qi + 1))
                    v.tensor_tensor(p6.a2sb[:, hc * 512:(hc + 1) * 512],
                                    p6.sgh[qi % 2][:, :],
                                    p6.a2sb[:, hc * 512:(hc + 1) * 512],
                                    ALU.mult).then_inc(p6.a2s, 1)
                for dc in range(NCH):
                    di = th * NCH + dc
                    v.wait_ge(p6.dtr, di + 1)
                    for j in range(4):
                        tt = th * 4 + j
                        inst = v.tensor_tensor(
                            stream[:, tt * D + dc * P: tt * D + (dc + 1) * P],
                            p6.ptrb[di % 2][:, j * P:(j + 1) * P],
                            stream[:, tt * D + dc * P: tt * D + (dc + 1) * P],
                            ALU.add)
                        if j == 3:
                            inst.then_inc(p6.dev, 1)
            v.wait_ge(p6.dev, 32)
            for tt in range(NTT):
                ln_vector(p6, tt, 56, 32)

        @block.scalar
        def _(s):
            s.wait_ge(cst, 81)
            SCALE2 = 128.0 ** -0.5
            for h in range(8):
                for gq in range(4):
                    idx = h * 4 + gq
                    rec = p2.sched[idx]
                    if idx > 1:
                        prev = p2.sched[idx - 2]
                        s.wait_ge(p2.tsem, prev["kt1"])
                    if idx > 0:
                        s.wait_ge(p2.zsem, p2.sched[idx - 1]["zsc1"])
                    smm = rec["smm0"]
                    mk = rec["mask0"]
                    for i in range(4):
                        qt = 4 * gq + i
                        chs = chunks_of(qt)
                        for (kc, cw) in chs:
                            if kc == len(chs) - 1:
                                mk += 1
                                s.wait_ge(p2.msem, mk)
                            else:
                                s.wait_ge(p2.ssem, smm + 1)
                            s.activation(p2.psb[idx % 2][:, i * T + kc * 512: i * T + kc * 512 + cw],
                                         p2.ps_s[smm % 4][:, 0:cw], AF.Exp,
                                         bias=shcol[:, :], scale=SCALE2,
                                         accum_out=p2.zp[:, i * 4 + kc:i * 4 + kc + 1]
                                         ).then_inc(p2.esem, 1)
                            smm += 1
            def ln_scalar(ph, tt, lnv_base, lns_base):
                sl = slice(tt * D, (tt + 1) * D)
                s.wait_ge(ph.lns, lns_base + 2 * tt)
                s.activation(scr[:, :], stream[:, sl], AF.Square,
                             accum_out=sq[:, tt:tt + 1]).then_inc(ph.lns, 1)
                s.wait_ge(ph.lnv, lnv_base + 7 * tt + 4)
                s.activation(rsd[:, tt:tt + 1], varv[:, tt:tt + 1], AF.Sqrt,
                             bias=epsln[:, :], scale=-1.0).then_inc(ph.lns, 1)

            s.wait_ge(p3.evs, 32)
            for tt in range(NTT):
                ln_scalar(p3, tt, 0, 0)
            ntr = 0
            for tt in range(NTT):
                for cg in range(4):
                    s.wait_ge(p3.trs, ntr + 1)
                    s.copy(x_f[:, :].rearrange("p (c t) -> p c t", c=NCH)
                           [:, cg * 4:(cg + 1) * 4, tt * P:(tt + 1) * P],
                           p3.pxf[ntr % 2][:, :].rearrange("p (c4 t) -> p c4 t", c4=4)
                           ).then_inc(p3.xfs, 1)
                    ntr += 1
            for tt in range(NTT):
                s.wait_ge(p3.mm, 32 + tt + 1)
                s.activation(p3.gexp[:, tt * 8:(tt + 1) * 8], p3.pg[0][:, 0:8],
                             AF.Exp).then_inc(p3.gsem, 1)
            for e in range(8):
                pi = 0
                for (foff, fw) in PIECES:
                    for tt in range(NTT):
                        gidx = e * 16 + pi
                        s.wait_ge(p3.mm, mm_gu3(gidx))
                        if gidx >= 2:
                            s.wait_ge(p3.a2s, gidx - 1)
                        s.activation(p3.gs[gidx % 2][:, 0:fw], p3.pg[gidx % 2][:, 0:fw],
                                     AF.Silu).then_inc(p3.sil, 1)
                        s.wait_ge(p3.trs, 32 + gidx + 1)
                        if e >= 1 and pi == 0:
                            s.wait_ge(p3.dmm, 192 * e)
                        nt = fw // P
                        s.copy(p3.a2f[:, tt * 6 * P + (foff // P) * P:
                                      tt * 6 * P + (foff // P) * P + nt * P],
                               p3.ptr[gidx % 2][:, 0:nt * P]).then_inc(p3.a2e, 1)
                        pi += 1
            s.wait_ge(p3.dev, 256)
            for tt in range(NTT):
                ln_scalar(p3, tt, 56, 16)
            ntr = 0
            for tt in range(NTT):
                for cg in range(4):
                    s.wait_ge(p4.trs, ntr + 1)
                    s.copy(x_f[:, :].rearrange("p (c t) -> p c t", c=NCH)
                           [:, cg * 4:(cg + 1) * 4, tt * P:(tt + 1) * P],
                           p4.ps[ntr % 2][:, :].rearrange("p (c4 t) -> p c4 t", c4=4)
                           ).then_inc(p4.xfs, 1)
                    ntr += 1
            gi = 0
            for mi in range(3):
                for ob in range(4):
                    for tt in range(NTT):
                        grp = gi * 8 + tt
                        s.wait_ge(p4.mm, grp + 1)
                        if mi > 0 and ob == 0 and tt == 0:
                            s.wait_ge(p4.sto, 16 * mi)
                        s.copy(p4.stage[:, tt * D + ob * 512: tt * D + (ob + 1) * 512],
                               p4.ps[grp % 2][:, :]).then_inc(p4.esem, 1)
                    gi += 1
            for tt in range(NTT):
                s.wait_ge(p4.mgk, tt + 1)
                s.activation(p4.gks[:, tt * 32:(tt + 1) * 32], p4.pgk[:, :],
                             AF.Sigmoid).then_inc(p4.gsig, 1)
            for tt in range(NTT):
                s.wait_ge(p4.gsig, NTT + tt)
                s.activation(p4.gks[:, tt * 32:(tt + 1) * 32],
                             p4.gks[:, tt * 32:(tt + 1) * 32],
                             AF.Ln).then_inc(p4.gsig, 1)
            for kt in range(16):
                s.wait_ge(p5.gtr, 2 * kt + 1)
                s.copy(p5.gf[0:32, kt * P:(kt + 1) * P],
                       p5.pbc[0:32, 0:P]).then_inc(p5.gtr, 1)
            def p5_s_evac(pidx):
                prec = p5.sched[pidx]
                pgq = pidx % 4
                pktc = prec["kt0"]
                for pkt in range(4 * pgq + 4):
                    pi0 = max(0, pkt - 4 * pgq)
                    if pktc % 2 == 1:
                        s.wait_ge(p5.tsem, pktc + 1)
                        s.copy(p5.ptsb[pktc % 2][:, pi0 * P:512],
                               p5.pt_ps[pktc % 2][:, pi0 * P:512]
                               ).then_inc(p5.ptsem_s, 1)
                    pktc += 1

            for pr in range(16):
                if pr >= 1:
                    p5_s_evac(pr * 4 - 1)
                s.wait_ge(p5.gqt, pr + 1)
                if pr >= 1:
                    s.wait_ge(p5.esem, p5.sched[(pr - 1) * 4 + 3]["exp1"])
                s.activation(p5.gq16[:, :], p5.pbc[:, 0:16], AF.Copy,
                             scale=1.0 / 16.0).then_inc(p5.gqe, 1)
                for j in range(4):
                    s.wait_ge(p5.bcs, 4 * pr + j + 1)
                    s.copy(p5.gkbc[:, j * 512:(j + 1) * 512], p5.pbc[:, :]
                           ).then_inc(p5.bce, 1)
                for gq in range(4):
                    rec = p5.sched[pr * 4 + gq]
                    smm = rec["smm0"]
                    if gq == 0:
                        s.wait_ge(p5.bce, 4 * (pr + 1))
                    for i in range(4):
                        qt = 4 * gq + i
                        for (kc, cw) in chunks_of(qt):
                            if smm >= 2:
                                s.wait_ge(p5.asem, smm - 1)
                            s.activation(p5.esb[smm % 2][:, 0:cw],
                                         p5.gkbc[:, kc * 512:kc * 512 + cw],
                                         AF.Exp, bias=p5.gq16[:, qt:qt + 1],
                                         scale=-1.0 / 16.0).then_inc(p5.esem, 1)
                            smm += 1
                    if gq >= 1:
                        p5_s_evac(pr * 4 + gq - 1)
            p5_s_evac(63)
            s.wait_ge(p3.lns, 32)
            s.wait_ge(p6.ldm, 48)
            gi = 0
            for ob in range(8):
                for tt in range(NTT):
                    s.wait_ge(p6.mm, gi + 1)
                    s.activation(p6.sg_sb[:, tt * D + ob * 256: tt * D + (ob + 1) * 256],
                                 p6.ps[gi % 2][:, 0:256], AF.Silu).then_inc(p6.sil, 1)
                    gi += 1
            for tt in range(NTT):
                s.wait_ge(p6.ldo[0], 16 * (tt + 1))
                if tt >= 1:
                    s.wait_ge(p6.onr, 5 * (tt - 1) + 1)
                s.activation(scr[:, :], p6.otb[tt % 2][:, :], AF.Square).then_inc(p6.lns, 1)
                s.wait_ge(p6.onr, 5 * tt + 1)
                s.activation(p6.hvar[:, :], p6.hvar[:, :], AF.Sqrt,
                             bias=epsrm[:, :], scale=1.0 / 64.0).then_inc(p6.lns, 1)
            ntr = 0
            for tt in range(NTT):
                for cg in range(4):
                    s.wait_ge(p6.trs, ntr + 1)
                    s.copy(p6.ogf[:, :].rearrange("p (c t) -> p c t", c=NCH)
                           [:, cg * 4:(cg + 1) * 4, tt * P:(tt + 1) * P],
                           p6.ptrb[ntr % 2][:, 0:512].rearrange("p (c4 t) -> p c4 t", c4=4)
                           ).then_inc(p6.xfs, 1)
                    ntr += 1
            s.wait_ge(p6.evs, NWO6)
            for tt in range(NTT):
                ln_scalar(p6, tt, 0, 16)
            for tt in range(NTT):
                for cg in range(4):
                    s.wait_ge(p6.trs, ntr + 1)
                    s.copy(x_f[:, :].rearrange("p (c t) -> p c t", c=NCH)
                           [:, cg * 4:(cg + 1) * 4, tt * P:(tt + 1) * P],
                           p6.ps[ntr % 2][:, :].rearrange("p (c4 t) -> p c4 t", c4=4)
                           ).then_inc(p6.xfs, 1)
                    ntr += 1
            for th in range(2):
                tsl = slice(th * 512, (th + 1) * 512)
                s.wait_ge(p6.rsm, 2 * th + 1)
                s.copy(p6.rsrow[0:1, tsl], p6.prow[0:1, :]).then_inc(p6.rse, 1)
                for hc in range(NHC):
                    qi = th * NHC + hc
                    s.wait_ge(p6.mm, NGP6 + NWO6 + qi + 1)
                    if qi >= 2:
                        s.wait_ge(p6.a2s, qi - 1)
                    s.activation(p6.sgh[qi % 2][:, :], p6.ps[qi % 2][:, :], AF.Silu,
                                 scale=p6.sgc[:, hc:hc + 1]).then_inc(p6.gup, 1)
                    s.activation(p6.a2sb[:, hc * 512:(hc + 1) * 512], p6.pu[qi % 2][:, :],
                                 AF.Copy, scale=p6.suc[:, hc:hc + 1]).then_inc(p6.gup, 1)
                s.wait_ge(p6.rsm, 2 * th + 2)
                s.copy(p6.csrow[0:1, tsl], p6.prow[0:1, :]).then_inc(p6.rse, 1)
                for dc in range(NCH):
                    di = th * NCH + dc
                    s.wait_ge(p6.dmm, di + 1)
                    if di >= 2:
                        s.wait_ge(p6.dtr, di - 1)
                    s.activation(p6.yfb[di % 2][:, :], p6.pu[di % 2][:, :], AF.Copy,
                                 scale=p6.sdc[:, dc:dc + 1]).then_inc(p6.dse, 1)
            s.wait_ge(p6.dev, 32)
            for tt in range(NTT):
                ln_scalar(p6, tt, 56, 32)

    ctx.close()
    return nc


from concourse.bass_utils import run_bass_kernel_spmd
import ml_dtypes

_BF = ml_dtypes.bfloat16
_cc = np.ascontiguousarray


def _planar_perm():
    perm = np.empty(2048, np.int64)
    for h in range(32):
        for i in range(64):
            perm[h * 64 + i] = h * 64 + (2 * i if i < 32 else 2 * (i - 32) + 1)
    return perm


def kernel(L0_attn_x, L0_attn_w_attn, L0_attn_b_attn, L0_attn_w_proj, L0_attn_b_proj,
           L0_ffn_gate_w, L0_ffn_w1, L0_ffn_w2, L0_ffn_w3,
           L1_attn_Wq, L1_attn_Wk, L1_attn_Wv, L1_attn_Wg, L1_attn_Wgk, L1_attn_Wo,
           L1_ffn_wq_gate, L1_ffn_s_gate, L1_ffn_wq_up, L1_ffn_s_up,
           L1_ffn_wq_down, L1_ffn_s_down):
    x = np.asarray(L0_attn_x, np.float32)
    perm = _planar_perm()
    weights = {
        "w_attn": np.asarray(L0_attn_w_attn, np.float32),
        "b_attn": np.asarray(L0_attn_b_attn, np.float32),
        "w_proj": np.asarray(L0_attn_w_proj, np.float32),
        "b_proj": np.asarray(L0_attn_b_proj, np.float32),
        "gate_w": np.asarray(L0_ffn_gate_w, np.float32),
        "w1": np.asarray(L0_ffn_w1, np.float32),
        "w2": np.asarray(L0_ffn_w2, np.float32),
        "w3": np.asarray(L0_ffn_w3, np.float32),
        "WqT": _cc(np.asarray(L1_attn_Wq, np.float32).T[:, perm]),
        "WkT": _cc(np.asarray(L1_attn_Wk, np.float32).T[:, perm]),
        "WvT": _cc(np.asarray(L1_attn_Wv, np.float32).T),
        "WgT": _cc(np.asarray(L1_attn_Wg, np.float32).T),
        "WgkT": _cc(np.asarray(L1_attn_Wgk, np.float32).T),
        "WoT": _cc(np.asarray(L1_attn_Wo, np.float32).T),
        "wqgT": _cc(np.asarray(L1_ffn_wq_gate, np.int32).T),
        "wquT": _cc(np.asarray(L1_ffn_wq_up, np.int32).T),
        "wqdT": _cc(np.asarray(L1_ffn_wq_down, np.int32).T),
        "sg2d": _cc(np.asarray(L1_ffn_s_gate, np.float32).reshape(44, 128).T),
        "su2d": _cc(np.asarray(L1_ffn_s_up, np.float32).reshape(44, 128).T),
        "sd2d": _cc(np.asarray(L1_ffn_s_down, np.float32).reshape(16, 128).T),
    }
    id32 = np.eye(128, dtype=np.float32)
    consts = {
        "id32": id32,
        "idbf": id32.astype(_BF),
        "maskneg": np.triu(np.full((128, 128), -1e30, np.float32), 1),
        "maskbf": np.tril(np.ones((128, 128), np.float32)).astype(_BF),
    }
    inv = 1.0 / 10000.0 ** (np.arange(0, 64, 2) / 64.0)
    in_maps = []
    for c in range(8):
        b, half = c // 2, c % 2
        sl = slice(half * 1024, half * 1024 + 1024)
        pos = np.arange(half * 1024, half * 1024 + 1024, dtype=np.float64)
        fr = pos[:, None] * inv[None, :]
        cosm = np.concatenate([np.cos(fr), np.cos(fr)], 1).astype(np.float32)
        sinm = np.concatenate([np.sin(fr), np.sin(fr)], 1).astype(np.float32)
        m = {"x_f": _cc(x[b, sl].T), "x_t": _cc(x[b, sl]),
             "coreinfo": np.array([[1024 * half, 1024 * half, 1024 * half,
                                    16 * half]], np.int32),
             "rope_cos": cosm, "rope_sin": sinm}
        m.update(weights)
        m.update(consts)
        in_maps.append(m)
    nc = build_fused()
    res = run_bass_kernel_spmd(nc, in_maps, core_ids=list(range(8)))
    out = np.empty((4, 2048, 2048), np.float32)
    for c in range(8):
        b, half = c // 2, c % 2
        out[b, half * 1024:half * 1024 + 1024] = res.results[c]["y"]
    return out
